# version 06
# baseline (speedup 1.0000x reference)
import contextlib
import numpy as np
import concourse.bass as bass
import concourse.bacc as bacc
import concourse.mybir as mybir
from concourse.bass_utils import run_bass_kernel_spmd

F32 = mybir.dt.float32
F32R = mybir.dt.float32r
BF16 = mybir.dt.bfloat16
AF = mybir.ActivationFunctionType
ALU = mybir.AluOpType
AXX = mybir.AxisListType.X

L = 4096
DM = 1024
DH = 1024
NDT = 8
NST = 16
RNK = 64
DC = 4
HID = 4096
NHT = 32
NSTAGE = 12
TS = 1024
NSC = L // TS
TMM = 512
NCH = L // TMM
LHALF = L // 2
TF = 512
NFC = LHALF // TF
EPS = 1e-5
NTOK = L // 128

RG_PAIRS = [[0, 1], [2, 3], [4, 5], [6, 7]]

COMPUTE_ENGINES = ("tensor", "vector", "scalar", "gpsimd", "sync")
DMA_RING = 12


class _Op:
    __slots__ = ("idx", "eng", "fn", "reads", "writes", "is_dma", "is_async",
                 "deps", "need_ms", "ms_val", "ring_slot", "pre_throttle",
                 "tag")

    def __init__(self, idx, eng, fn, reads, writes, is_dma, tag):
        self.is_async = False
        self.idx = idx
        self.eng = eng
        self.fn = fn
        self.reads = reads
        self.writes = writes
        self.is_dma = is_dma
        self.deps = []
        self.need_ms = False
        self.ms_val = None
        self.ring_slot = None
        self.pre_throttle = None
        self.tag = tag


class SemPool:

    def __init__(self, nc, stack, rings=("sync", "gpsimd", "scalar", "vector")):
        self.csem = {}
        self.cctr = {}
        for e in COMPUTE_ENGINES:
            self.csem[e] = stack.enter_context(nc.semaphore(f"ms_{e}"))
            self.cctr[e] = 0
        self.rings = {}
        for e in rings:
            self.rings[e] = {
                "sems": [stack.enter_context(nc.semaphore(f"ms_{e[:2]}d{i}"))
                         for i in range(DMA_RING)],
                "uses": [0] * DMA_RING,
                "next": 0,
            }


class MiniSched:
    def __init__(self, nc, pool, name):
        self.nc = nc
        self.name = name
        self.ops = []
        self.last_writer = {}
        self.readers = {}
        self.csem = pool.csem
        self.cctr = pool.cctr
        self.rings = pool.rings
        self.planned = False

    def op(self, eng, fn, reads=(), writes=(), tag=""):
        o = _Op(len(self.ops), eng, fn, tuple(reads), tuple(writes), False, tag)
        self._link(o)
        self.ops.append(o)
        return o

    def dma(self, eng, fn, reads=(), writes=(), tag=""):
        o = _Op(len(self.ops), eng, fn, tuple(reads), tuple(writes), True, tag)
        self._link(o)
        self.ops.append(o)
        return o

    def cc(self, eng, fn, reads=(), writes=(), tag=""):
        o = _Op(len(self.ops), eng, fn, tuple(reads), tuple(writes), False, tag)
        o.is_async = True
        o.need_ms = True
        self._link(o)
        self.ops.append(o)
        return o

    def _link(self, o):
        deps = {}
        for r in o.reads:
            w = self.last_writer.get(r)
            if w is not None:
                deps[w.idx] = w
        for w_ in o.writes:
            w = self.last_writer.get(w_)
            if w is not None:
                deps[w.idx] = w
            for rd in self.readers.get(w_, ()):
                deps[rd.idx] = rd
        o.deps = [deps[i] for i in sorted(deps) if i != o.idx]
        for w_ in o.writes:
            self.last_writer[w_] = o
            self.readers[w_] = []
        for r in o.reads:
            self.readers.setdefault(r, []).append(o)

    def plan(self):
        assert not self.planned
        for o in self.ops:
            if o.is_dma:
                ring = self.rings[o.eng]
                s = ring["next"]
                if ring["uses"][s] > 0:
                    o.pre_throttle = (ring["sems"][s], 16 * ring["uses"][s])
                ring["uses"][s] += 1
                ring["next"] = (s + 1) % DMA_RING
                o.ring_slot = s
                o.ms_val = 16 * ring["uses"][s]
            for d in o.deps:
                if (not d.is_dma) and d.eng != o.eng:
                    d.need_ms = True
        for o in self.ops:
            if (not o.is_dma) and o.need_ms:
                self.cctr[o.eng] += 1
                o.ms_val = self.cctr[o.eng]
        self.planned = True

    def emit(self, eng_name, eng):
        assert self.planned
        for o in self.ops:
            if o.eng != eng_name:
                continue
            need_drain = False
            for d in o.deps:
                if d.is_dma:
                    ring = self.rings[d.eng]
                    eng.wait_ge(ring["sems"][d.ring_slot], d.ms_val)
                elif d.eng == o.eng and not d.is_async:
                    need_drain = True
                else:
                    assert d.ms_val is not None, (o.tag, d.tag)
                    eng.wait_ge(self.csem[d.eng], d.ms_val)
            if need_drain:
                eng.drain()
            if o.pre_throttle is not None:
                eng.wait_ge(*o.pre_throttle)
            instr = o.fn(eng)
            if o.is_dma:
                instr.then_inc(self.rings[o.eng]["sems"][o.ring_slot], 16)
            elif o.need_ms:
                instr.then_inc(self.csem[o.eng], 1)

    def quiesce_ops(self):
        for e, ring in self.rings.items():
            def fn(eng, _ring=ring):
                for s in range(DMA_RING):
                    if _ring["uses"][s] > 0:
                        eng.wait_ge(_ring["sems"][s], 16 * _ring["uses"][s])
                return eng.nop()
            self.op(e, fn, tag="quiesce")

    def run_block(self, blk):
        self.quiesce_ops()
        self.plan()

        @blk.sync
        def _(e):
            self.emit("sync", e)

        @blk.scalar
        def _(e):
            self.emit("scalar", e)

        @blk.vector
        def _(e):
            self.emit("vector", e)

        @blk.tensor
        def _(e):
            self.emit("tensor", e)

        @blk.gpsimd
        def _(e):
            self.emit("gpsimd", e)


class Env:
    pass


def gen_ln(nc, E, ms, B):
    xln, lsq, ls1, ls2, lsv, lmu, lnmu, lvar, lrst, lt0, xnf, xnb, xnbt = (
        B.xln, B.lsq, B.ls1, B.ls2, B.lsv, B.lmu, B.lnmu, B.lvar, B.lrst,
        B.lt0, B.xnf, B.xnb, B.xnbt)
    if True:
        for tt in range(NTOK):
            p = tt % 2
            ms.dma("sync", lambda e, tt=tt, p=p: e.dma_start(
                out=xln[:, p * DM:(p + 1) * DM],
                in_=E.x_in[tt * 128:(tt + 1) * 128, :]),
                writes=(f"xln{p}",), tag=f"ldx{tt}")
            ms.op("scalar", lambda e, p=p: e.activation(
                lsq[:, :], xln[:, p * DM:(p + 1) * DM], AF.Square,
                accum_out=ls2[:, p:p + 1]),
                reads=(f"xln{p}",), writes=("lsq", f"ls2{p}"), tag=f"sq{tt}")
            ms.op("vector", lambda e, p=p: e.reduce_sum(
                ls1[:, p:p + 1], xln[:, p * DM:(p + 1) * DM], axis=AXX),
                reads=(f"xln{p}",), writes=(f"ls1{p}",))
            ms.op("vector", lambda e, p=p: e.tensor_scalar_mul(
                lmu[:, p:p + 1], ls1[:, p:p + 1], 1.0 / DM),
                reads=(f"ls1{p}",), writes=(f"lmu{p}",))
            ms.op("vector", lambda e, p=p: e.tensor_scalar_mul(
                lnmu[:, p:p + 1], ls1[:, p:p + 1], -1.0 / DM),
                reads=(f"ls1{p}",), writes=(f"lnmu{p}",))
            ms.op("vector", lambda e, p=p: e.tensor_mul(
                lvar[:, p:p + 1], lmu[:, p:p + 1], lmu[:, p:p + 1]),
                reads=(f"lmu{p}",), writes=(f"lvar{p}",))
            ms.op("vector", lambda e, p=p: e.scalar_tensor_tensor(
                lvar[:, p:p + 1], ls2[:, p:p + 1], 1.0 / DM, lvar[:, p:p + 1],
                ALU.mult, ALU.subtract),
                reads=(f"ls2{p}", f"lvar{p}"), writes=(f"lvar{p}",))
            ms.op("vector", lambda e, p=p: e.tensor_scalar_add(
                lvar[:, p:p + 1], lvar[:, p:p + 1], EPS),
                reads=(f"lvar{p}",), writes=(f"lvar{p}",))
            ms.op("scalar", lambda e, p=p: e.activation(
                lsv[:, p:p + 1], lvar[:, p:p + 1], AF.Sqrt),
                reads=(f"lvar{p}",), writes=(f"lsv{p}",))
            ms.op("vector", lambda e, p=p: e.reciprocal(
                lrst[:, p:p + 1], lsv[:, p:p + 1]),
                reads=(f"lsv{p}",), writes=(f"lrst{p}",))
            ms.op("vector", lambda e, p=p: e.tensor_mul(
                lt0[:, p:p + 1], lrst[:, p:p + 1], lrst[:, p:p + 1]),
                reads=(f"lrst{p}",), writes=(f"lt0{p}",))
            ms.op("vector", lambda e, p=p: e.tensor_mul(
                lt0[:, p:p + 1], lt0[:, p:p + 1], lvar[:, p:p + 1]),
                reads=(f"lt0{p}", f"lvar{p}"), writes=(f"lt0{p}",))
            ms.op("vector", lambda e, p=p: e.tensor_scalar(
                lt0[:, p:p + 1], lt0[:, p:p + 1], -0.5, 1.5, ALU.mult, ALU.add),
                reads=(f"lt0{p}",), writes=(f"lt0{p}",))
            ms.op("vector", lambda e, p=p: e.tensor_mul(
                lrst[:, p:p + 1], lrst[:, p:p + 1], lt0[:, p:p + 1]),
                reads=(f"lrst{p}", f"lt0{p}"), writes=(f"lrst{p}",))
            ms.op("vector", lambda e, p=p: e.tensor_scalar(
                xnf[:, p * DM:(p + 1) * DM], xln[:, p * DM:(p + 1) * DM],
                lnmu[:, p:p + 1], lrst[:, p:p + 1], ALU.add, ALU.mult),
                reads=(f"xln{p}", f"lnmu{p}", f"lrst{p}"), writes=(f"xnf{p}",))
            ms.op("vector", lambda e, p=p: e.tensor_scalar(
                xnb[:, p * DM:(p + 1) * DM], xln[:, p * DM:(p + 1) * DM],
                lnmu[:, p:p + 1], lrst[:, p:p + 1], ALU.add, ALU.mult),
                reads=(f"xln{p}", f"lnmu{p}", f"lrst{p}"), writes=(f"xnb{p}",))
            ms.dma("gpsimd", lambda e, tt=tt, p=p: e.dma_start(
                out=E.xn_tm[tt * 128:(tt + 1) * 128, :],
                in_=xnf[:, p * DM:(p + 1) * DM]),
                reads=(f"xnf{p}",), tag=f"st_tm{tt}")
            for j in range(8):
                k = tt * 8 + j
                q = k % 2
                ms.dma("sync", lambda e, p=p, j=j, q=q: e.dma_start_transpose(
                    xnbt[:, q * 128:(q + 1) * 128],
                    xnb[:, p * DM + j * 128: p * DM + (j + 1) * 128]),
                    reads=(f"xnb{p}",), writes=(f"xnbt{q}",))
                ms.dma("gpsimd", lambda e, tt=tt, j=j, q=q: e.dma_start(
                    out=E.xn_fm[j * 128:(j + 1) * 128, tt * 128:(tt + 1) * 128],
                    in_=xnbt[:, q * 128:(q + 1) * 128]),
                    reads=(f"xnbt{q}",))
            yield


def gen_fold(nc, E, ms, B):
    fstg, fstr, fR, fvs, fva, fbf, fid, fide, fub, fhb, fps = (
        B.fstg, B.fstr, B.fR, B.fvs, B.fva, B.fbf, B.fid, B.fide, B.fub,
        B.fhb, B.fps)
    if True:
        ms.dma("scalar", lambda e: e.dma_start(out=fide[:, :], in_=E.ident_in[:, :]),
               writes=("fide",))
        ms.op("scalar", lambda e: e.activation(fid[:, :], fide[:, :], AF.Identity),
              reads=("fide",), writes=("fid",))
        ms.dma("scalar", lambda e: e.dma_start(
            out=fub[:, :], in_=bass.AP(E.upb, 0, [[1, 128], [128, 32]]),
            allow_slow_non_contiguous=True),
            writes=("fub",))

        slot = [0]

        def ldw(src, si, gi, gstride):
            sl = slot[0]
            slot[0] = (slot[0] + 1) % 4
            base = (si * gstride + gi) * 128 * 8 * 128
            ms.dma("gpsimd", lambda e, src=src, base=base, sl=sl: e.dma_start(
                out=fstg[:, sl * 8 * 128: (sl + 1) * 8 * 128],
                in_=bass.AP(src, base, [[8 * 128, 128], [1, 8 * 128]])),
                writes=(f"stg{sl}",), tag="ldw")
            ms.op("vector", lambda e, sl=sl: e.tensor_copy(
                fstr[:, sl * 8 * 128: (sl + 1) * 8 * 128],
                fstg[:, sl * 8 * 128: (sl + 1) * 8 * 128]),
                reads=(f"stg{sl}",), writes=(f"str{sl}",), tag="castw")
            return sl

        for bf in ("u", "d"):
            wsrc = E.bfu_within if bf == "u" else E.bfd_within
            xsrc = E.bfu_cross if bf == "u" else E.bfd_cross
            seed = 6 if bf == "u" else 0
            chain = list(range(5, -1, -1)) if bf == "u" else list(range(1, 7))

            for g in range(4):
                sl = slot[0]
                slot[0] = (slot[0] + 1) % 4
                base = (seed * 4 + g) * 128 * 8 * 128
                ms.dma("gpsimd", lambda e, base=base, sl=sl, wsrc=wsrc: e.dma_start(
                    out=fstg[:, sl * 8 * 128:(sl + 1) * 8 * 128],
                    in_=bass.AP(wsrc, base, [[8 * 128, 128], [1, 8 * 128]])),
                    writes=(f"stg{sl}",), tag="ldseed")
                ms.op("vector", lambda e, g=g, sl=sl: e.tensor_copy(
                    fR[:, g * 8 * 128:(g + 1) * 8 * 128],
                    fstg[:, sl * 8 * 128:(sl + 1) * 8 * 128]),
                    reads=(f"stg{sl}",),
                    writes=tuple(f"R{g * 8 + jj}" for jj in range(8)), tag="seedcast")
                yield

            if bf == "u" and "fRs" in E.dump_t:
                ms.dma("gpsimd", lambda e: e.dma_start(
                    out=E.dump_t["fRs"][:, :], in_=fR[:, :].bitcast(F32)),
                    reads=tuple(f"R{j}" for j in range(32)), tag="dbgRs")

            for ss in chain:
                for rnd in range(4):
                    sl = ldw(wsrc, ss, rnd, 4)
                    for q in range(8):
                        j = rnd * 8 + q
                        ms.op("tensor", lambda e, q=q, j=j, sl=sl: e.matmul(
                            fps[:, q * 512: q * 512 + 128],
                            fstr[:, sl * 8 * 128 + q * 128: sl * 8 * 128 + (q + 1) * 128],
                            fR[:, j * 128:(j + 1) * 128],
                            start=True, stop=True),
                            reads=(f"str{sl}", f"R{j}"), writes=(f"pb{q}",),
                            tag=f"cmm{bf}{ss}")
                    ms.op("scalar", lambda e, rnd=rnd: e.activation(
                        bass.AP(fR, rnd * 8 * 128, [[32 * 128, 128], [128, 8], [1, 128]]),
                        bass.AP(fps, 0, [[4096, 128], [512, 8], [1, 128]]),
                        AF.Identity),
                        reads=tuple(f"pb{q}" for q in range(8)),
                        writes=tuple(f"R{rnd * 8 + q}" for q in range(8)),
                        tag=f"cev{bf}{ss}")
                    yield

            if bf == "u" and "fRu" in E.dump_t:
                ms.dma("gpsimd", lambda e: e.dma_start(
                    out=E.dump_t["fRu"][:, :], in_=fR[:, :].bitcast(F32)),
                    reads=tuple(f"R{j}" for j in range(32)), tag="dbgR")

            chunks = [(0, 512), (512, 512), (1024, 2)] if bf == "u" else [(0, 512), (512, 512)]
            vsrc = E.wexpc if bf == "u" else E.wcon
            for (c0, w) in chunks:
                vslot = [0]
                for g in range(8):
                    sl = vslot[0]
                    vslot[0] ^= 1
                    if w == 512:
                        ci = c0 // 512
                        ms.dma("gpsimd", lambda e, g=g, ci=ci, sl=sl, vsrc=vsrc:
                               e.dma_start(
                            out=fvs[:, sl * 2048:(sl + 1) * 2048],
                            in_=bass.AP(vsrc, (ci * 8 + g) * 128 * 2048,
                                        [[2048, 128], [1, 2048]])),
                            writes=(f"vs{sl}",), tag="ldv")
                        ms.op("vector", lambda e, g=g, sl=sl: e.tensor_copy(
                            fva[:, g * 2048:(g + 1) * 2048],
                            fvs[:, sl * 2048:(sl + 1) * 2048]),
                            reads=(f"vs{sl}",), writes=(f"va{g}",), tag="castv")
                    else:
                        ms.dma("gpsimd", lambda e, g=g, sl=sl: e.dma_start(
                            out=fvs[:, sl * 2048: sl * 2048 + 8],
                            in_=bass.AP(E.wexpcb, g * 128 * 8, [[8, 128], [1, 8]])),
                            writes=(f"vs{sl}",), tag="ldvb")
                        ms.op("vector", lambda e, g=g, sl=sl: e.tensor_copy(
                            bass.AP(fva, g * 4 * 512,
                                    [[32 * 512, 128], [512, 4], [1, 2]]),
                            bass.AP(fvs, sl * 2048, [[2 * 2048, 128], [2, 4], [1, 2]])),
                            reads=(f"vs{sl}",), writes=(f"va{g}",), tag="castvb")
                    if g % 2 == 1:
                        yield
                if bf == "u":
                    passes = [("W", None)] + [("X", xi) for xi in range(5)]
                else:
                    passes = [("X", xi) for xi in range(4, -1, -1)] + [("W", None)]
                for (kind, xi) in passes:
                    ts_ = 0 if kind == "W" else (1 << xi)
                    if kind == "W" or ts_ < 4:
                        for g in range(8):
                            if kind == "X":
                                sl = ldw(xsrc, xi, g, 8)
                            ph = g % 2
                            for q in range(4):
                                jo = g * 4 + q
                                pslice = fps[:, ph * 2048 + q * 512: ph * 2048 + q * 512 + w]
                                if kind == "W":
                                    ms.op("tensor", lambda e, jo=jo, w=w, pslice=pslice:
                                          e.matmul(
                                        pslice, fR[:, jo * 128:(jo + 1) * 128],
                                        fva[:, jo * 512: jo * 512 + w],
                                        start=True, stop=True),
                                        reads=(f"R{jo}", f"va{jo // 4}"),
                                        writes=(f"pb{ph * 4 + q}",), tag=f"pw{bf}")
                                else:
                                    ja = jo & ~ts_
                                    jb = ja + ts_
                                    ms.op("tensor", lambda e, q=q, w=w, sl=sl,
                                          pslice=pslice, ja=ja: e.matmul(
                                        pslice,
                                        fstr[:, sl * 8 * 128 + (2 * q) * 128: sl * 8 * 128 + (2 * q + 1) * 128],
                                        fva[:, ja * 512: ja * 512 + w],
                                        start=True, stop=False),
                                        reads=(f"str{sl}", f"va{ja // 4}"),
                                        writes=(f"pb{ph * 4 + q}",), tag=f"px{bf}{xi}a")
                                    ms.op("tensor", lambda e, q=q, w=w, sl=sl,
                                          pslice=pslice, jb=jb: e.matmul(
                                        pslice,
                                        fstr[:, sl * 8 * 128 + (2 * q + 1) * 128: sl * 8 * 128 + (2 * q + 2) * 128],
                                        fva[:, jb * 512: jb * 512 + w],
                                        start=False, stop=True),
                                        reads=(f"str{sl}", f"va{jb // 4}"),
                                        writes=(f"pb{ph * 4 + q}",), tag=f"px{bf}{xi}b")
                            ms.op("scalar", lambda e, g=g, w=w, ph=ph: e.activation(
                                bass.AP(fva, g * 4 * 512,
                                        [[32 * 512, 128], [512, 4], [1, w]]),
                                bass.AP(fps, ph * 2048, [[4096, 128], [512, 4], [1, w]]),
                                AF.Identity),
                                reads=tuple(f"pb{ph * 4 + qq}" for qq in range(4)),
                                writes=(f"va{g}",), tag=f"pe{bf}")
                            yield
                    else:
                        gd = ts_ // 4
                        for g in range(8):
                            if g & gd:
                                continue
                            g2 = g | gd
                            sl1 = ldw(xsrc, xi, g, 8)
                            sl2 = ldw(xsrc, xi, g2, 8)
                            for (gg, sl, pbase) in ((g, sl1, 0), (g2, sl2, 4)):
                                for q in range(4):
                                    jo = gg * 4 + q
                                    ja = jo & ~ts_
                                    jb = ja + ts_
                                    pslice = fps[:, (pbase + q) * 512:
                                                 (pbase + q) * 512 + w]
                                    ms.op("tensor", lambda e, q=q, w=w, sl=sl,
                                          pslice=pslice, ja=ja: e.matmul(
                                        pslice,
                                        fstr[:, sl * 8 * 128 + (2 * q) * 128: sl * 8 * 128 + (2 * q + 1) * 128],
                                        fva[:, ja * 512: ja * 512 + w],
                                        start=True, stop=False),
                                        reads=(f"str{sl}", f"va{ja // 4}"),
                                        writes=(f"pb{pbase + q}",), tag=f"px{bf}{xi}a")
                                    ms.op("tensor", lambda e, q=q, w=w, sl=sl,
                                          pslice=pslice, jb=jb: e.matmul(
                                        pslice,
                                        fstr[:, sl * 8 * 128 + (2 * q + 1) * 128: sl * 8 * 128 + (2 * q + 2) * 128],
                                        fva[:, jb * 512: jb * 512 + w],
                                        start=False, stop=True),
                                        reads=(f"str{sl}", f"va{jb // 4}"),
                                        writes=(f"pb{pbase + q}",), tag=f"px{bf}{xi}b")
                            for (gg, pbase) in ((g, 0), (g2, 4)):
                                ms.op("scalar", lambda e, gg=gg, w=w, pbase=pbase:
                                      e.activation(
                                    bass.AP(fva, gg * 4 * 512,
                                            [[32 * 512, 128], [512, 4], [1, w]]),
                                    bass.AP(fps, pbase * 512,
                                            [[4096, 128], [512, 4], [1, w]]),
                                    AF.Identity),
                                    reads=tuple(f"pb{pbase + qq}" for qq in range(4)),
                                    writes=(f"va{gg}",), tag=f"pe{bf}")
                            yield
                    if (bf == "u" and c0 == 1024 and kind == "W"
                            and "vW" in E.dump_t):
                        ms.dma("gpsimd", lambda e: e.dma_start(
                            out=E.dump_t["vW"][:, :], in_=fva[:, :].bitcast(F32)),
                            reads=tuple(f"va{g}" for g in range(8)), tag="dbgW")
                if bf == "u" and w == 2:
                    ms.op("vector", lambda e: e.tensor_add(
                        fhb[:, :],
                        bass.AP(fva, 0, [[32 * 512, 128], [512, 32], [1, 1]]).bitcast(F32),
                        fub[:, :]),
                        reads=tuple(f"va{g}" for g in range(8)) + ("fub",),
                        writes=("fhb",), tag="bias")
                    ms.dma("gpsimd", lambda e: e.dma_start(
                        out=bass.AP(E.hbias, 0, [[1, 128], [128, 32]]),
                        in_=fhb[:, :], allow_slow_non_contiguous=True),
                        reads=("fhb",), tag="sthb")
                elif bf == "u":
                    for tb in range(w // 128):
                        for j in range(32):
                            ms.op("tensor", lambda e, j=j, tb=tb: e.transpose(
                                fps[:, j * 128:(j + 1) * 128].bitcast(F32R),
                                fva[:, j * 512 + tb * 128: j * 512 + (tb + 1) * 128],
                                fid[:, :]),
                                reads=(f"va{j // 4}", "fid"),
                                writes=(f"pb{j // 4}",), tag="tr")
                        ms.op("scalar", lambda e: e.activation(
                            fbf[:, :], fps[:, :], AF.Identity),
                            reads=tuple(f"pb{qq}" for qq in range(8)),
                            writes=("fbf0", "fbf1"), tag="trev")
                        for j in range(32):
                            ms.dma("gpsimd", lambda e, j=j, tb=tb, c0=c0: e.dma_start(
                                out=E.wexp_p[c0 + tb * 128: c0 + (tb + 1) * 128,
                                             j * 128:(j + 1) * 128],
                                in_=fbf[:, j * 128:(j + 1) * 128]),
                                reads=(f"fbf{j // 16}",), tag="stwe")
                        yield
                else:
                    for g in range(8):
                        fb = g % 2
                        ms.op("scalar", lambda e, g=g, w=w, fb=fb: e.activation(
                            bass.AP(fbf, fb * 2048, [[4096, 128], [512, 4], [1, w]]),
                            bass.AP(fva, g * 4 * 512,
                                    [[32 * 512, 128], [512, 4], [1, w]]),
                            AF.Identity),
                            reads=(f"va{g}",), writes=(f"fbf{fb}",), tag="dnev")
                        for q in range(4):
                            jo = g * 4 + q
                            ms.dma("gpsimd", lambda e, jo=jo, q=q, fb=fb, c0=c0, w=w:
                                   e.dma_start(
                                out=E.wcon_p[jo * 128:(jo + 1) * 128, c0:c0 + w],
                                in_=bass.AP(fbf, fb * 2048 + q * 512,
                                            [[4096, 128], [1, w]])),
                                reads=(f"fbf{fb}",), tag="stwc")
                        if fb == 1:
                            yield


def phase_lnfold(nc, E, st, do_ln=True, do_fold=True):
    with contextlib.ExitStack() as est:
        t_ = lambda name, shape, dt=F32: est.enter_context(
            nc.sbuf_tensor(name, shape, dt))
        B = Env()
        if do_ln:
            B.xln = t_("xln", [128, 2 * DM])
            B.lsq = t_("lsq", [128, DM])
            for nm in ("ls1", "ls2", "lsv", "lmu", "lnmu", "lvar", "lrst", "lt0"):
                setattr(B, nm, t_(nm, [128, 2]))
            B.xnf = t_("xnf", [128, 2 * DM])
            B.xnb = t_("xnb", [128, 2 * DM], BF16)
            B.xnbt = t_("xnbt", [128, 2 * 128], BF16)
        if do_fold:
            B.fstg = t_("fstg", [128, 4 * 8 * 128])
            B.fstr = t_("fstr", [128, 4 * 8 * 128], F32R)
            B.fR = t_("fR", [128, 32 * 128], F32R)
            B.fvs = t_("fvs", [128, 2 * 4 * 512])
            B.fva = t_("fva", [128, 32 * 512], F32R)
            B.fbf = t_("fbf", [128, 4096], BF16)
            B.fid = t_("fid", [128, 128], F32R)
            B.fide = t_("fide", [128, 128])
            B.fub = t_("fub", [128, 32])
            B.fhb = t_("fhb", [128, 32])
            B.fps = est.enter_context(nc.psum_tensor("fps", [128, 4096], F32))
        blk = est.enter_context(nc.Block())
        ms = MiniSched(nc, E.pool, "lf")
        gens = []
        if do_ln:
            gens.append((gen_ln(nc, E, ms, B), 1))
        if do_fold:
            gens.append((gen_fold(nc, E, ms, B), 10))
        live = [g for g in gens]
        while live:
            nxt = []
            for (g, k) in live:
                alive = True
                for _ in range(k):
                    try:
                        next(g)
                    except StopIteration:
                        alive = False
                        break
                if alive:
                    nxt.append((g, k))
            live = nxt
        ms.run_block(blk)


def phase_mamba(nc, E, st, li):
    xsrc = E.xn_fm if li == 0 else E.cout_o1
    cin = E.cin_o1 if li == 0 else E.cin_o2
    cout = E.cout_o1 if li == 0 else E.cout_o2
    TM = 512
    NC = L // TM

    with contextlib.ExitStack() as est:
        t_ = lambda name, shape, dt=F32: est.enter_context(
            nc.sbuf_tensor(f"{name}_{li}", shape, dt))
        wstg = t_("wstg", [128, 2 * 2048])
        winb = t_("winb", [128, 8 * 2048], BF16)
        wxpb = t_("wxpb", [128, 8 * 96], BF16)
        mcv = t_("mcv", [128, 8 * DC])
        mbc = t_("mbc", [128, 8])
        xrh = t_("xrh", [128, 2 * 8 * TM], BF16)
        xcs = t_("xcs", [128, 8 * (TM + 3)])
        cac = t_("cac", [128, TM])
        mub = t_("mub", [128, 8 * TM], BF16)
        mzb = t_("mzb", [128, 8 * TM], BF16)
        xdp = t_("xdp", [128, 8 * TM])
        mps = est.enter_context(nc.psum_tensor(f"mps_{li}", [128, 4096], F32))
        blk = est.enter_context(nc.Block())
        ms = MiniSched(nc, E.pool, "ma")

        for kd in range(8):
            sl = kd % 2
            ms.dma("sync", lambda e, kd=kd, sl=sl: e.dma_start(
                out=wstg[:, sl * 2048:(sl + 1) * 2048],
                in_=E.win[li][kd * 128:(kd + 1) * 128, :]),
                writes=(f"wstg{sl}",), tag="ldwin")
            ms.op("scalar", lambda e, kd=kd, sl=sl: e.activation(
                winb[:, kd * 2048:(kd + 1) * 2048],
                wstg[:, sl * 2048:(sl + 1) * 2048], AF.Identity),
                reads=(f"wstg{sl}",), writes=(f"winb{kd}",), tag="cwin")
        for kd in range(8):
            sl = kd % 2
            ms.dma("sync", lambda e, kd=kd, sl=sl: e.dma_start(
                out=bass.AP(wstg, sl * 2048, [[2 * 2048, 128], [1, 96]]),
                in_=E.wxp[li][kd * 128:(kd + 1) * 128, :]),
                writes=(f"wstg{sl}",), tag="ldwxp")
            ms.op("scalar", lambda e, kd=kd, sl=sl: e.activation(
                wxpb[:, kd * 96:(kd + 1) * 96],
                bass.AP(wstg, sl * 2048, [[2 * 2048, 128], [1, 96]]), AF.Identity),
                reads=(f"wstg{sl}",), writes=(f"wxpb{kd}",), tag="cwxp")
        ms.dma("sync", lambda e: e.dma_start(out=mcv[:, :], in_=bass.AP(
            E.wconv[li], 0, [[DC, 128], [128 * DC, 8], [1, DC]])),
            writes=("mcv",), tag="ldcv")
        ms.dma("sync", lambda e: e.dma_start(out=mbc[:, :], in_=bass.AP(
            E.bconv[li], 0, [[1, 128], [128, 8]]),
            allow_slow_non_contiguous=True), writes=("mbc",), tag="ldbc")

        for t in range(NC):
            p = t % 2
            if li == 0:
                xap = bass.AP(xsrc, t * TM, [[L, 128], [128 * L, 8], [1, TM]])
            else:
                xap = bass.AP(xsrc, t * DM * TM, [[TM, 128], [128 * TM, 8], [1, TM]])
            ms.dma("sync", lambda e, xap=xap, p=p: e.dma_start(
                out=xrh[:, p * 8 * TM:(p + 1) * 8 * TM], in_=xap),
                writes=(f"xrh{p}",), tag="ldxr")
            for m in range(16):
                bank = m % 8
                for kd in range(8):
                    ms.op("tensor", lambda e, m=m, kd=kd, bank=bank, p=p: e.matmul(
                        mps[:, bank * 512: bank * 512 + TM],
                        winb[:, kd * 2048 + m * 128: kd * 2048 + (m + 1) * 128],
                        xrh[:, p * 8 * TM + kd * TM: p * 8 * TM + (kd + 1) * TM],
                        start=(kd == 0), stop=(kd == 7)),
                        reads=(f"winb{kd}", f"xrh{p}"), writes=(f"pb{bank}",),
                        tag="mmin")
                if m < 8:
                    ms.op("scalar", lambda e, m=m, bank=bank: e.activation(
                        xcs[:, m * (TM + 3) + 3: (m + 1) * (TM + 3)],
                        mps[:, bank * 512: bank * 512 + TM], AF.Identity),
                        reads=(f"pb{bank}",), writes=(f"xcs{m}",), tag="evxc")
                else:
                    d = m - 8
                    ms.op("scalar", lambda e, d=d, bank=bank: e.activation(
                        mzb[:, d * TM:(d + 1) * TM],
                        mps[:, bank * 512: bank * 512 + TM], AF.Silu),
                        reads=(f"pb{bank}",), writes=(f"mzb{d}",), tag="evz")
            ms.dma("gpsimd", lambda e, t=t: e.dma_start(
                out=bass.AP(E.zt[li], t * TM, [[L, 128], [128 * L, 8], [1, TM]]),
                in_=mzb[:, :]),
                reads=tuple(f"mzb{d}" for d in range(8)), tag="stz")
            for d in range(8):
                if t == 0:
                    ms.op("vector", lambda e, d=d: e.memset(
                        xcs[:, d * (TM + 3): d * (TM + 3) + 3], 0.0),
                        writes=(f"xch{d}",), tag="cvz")
                ms.op("vector", lambda e, d=d: e.tensor_scalar_mul(
                    cac[:, :], xcs[:, d * (TM + 3): d * (TM + 3) + TM],
                    mcv[:, d * DC: d * DC + 1]),
                    reads=(f"xcs{d}", f"xch{d}", "mcv"), writes=("cac",), tag="cv0")
                for k in range(1, DC):
                    ms.op("vector", lambda e, d=d, k=k: e.scalar_tensor_tensor(
                        cac[:, :], xcs[:, d * (TM + 3) + k: d * (TM + 3) + k + TM],
                        mcv[:, d * DC + k: d * DC + k + 1], cac[:, :],
                        ALU.mult, ALU.add),
                        reads=(f"xcs{d}", f"xch{d}", "mcv", "cac"), writes=("cac",),
                        tag=f"cv{k}")
                ms.op("scalar", lambda e, d=d: e.activation(
                    mub[:, d * TM:(d + 1) * TM], cac[:, :], AF.Silu,
                    bias=mbc[:, d:d + 1]),
                    reads=("cac", "mbc"), writes=(f"mub{d}",), tag="evu")
                ms.op("vector", lambda e, d=d: e.tensor_copy(
                    xcs[:, d * (TM + 3): d * (TM + 3) + 3],
                    xcs[:, d * (TM + 3) + TM: d * (TM + 3) + TM + 3]),
                    reads=(f"xcs{d}",), writes=(f"xch{d}",), tag="halo")
            ms.dma("gpsimd", lambda e, t=t: e.dma_start(
                out=bass.AP(E.ut[li], t * TM, [[L, 128], [128 * L, 8], [1, TM]]),
                in_=mub[:, :]),
                reads=tuple(f"mub{d}" for d in range(8)), tag="stu")
            for kd in range(8):
                ms.op("tensor", lambda e, kd=kd: e.matmul(
                    mps[0:96, 0:TM],
                    wxpb[:, kd * 96:(kd + 1) * 96],
                    mub[:, kd * TM:(kd + 1) * TM],
                    start=(kd == 0), stop=(kd == 7)),
                    reads=(f"mub{kd}",), writes=("pb0",), tag="mmxp")
            ms.op("scalar", lambda e, t=t: e.activation(
                bass.AP(xdp, t * TM, [[8 * TM, 96], [1, TM]]),
                mps[0:96, 0:TM], AF.Identity),
                reads=("pb0",), writes=(f"xdp{t}",), tag="evxp")
            ms.dma("gpsimd", lambda e, t=t: e.dma_start(
                out=bass.AP(E.cin_xd[li], t * 96 * TM, [[TM, 96], [1, TM]]),
                in_=bass.AP(xdp, t * TM, [[8 * TM, 96], [1, TM]])),
                reads=(f"xdp{t}",), writes=(f"cinxd{t}",), tag="stxd")
            ms.cc("gpsimd", lambda e, t=t: e.collective_compute(
                "AllReduce", ALU.add, replica_groups=RG_PAIRS,
                ins=[bass.AP(E.cin_xd[li], t * 96 * TM, [[TM, 96], [1, TM]])],
                outs=[bass.AP(E.cout_xd[li], t * 96 * TM, [[TM, 96], [1, TM]])]),
                reads=(f"cinxd{t}",), writes=(f"xdred{t}",), tag="ccxd")
        ms.op("gpsimd", lambda e: e.nop(),
              reads=tuple(f"xdred{t}" for t in range(NC)), tag="ccwait")
        ms.run_block(blk)

    TS_ = 512
    NS = L // TS_
    with contextlib.ExitStack() as est:
        t_ = lambda name, shape, dt=F32: est.enter_context(
            nc.sbuf_tensor(f"{name}b{li}", shape, dt))
        dtb = t_("dtb", [64, L], BF16)
        bcb = t_("bcb", [32, L], BF16)
        wdtb = t_("wdtb", [64, 1024], BF16)
        woutb = t_("woutb", [128, 8 * 1024], BF16)
        wstg2 = t_("wstg2", [128, 1024])
        maneg = t_("maneg", [128, 8 * NST])
        mbdt = t_("mbdt", [128, 8])
        mdv = t_("mdv", [128, 8])
        idb = t_("idb", [128, 128], BF16)
        idf = t_("idf", [128, 128])
        mdel = t_("mdel", [128, 8 * TS_])
        mdu = t_("mdu", [128, 8 * TS_], BF16)
        muc = t_("muc", [128, 8 * TS_], BF16)
        mzc = t_("mzc", [128, 8 * TS_], BF16)
        mbt = t_("mbt", [128, NST * TS_], BF16)
        mct = t_("mct", [128, NST * TS_], BF16)
        mda = t_("mda", [128, 4 * TS_])
        mdbu = t_("mdbu", [128, 2 * NST * TS_], BF16)
        mh = t_("mh", [128, NST * TS_], BF16)
        mhst = t_("mhst", [128, 8 * NST])
        mtt = t_("mtt", [128, TS_])
        msp = t_("msp", [128, 4 * TS_])
        my2 = t_("my2", [128, 8 * TS_], BF16)
        ost = t_("ost", [128, 2 * TS_], BF16 if li == 0 else F32)
        mps2 = est.enter_context(nc.psum_tensor(f"mps2_{li}", [128, 4096], F32))
        blk = est.enter_context(nc.Block())
        ms = MiniSched(nc, E.pool, "mb")

        for q in range(NS):
            ms.dma("sync", lambda e, q=q: e.dma_start(
                out=wstg2[0:96, 0:TS_],
                in_=bass.AP(E.cout_xd[li], q * 96 * TS_, [[TS_, 96], [1, TS_]])),
                writes=("wstg0",), tag="ldxd")
            ms.op("scalar", lambda e, q=q: e.activation(
                dtb[:, q * TS_:(q + 1) * TS_], wstg2[0:64, 0:TS_], AF.Identity),
                reads=("wstg0",), writes=("dtb",), tag="cdt")
            ms.op("scalar", lambda e, q=q: e.activation(
                bcb[:, q * TS_:(q + 1) * TS_], wstg2[64:96, 0:TS_], AF.Identity),
                reads=("wstg0",), writes=("bcb",), tag="cbc")
        ms.dma("gpsimd", lambda e: e.dma_start(out=E.bct[li][:, :], in_=bcb[:, :]),
               reads=("bcb",), writes=("bct",), tag="stbc")
        ms.dma("sync", lambda e: e.dma_start(out=wstg2[0:64, 0:1024],
                                             in_=E.wdt[li][:, :]),
               writes=("wstg0",), tag="ldwdt")
        ms.op("scalar", lambda e: e.activation(wdtb[:, :], wstg2[0:64, 0:1024],
                                               AF.Identity),
              reads=("wstg0",), writes=("wdtb",), tag="cwdt")
        for kd in range(8):
            ms.dma("sync", lambda e, kd=kd: e.dma_start(
                out=wstg2[:, 0:1024],
                in_=E.wout[li][kd * 128:(kd + 1) * 128, :]),
                writes=("wstg0",), tag="ldwo")
            ms.op("scalar", lambda e, kd=kd: e.activation(
                woutb[:, kd * 1024:(kd + 1) * 1024],
                wstg2[:, 0:1024], AF.Identity),
                reads=("wstg0",), writes=(f"woutb{kd}",), tag="cwo")
        ms.dma("sync", lambda e: e.dma_start(
            out=maneg[:, :], in_=bass.AP(E.alog[li], 0,
                                         [[NST, 128], [128 * NST, 8], [1, NST]])),
            writes=("maneg",), tag="lda")
        ms.op("scalar", lambda e: e.activation(maneg[:, :], maneg[:, :], AF.Exp),
              reads=("maneg",), writes=("maneg",), tag="expa")
        ms.op("vector", lambda e: e.tensor_scalar_mul(maneg[:, :], maneg[:, :], -1.0),
              reads=("maneg",), writes=("maneg",), tag="nega")
        ms.dma("sync", lambda e: e.dma_start(
            out=mbdt[:, :], in_=bass.AP(E.bdt[li], 0, [[1, 128], [128, 8]]),
            allow_slow_non_contiguous=True), writes=("mbdt",), tag="ldbdt")
        ms.dma("sync", lambda e: e.dma_start(
            out=mdv[:, :], in_=bass.AP(E.dvec[li], 0, [[1, 128], [128, 8]]),
            allow_slow_non_contiguous=True), writes=("mdv",), tag="lddv")
        ms.dma("sync", lambda e: e.dma_start(out=idf[:, :], in_=E.ident_in[:, :]),
               writes=("idf",), tag="ldid")
        ms.op("scalar", lambda e: e.activation(idb[:, :], idf[:, :], AF.Identity),
              reads=("idf",), writes=("idb",), tag="cid")

        def sec_C(t):
            ms.dma("sync", lambda e, t=t: e.dma_start(
                out=muc[:, :],
                in_=bass.AP(E.ut[li], t * TS_, [[L, 128], [128 * L, 8], [1, TS_]])),
                writes=("muc",), tag="lduc")
            ms.dma("sync", lambda e, t=t: e.dma_start(
                out=mzc[:, :],
                in_=bass.AP(E.zt[li], t * TS_, [[L, 128], [128 * L, 8], [1, TS_]])),
                writes=("mzc",), tag="ldzc")
            for n in range(NST):
                ms.dma("sync", lambda e, n=n, t=t: e.dma_start(
                    out=mbt[:, n * TS_:(n + 1) * TS_],
                    in_=bass.AP(E.bct[li], n * L + t * TS_, [[0, 128], [1, TS_]])),
                    reads=("bct",), writes=(f"mbt{n // 4}",), tag="bcb_")
                ms.dma("sync", lambda e, n=n, t=t: e.dma_start(
                    out=mct[:, n * TS_:(n + 1) * TS_],
                    in_=bass.AP(E.bct[li], (NST + n) * L + t * TS_,
                                [[0, 128], [1, TS_]])),
                    reads=("bct",), writes=(f"mct{n // 4}",), tag="bcc_")
            for d in range(8):
                sp = d % 2
                bk = d % 4
                ms.op("tensor", lambda e, d=d, t=t, bk=bk: e.matmul(
                    mps2[:, bk * 512: bk * 512 + TS_],
                    wdtb[:, d * 128:(d + 1) * 128],
                    dtb[:, t * TS_:(t + 1) * TS_],
                    start=True, stop=True),
                    reads=("wdtb", "dtb"), writes=(f"pb{bk}",), tag="mmdt")
                ms.op("scalar", lambda e, d=d, sp=sp, bk=bk: e.activation(
                    msp[:, sp * 2 * TS_: sp * 2 * TS_ + TS_],
                    mps2[:, bk * 512: bk * 512 + TS_], AF.Abs,
                    bias=mbdt[:, d:d + 1]),
                    reads=(f"pb{bk}", "mbdt"), writes=(f"mspa{sp}",), tag="spabs")
                ms.op("scalar", lambda e, d=d, sp=sp, bk=bk: e.activation(
                    msp[:, sp * 2 * TS_ + TS_: sp * 2 * TS_ + 2 * TS_],
                    mps2[:, bk * 512: bk * 512 + TS_], AF.Relu,
                    bias=mbdt[:, d:d + 1]),
                    reads=(f"pb{bk}", "mbdt"), writes=(f"mspr{sp}",), tag="sprelu")
                ms.op("scalar", lambda e, sp=sp: e.activation(
                    msp[:, sp * 2 * TS_: sp * 2 * TS_ + TS_],
                    msp[:, sp * 2 * TS_: sp * 2 * TS_ + TS_], AF.Exp, scale=-1.0),
                    reads=(f"mspa{sp}",), writes=(f"mspa{sp}",), tag="spexp")
                ms.op("scalar", lambda e, sp=sp: e.activation(
                    msp[:, sp * 2 * TS_: sp * 2 * TS_ + TS_],
                    msp[:, sp * 2 * TS_: sp * 2 * TS_ + TS_], AF.Ln, bias=1.0),
                    reads=(f"mspa{sp}",), writes=(f"mspa{sp}",), tag="spln")
                ms.op("vector", lambda e, d=d, sp=sp: e.tensor_add(
                    mdel[:, d * TS_:(d + 1) * TS_],
                    msp[:, sp * 2 * TS_: sp * 2 * TS_ + TS_],
                    msp[:, sp * 2 * TS_ + TS_: sp * 2 * TS_ + 2 * TS_]),
                    reads=(f"mspa{sp}", f"mspr{sp}"), writes=(f"mdel{d}",),
                    tag="spadd")
                ms.op("vector", lambda e, d=d: e.tensor_mul(
                    mdu[:, d * TS_:(d + 1) * TS_],
                    mdel[:, d * TS_:(d + 1) * TS_],
                    muc[:, d * TS_:(d + 1) * TS_]),
                    reads=(f"mdel{d}", "muc"), writes=(f"mdu{d}",), tag="du")
                if d == 0:
                    for n in range(4):
                        ms.op("scalar", lambda e, n=n: e.activation(
                            mda[:, n * TS_:(n + 1) * TS_],
                            mdel[:, 0:TS_], AF.Exp,
                            scale=maneg[:, n:n + 1]),
                            reads=("mdel0", "maneg"), writes=(f"mda{n}",),
                            tag="exppre")

        def tails(t, d):
            bk = d % 4
            ms.op("vector", lambda e, d=d, bk=bk: e.scalar_tensor_tensor(
                mtt[:, :], muc[:, d * TS_:(d + 1) * TS_],
                mdv[:, d:d + 1],
                mps2[:, bk * 512: bk * 512 + TS_],
                ALU.mult, ALU.add),
                reads=("muc", "mdv", f"pb{bk}"), writes=("mtt",), tag="tail1")
            ms.op("vector", lambda e, d=d: e.tensor_mul(
                my2[:, d * TS_:(d + 1) * TS_], mtt[:, :],
                mzc[:, d * TS_:(d + 1) * TS_]),
                reads=("mtt", "mzc"), writes=(f"my2{d}",), tag="tail2")

        def wave1(t, kd):
            for mo in range(4):
                ms.op("tensor", lambda e, mo=mo, kd=kd: e.matmul(
                    mps2[:, (4 + mo) * 512: (4 + mo) * 512 + TS_],
                    woutb[:, kd * 1024 + mo * 128: kd * 1024 + (mo + 1) * 128],
                    my2[:, kd * TS_:(kd + 1) * TS_],
                    start=(kd == 0), stop=(kd == 7)),
                    reads=(f"woutb{kd}", f"my2{kd}"), writes=(f"pb{4 + mo}",),
                    tag="mmo1")

        def sec_D(t):
            for d in range(8):
                sx = (d % 2) * NST * TS_
                kx = f"X{d % 2}"
                ms.op("vector", lambda e, d=d, sx=sx: e.tensor_mul(
                    mdbu[:, sx: sx + NST * TS_],
                    bass.AP(mdu, d * TS_, [[8 * TS_, 128], [0, NST], [1, TS_]]),
                    mbt[:, :]),
                    reads=(f"mdu{d}",) + tuple(f"mbt{g}" for g in range(4)),
                    writes=(kx,), tag="dbu")
                for n in range(NST):
                    pp = n % 4
                    if not (d == 0 and n < 4):
                        ms.op("scalar", lambda e, d=d, n=n, pp=pp: e.activation(
                            mda[:, pp * TS_:(pp + 1) * TS_],
                            mdel[:, d * TS_:(d + 1) * TS_], AF.Exp,
                            scale=maneg[:, d * NST + n: d * NST + n + 1]),
                            reads=(f"mdel{d}", "maneg"), writes=(f"mda{pp}",),
                            tag="expda")
                    if t == 0:
                        init = 0.0
                        rd_in = ()
                    else:
                        init = mhst[:, d * NST + n: d * NST + n + 1]
                        rd_in = (f"mhst{d}",)
                    ms.op("vector",
                          lambda e, n=n, pp=pp, sx=sx, init=init: e.tensor_tensor_scan(
                              mh[:, n * TS_:(n + 1) * TS_],
                              mda[:, pp * TS_:(pp + 1) * TS_],
                              mdbu[:, sx + n * TS_: sx + (n + 1) * TS_],
                              init, ALU.mult, ALU.add),
                          reads=(f"mda{pp}", kx) + rd_in,
                          writes=(f"mhn{n}",), tag="scan")
                ms.op("vector", lambda e, d=d: e.tensor_copy(
                    mhst[:, d * NST:(d + 1) * NST],
                    bass.AP(mh, TS_ - 1, [[NST * TS_, 128], [TS_, NST], [1, 1]])),
                    reads=tuple(f"mhn{n}" for n in range(NST)),
                    writes=(f"mhst{d}",), tag="hsave")
                ms.op("vector", lambda e, sx=sx: e.tensor_mul(
                    mdbu[:, sx: sx + NST * TS_], mh[:, :], mct[:, :]),
                    reads=tuple(f"mhn{n}" for n in range(NST))
                    + tuple(f"mct{g}" for g in range(4)),
                    writes=(kx,), tag="hc")
                for n in range(NST):
                    ms.op("tensor", lambda e, d=d, n=n, sx=sx: e.matmul(
                        mps2[:, (d % 4) * 512: (d % 4) * 512 + TS_],
                        idb[:, :],
                        mdbu[:, sx + n * TS_: sx + (n + 1) * TS_],
                        start=(n == 0), stop=(n == NST - 1)),
                        reads=(kx, "idb"), writes=(f"pb{d % 4}",), tag="mmy")
                if d >= 1:
                    tails(t, d - 1)
                    wave1(t, d - 1)
            tails(t, 7)
            wave1(t, 7)
            for mo in range(4):
                po = mo % 2
                ms.op("scalar", lambda e, mo=mo, po=po: e.activation(
                    ost[:, po * TS_:(po + 1) * TS_],
                    mps2[:, (4 + mo) * 512: (4 + mo) * 512 + TS_], AF.Identity),
                    reads=(f"pb{4 + mo}",), writes=(f"ost{po}",), tag="evo")
                ms.dma("gpsimd", lambda e, t=t, mo=mo, po=po: e.dma_start(
                    out=bass.AP(cin, t * DM * TS_ + mo * 128 * TS_,
                                [[TS_, 128], [1, TS_]]),
                    in_=ost[:, po * TS_:(po + 1) * TS_]),
                    reads=(f"ost{po}",), writes=(f"cino{t}_{mo}",), tag="sto1")

        def sec_E2(t):
            for mo in range(4, 8):
                po = mo % 2
                for kd in range(8):
                    ms.op("tensor", lambda e, mo=mo, kd=kd: e.matmul(
                        mps2[:, mo * 512: mo * 512 + TS_],
                        woutb[:, kd * 1024 + mo * 128: kd * 1024 + (mo + 1) * 128],
                        my2[:, kd * TS_:(kd + 1) * TS_],
                        start=(kd == 0), stop=(kd == 7)),
                        reads=(f"woutb{kd}", f"my2{kd}"), writes=(f"pb{mo}",),
                        tag="mmo2")
                ms.op("scalar", lambda e, mo=mo, po=po: e.activation(
                    ost[:, po * TS_:(po + 1) * TS_],
                    mps2[:, mo * 512: mo * 512 + TS_], AF.Identity),
                    reads=(f"pb{mo}",), writes=(f"ost{po}",), tag="evo")
                ms.dma("gpsimd", lambda e, t=t, mo=mo, po=po: e.dma_start(
                    out=bass.AP(cin, t * DM * TS_ + mo * 128 * TS_,
                                [[TS_, 128], [1, TS_]]),
                    in_=ost[:, po * TS_:(po + 1) * TS_]),
                    reads=(f"ost{po}",), writes=(f"cino{t}_{mo}",), tag="sto1")
            ms.cc("gpsimd", lambda e, t=t: e.collective_compute(
                "AllReduce", ALU.add, replica_groups=RG_PAIRS,
                ins=[bass.AP(cin, t * DM * TS_, [[TS_, DM], [1, TS_]])],
                outs=[bass.AP(cout, t * DM * TS_, [[TS_, DM], [1, TS_]])]),
                reads=tuple(f"cino{t}_{mo}" for mo in range(8)),
                writes=(f"ored{t}",), tag="cco")

        sec_C(0)
        for t in range(NS):
            sec_D(t)
            if t + 1 < NS:
                sec_C(t + 1)
            sec_E2(t)
        ms.op("gpsimd", lambda e: e.nop(),
              reads=tuple(f"ored{t}" for t in range(NS)), tag="ccwait")
        ms.run_block(blk)


def phase_ffn(nc, E, st):
    pass
    with (
        nc.sbuf_tensor("we", [128, 8 * 32 * 128], BF16) as we,
        nc.sbuf_tensor("wc", [128, 32 * 8 * 128], BF16) as wc,
        nc.sbuf_tensor("fxn", [128, 2 * 8 * TF], BF16) as fxn,
        nc.sbuf_tensor("fh2", [128, 32 * TF], BF16) as fh2,
        nc.sbuf_tensor("fhb2", [128, 32], F32) as fhb2,
        nc.sbuf_tensor("fbs", [128, 8], F32) as fbs,
        nc.sbuf_tensor("fbc", [128, 8], F32) as fbc,
        nc.sbuf_tensor("fbd", [128, 8], F32) as fbd,
        nc.sbuf_tensor("fo", [128, 2 * TF], F32) as fo,
        nc.psum_tensor("fps2", [128, 4096], F32) as fps2,
        nc.Block() as blk,
    ):
        ms = MiniSched(nc, E.pool, "ff")

        def mk_hreg(e):
            if not hasattr(E, "hreg"):
                E.hreg = e.alloc_register("hoff")
                pid = e.partition_id()
                e.reg_alu(E.hreg, pid, 1, ALU.bitwise_and)
                e.reg_mul(E.hreg, E.hreg, LHALF)
            return e.nop()
        ms.op("sync", mk_hreg, tag="hreg")
        for kd in range(8):
            ms.dma("sync", lambda e, kd=kd: e.dma_start(
                out=we[:, kd * 32 * 128:(kd + 1) * 32 * 128],
                in_=E.wexp_p[kd * 128:(kd + 1) * 128, :]),
                writes=(f"we{kd}",), tag="ldwe")
        for kh in range(32):
            ms.dma("sync", lambda e, kh=kh: e.dma_start(
                out=wc[:, kh * 8 * 128:(kh + 1) * 8 * 128],
                in_=E.wcon_p[kh * 128:(kh + 1) * 128, :]),
                writes=(f"wc{kh}",), tag="ldwc")
        ms.dma("sync", lambda e: e.dma_start(
            out=fhb2[:, :], in_=bass.AP(E.hbias, 0, [[1, 128], [128, 32]]),
            allow_slow_non_contiguous=True), writes=("fhb2",))
        ms.dma("sync", lambda e: e.dma_start(
            out=fbc[:, :], in_=bass.AP(E.bcon, 0, [[1, 128], [128, 8]]),
            allow_slow_non_contiguous=True), writes=("fbc",))
        ms.dma("sync", lambda e: e.dma_start(
            out=fbd[:, :], in_=bass.AP(E.dbias, 0, [[1, 128], [128, 8]]),
            allow_slow_non_contiguous=True), writes=("fbd",))
        ms.op("vector", lambda e: e.tensor_add(fbs[:, :], fbc[:, :], fbd[:, :]),
              reads=("fbc", "fbd"), writes=("fbs",))

        for fc in range(NFC):
            p = fc % 2

            def ldxn(e, p=p, fc=fc):
                with e.register() as r:
                    e.reg_add(r, E.hreg, fc * TF)
                    return e.dma_start(
                        out=fxn[:, p * 8 * TF:(p + 1) * 8 * TF],
                        in_=bass.AP(E.xn_fm, r, [[L, 128], [128 * L, 8], [1, TF]]))
            ms.dma("sync", ldxn, writes=(f"fxn{p}",), tag="ldxn")
            for mh in range(32):
                bank = mh % 4
                for kd in range(8):
                    ms.op("tensor", lambda e, mh=mh, kd=kd, bank=bank, p=p: e.matmul(
                        fps2[:, bank * 512: bank * 512 + TF],
                        we[:, (kd * 32 + mh) * 128:(kd * 32 + mh + 1) * 128],
                        fxn[:, p * 8 * TF + kd * TF: p * 8 * TF + (kd + 1) * TF],
                        start=(kd == 0), stop=(kd == 7)),
                        reads=(f"we{kd}", f"fxn{p}"), writes=(f"pb{bank}",),
                        tag="mmh")
                ms.op("scalar", lambda e, mh=mh, bank=bank: e.activation(
                    fh2[:, mh * TF:(mh + 1) * TF],
                    fps2[:, bank * 512: bank * 512 + TF],
                    AF.Gelu, bias=fhb2[:, mh:mh + 1]),
                    reads=(f"pb{bank}", "fhb2"), writes=(f"fh2_{mh}",), tag="gelu")
            for mo in range(8):
                bank = 4 + mo % 4
                fsl = mo % 2
                for kh in range(32):
                    ms.op("tensor", lambda e, mo=mo, kh=kh, bank=bank: e.matmul(
                        fps2[:, bank * 512: bank * 512 + TF],
                        wc[:, (kh * 8 + mo) * 128:(kh * 8 + mo + 1) * 128],
                        fh2[:, kh * TF:(kh + 1) * TF],
                        start=(kh == 0), stop=(kh == 31)),
                        reads=(f"wc{kh}", f"fh2_{kh}"), writes=(f"pb{bank}",),
                        tag="mmc")
                ms.op("scalar", lambda e, mo=mo, bank=bank, fsl=fsl: e.activation(
                    fo[:, fsl * TF:(fsl + 1) * TF],
                    fps2[:, bank * 512: bank * 512 + TF],
                    AF.Identity, bias=fbs[:, mo:mo + 1]),
                    reads=(f"pb{bank}", "fbs"), writes=(f"fo{fsl}",), tag="fev")
                ms.dma("gpsimd", lambda e, fc=fc, mo=mo, fsl=fsl: e.dma_start(
                    out=bass.AP(E.ffn_o, mo * 128 * LHALF + fc * TF,
                                [[LHALF, 128], [1, TF]]),
                    in_=fo[:, fsl * TF:(fsl + 1) * TF]),
                    reads=(f"fo{fsl}",), writes=(f"ffo{fc}_{mo}",), tag="stff")
        ms.run_block(blk)


def phase_final(nc, E, st, with_attn):
    pass
    with (
        nc.sbuf_tensor("gfo", [128, 8 * TF], F32) as gfo,
        nc.sbuf_tensor("gat", [128, 8 * TF], F32) as gat,
        nc.sbuf_tensor("gx", [128, 2 * DM], F32) as gx,
        nc.sbuf_tensor("gxn", [128, 2 * DM], F32) as gxn,
        nc.sbuf_tensor("gtm", [128, 2 * DM], F32) as gtm,
        nc.sbuf_tensor("gid", [128, 128], F32) as gid,
        nc.psum_tensor("gps", [128, 2 * DM], F32) as gps,
        nc.Block() as blk,
    ):
        ms = MiniSched(nc, E.pool, "fn")
        ms.dma("sync", lambda e: e.dma_start(out=gid[:, :], in_=E.ident_in[:, :]),
               writes=("gid",))
        for fc in range(NFC):
            cT = fc * TF
            ms.dma("sync", lambda e, cT=cT: e.dma_start(
                out=gfo[:, :],
                in_=bass.AP(E.ffn_o, cT, [[LHALF, 128], [128 * LHALF, 8], [1, TF]])),
                writes=("gfo",), tag="ldfo")
            if with_attn:
                def ldat(e, fc=fc):
                    with e.register() as r:
                        e.reg_mul(r, E.hreg, DM)
                        e.reg_add(r, r, fc * DM * TF)
                        return e.dma_start(
                            out=gat[:, :],
                            in_=bass.AP(E.cout_o2, r,
                                        [[TF, 128], [128 * TF, 8], [1, TF]]))
                ms.dma("sync", ldat, writes=("gat",), tag="ldat")
                ms.op("vector", lambda e: e.tensor_add(gfo[:, :], gfo[:, :], gat[:, :]),
                      reads=("gfo", "gat"), writes=("gfo",), tag="addat")
            for v in range(4):
                pv = v % 2
                for j in range(8):
                    ms.op("tensor", lambda e, j=j, v=v, pv=pv: e.transpose(
                        gps[:, pv * DM + j * 128: pv * DM + (j + 1) * 128],
                        gfo[:, j * TF + v * 128: j * TF + (v + 1) * 128],
                        gid[:, :]),
                        reads=("gfo", "gid"), writes=(f"gpb{pv * 2 + j // 4}",),
                        tag="ftr")
                r0c = fc * TF + v * 128

                def ldx(e, r0c=r0c, pv=pv):
                    with e.register() as r:
                        e.reg_add(r, E.hreg, r0c)
                        e.reg_mul(r, r, DM)
                        return e.dma_start(
                            out=gx[:, pv * DM:(pv + 1) * DM],
                            in_=bass.AP(E.x_in, r, [[DM, 128], [1, DM]]))
                ms.dma("sync", ldx, writes=(f"gx{pv}",), tag="ldx")

                def ldxn2(e, r0c=r0c, pv=pv):
                    with e.register() as r:
                        e.reg_add(r, E.hreg, r0c)
                        e.reg_mul(r, r, DM)
                        return e.dma_start(
                            out=gxn[:, pv * DM:(pv + 1) * DM],
                            in_=bass.AP(E.xn_tm, r, [[DM, 128], [1, DM]]))
                ms.dma("sync", ldxn2, writes=(f"gxn{pv}",), tag="ldxn2")
                ms.op("vector", lambda e, pv=pv: e.tensor_add(
                    gtm[:, pv * DM:(pv + 1) * DM],
                    gps[:, pv * DM:(pv + 1) * DM],
                    gx[:, pv * DM:(pv + 1) * DM]),
                    reads=(f"gpb{pv * 2}", f"gpb{pv * 2 + 1}", f"gx{pv}"),
                    writes=(f"gtm{pv}",), tag="add1")
                if with_attn:
                    ms.op("vector", lambda e, pv=pv: e.tensor_add(
                        gtm[:, pv * DM:(pv + 1) * DM],
                        gtm[:, pv * DM:(pv + 1) * DM],
                        gxn[:, pv * DM:(pv + 1) * DM]),
                        reads=(f"gtm{pv}", f"gxn{pv}"), writes=(f"gtm{pv}",),
                        tag="add2")
                ms.dma("gpsimd", lambda e, fc=fc, v=v, pv=pv: e.dma_start(
                    out=E.y_out[fc * TF + v * 128: fc * TF + (v + 1) * 128, :],
                    in_=gtm[:, pv * DM:(pv + 1) * DM]),
                    reads=(f"gtm{pv}",), tag="sty")
        ms.run_block(blk)


def build_program(dump=(), phases=("ln",)):
    nc = bacc.Bacc(target_bir_lowering=False, trn_type="TRN2")
    dump = set(dump)
    E = Env()

    E.x_in = nc.dram_tensor("x", [L, DM], F32, kind="ExternalInput")
    E.y_out = nc.dram_tensor("y", [LHALF, DM], F32, kind="ExternalOutput")
    E.xn_fm = nc.dram_tensor("xn_fm", [DM, L], BF16, kind="Internal")
    E.xn_tm = nc.dram_tensor("xn_tm", [L, DM], F32, kind="Internal")
    E.wexpc = nc.dram_tensor("wexpc", [2, 8, 128, 4 * 512], F32,
                             kind="ExternalInput")
    E.wexpcb = nc.dram_tensor("wexpcb", [8, 128, 4 * 2], F32,
                              kind="ExternalInput")
    E.upb = nc.dram_tensor("upb", [HID, 1], F32, kind="ExternalInput")
    E.wcon = nc.dram_tensor("wcon", [2, 8, 128, 4 * 512], F32,
                            kind="ExternalInput")
    E.bcon = nc.dram_tensor("bcon", [DM, 1], F32, kind="ExternalInput")
    E.dbias = nc.dram_tensor("dbias", [DM, 1], F32, kind="ExternalInput")
    E.bfu_within = nc.dram_tensor("bfu_within", [7, 4, 128, 8 * 128], F32,
                                  kind="ExternalInput")
    E.bfd_within = nc.dram_tensor("bfd_within", [7, 4, 128, 8 * 128], F32,
                                  kind="ExternalInput")
    E.bfu_cross = nc.dram_tensor("bfu_cross", [5, 8, 128, 8 * 128], F32,
                                 kind="ExternalInput")
    E.bfd_cross = nc.dram_tensor("bfd_cross", [5, 8, 128, 8 * 128], F32,
                                 kind="ExternalInput")
    E.bfu_xv = nc.dram_tensor("bfu_xv", [128, 10 * 32], F32, kind="ExternalInput")
    E.ident_in = nc.dram_tensor("ident", [128, 128], F32, kind="ExternalInput")
    E.wexp_p = nc.dram_tensor("wexp_p", [DM, HID], BF16, kind="Internal")
    E.hbias = nc.dram_tensor("hbias", [HID, 1], F32, kind="Internal")
    E.wcon_p = nc.dram_tensor("wcon_p", [HID, DM], BF16, kind="Internal")
    E.ffn_o = nc.dram_tensor("ffn_o", [DM, LHALF], F32, kind="Internal")
    E.cin_o2 = nc.dram_tensor("cino2", [DM, L], F32, kind="Internal")
    E.cout_o2 = nc.dram_tensor("couto2", [DM, L], F32, kind="Internal")
    E.cin_o1 = nc.dram_tensor("cino1", [DM, L], BF16, kind="Internal")
    E.cout_o1 = nc.dram_tensor("couto1", [DM, L], BF16, kind="Internal")
    E.win, E.wconv, E.bconv, E.wxp, E.wdt, E.bdt = [], [], [], [], [], []
    E.alog, E.dvec, E.wout, E.zt, E.ut, E.bct = [], [], [], [], [], []
    E.cin_xd, E.cout_xd = [], []
    for i in range(2):
        E.win.append(nc.dram_tensor(f"win{i}", [DM, 2 * DH], F32, kind="ExternalInput"))
        E.wconv.append(nc.dram_tensor(f"wconv{i}", [DH, DC], F32, kind="ExternalInput"))
        E.bconv.append(nc.dram_tensor(f"bconv{i}", [DH, 1], F32, kind="ExternalInput"))
        E.wxp.append(nc.dram_tensor(f"wxp{i}", [DH, 96], F32, kind="ExternalInput"))
        E.wdt.append(nc.dram_tensor(f"wdt{i}", [RNK, DH], F32, kind="ExternalInput"))
        E.bdt.append(nc.dram_tensor(f"bdt{i}", [DH, 1], F32, kind="ExternalInput"))
        E.alog.append(nc.dram_tensor(f"alog{i}", [DH, NST], F32, kind="ExternalInput"))
        E.dvec.append(nc.dram_tensor(f"dvec{i}", [DH, 1], F32, kind="ExternalInput"))
        E.wout.append(nc.dram_tensor(f"wout{i}", [DH, DM], F32, kind="ExternalInput"))
        E.zt.append(nc.dram_tensor(f"zt{i}", [DH, L], BF16, kind="Internal"))
        E.ut.append(nc.dram_tensor(f"ut{i}", [DH, L], BF16, kind="Internal"))
        E.bct.append(nc.dram_tensor(f"bct{i}", [2 * NST, L], BF16, kind="Internal"))
        E.cin_xd.append(nc.dram_tensor(f"cinxd{i}", [96, L], F32, kind="Internal"))
        E.cout_xd.append(nc.dram_tensor(f"coutxd{i}", [96, L], F32, kind="Internal"))

    E.dump_t = {}

    def mkdump(name, shape, dtype=F32):
        if name in dump:
            E.dump_t[name] = nc.dram_tensor("dbg_" + name, shape, dtype,
                                            kind="ExternalOutput")

    mkdump("xnfm", [DM, L], BF16)
    mkdump("xntm", [L, DM], F32)
    mkdump("wexpp", [DM, HID], BF16)
    mkdump("fRu", [128, 32 * 128], F32)
    mkdump("fRs", [128, 32 * 128], F32)
    mkdump("vW", [128, 32 * 512], F32)
    mkdump("wconp", [HID, DM], BF16)
    mkdump("hbias", [HID, 1], F32)
    mkdump("ffn", [DM, LHALF], F32)
    mkdump("ut0", [DH, L], BF16)
    mkdump("zt0", [DH, L], BF16)
    mkdump("xd0", [96, L], F32)
    mkdump("o1", [DM, L], BF16)
    mkdump("o2", [DM, L], F32)

    st = contextlib.ExitStack()
    with st:
        E.pool = SemPool(nc, st)
        if "ln" in phases or "fold" in phases:
            phase_lnfold(nc, E, st, do_ln="ln" in phases,
                         do_fold="fold" in phases)
        if "mamba" in phases:
            phase_mamba(nc, E, st, 0)
            phase_mamba(nc, E, st, 1)
        if "ffn" in phases:
            phase_ffn(nc, E, st)
        if "final" in phases:
            phase_final(nc, E, st, with_attn="mamba" in phases)

        if E.dump_t:
            with nc.Block() as blk:
                ms = MiniSched(nc, E.pool, "du")
                for nm, src in (("xnfm", E.xn_fm), ("xntm", E.xn_tm),
                                ("wexpp", E.wexp_p), ("wconp", E.wcon_p),
                                ("hbias", E.hbias), ("ffn", E.ffn_o),
                                ("ut0", E.ut[0]), ("zt0", E.zt[0]),
                                ("xd0", E.cout_xd[0]), ("o1", E.cout_o1),
                                ("o2", E.cout_o2)):
                    if nm in E.dump_t:
                        ms.dma("gpsimd", lambda e, nm=nm, src=src: e.dma_start(
                            out=E.dump_t[nm][:, :], in_=src[:, :]))
                ms.run_block(blk)
        if "final" not in phases:
            with (
                nc.sbuf_tensor("fin", [128, DM], F32) as fin,
                nc.Block() as blk,
            ):
                ms = MiniSched(nc, E.pool, "fi")
                for c in range(LHALF // 128):
                    ms.dma("gpsimd", lambda e, c=c: e.dma_start(
                        out=fin[:, :], in_=E.x_in[c * 128:(c + 1) * 128, :]),
                        writes=("fin",))
                    ms.dma("gpsimd", lambda e, c=c: e.dma_start(
                        out=E.y_out[c * 128:(c + 1) * 128, :], in_=fin[:, :]),
                        reads=("fin",))
                ms.run_block(blk)

    nc.has_collectives = True
    nc.finalize()
    return nc, E


def _bf_host_tiles(weights):
    out = {"n": [], "t": [], "xu": [], "xd": [], "xv": []}
    for st_ in range(7):
        stride = 1 << st_
        dn = np.zeros((32, 128, 128), np.float32)
        w = weights[st_].reshape(HID // (2 * stride), stride, 2, 2)
        for fa in range(HID):
            if fa & stride:
                continue
            fb = fa + stride
            q = fa // (2 * stride)
            t_ = fa % stride
            j = fa // 128
            ia, ib = fa % 128, fb % 128
            dn[j, ia, ia] = w[q, t_, 0, 0]
            dn[j, ia, ib] = w[q, t_, 0, 1]
            dn[j, ib, ia] = w[q, t_, 1, 0]
            dn[j, ib, ib] = w[q, t_, 1, 1]
        out["n"].append(dn)
        out["t"].append(np.ascontiguousarray(dn.transpose(0, 2, 1)))
    for st_ in range(7, NSTAGE):
        stride = 1 << st_
        xu = np.zeros((32, 2, 128, 128), np.float32)
        xd = np.zeros((32, 2, 128, 128), np.float32)
        xv = np.zeros((128, 32, 2), np.float32)
        w = weights[st_].reshape(HID // (2 * stride), stride, 2, 2)
        for fo in range(HID):
            jo = fo // 128
            ii = fo % 128
            fa = fo & ~stride
            q = fa // (2 * stride)
            t_ = fa % stride
            if fo == fa:
                cu = (w[q, t_, 0, 0], w[q, t_, 0, 1])
                cd = (w[q, t_, 0, 0], w[q, t_, 1, 0])
            else:
                cu = (w[q, t_, 1, 0], w[q, t_, 1, 1])
                cd = (w[q, t_, 0, 1], w[q, t_, 1, 1])
            xu[jo, 0, ii, ii], xu[jo, 1, ii, ii] = cu
            xd[jo, 0, ii, ii], xd[jo, 1, ii, ii] = cd
            xv[ii, jo, 0], xv[ii, jo, 1] = cu
        out["xu"].append(xu)
        out["xd"].append(xd)
        out["xv"].append(xv)
    return out


def _host_inputs(core, args):
    b, h = core // 2, core % 2
    hs = slice(h * DH, (h + 1) * DH)
    d = {}
    d["x"] = np.ascontiguousarray(args["L0_attn_x"][b])
    for i in range(2):
        ipw = np.asarray(args["L0_attn_in_proj_w"][i], np.float32)
        wxc = ipw[h * DH:(h + 1) * DH, :]
        wz = ipw[2 * DH + h * DH: 2 * DH + (h + 1) * DH, :]
        d[f"win{i}"] = np.ascontiguousarray(np.concatenate([wxc, wz], 0).T)
        d[f"wconv{i}"] = np.ascontiguousarray(np.asarray(args["L0_attn_conv_w"][i], np.float32)[hs, 0, :])
        d[f"bconv{i}"] = np.ascontiguousarray(np.asarray(args["L0_attn_conv_b"][i], np.float32)[hs, None])
        d[f"wxp{i}"] = np.ascontiguousarray(np.asarray(args["L0_attn_x_proj_w"][i], np.float32)[:, hs].T)
        d[f"wdt{i}"] = np.ascontiguousarray(np.asarray(args["L0_attn_dt_proj_w"][i], np.float32)[hs, :].T)
        d[f"bdt{i}"] = np.ascontiguousarray(np.asarray(args["L0_attn_dt_proj_b"][i], np.float32)[hs, None])
        d[f"alog{i}"] = np.ascontiguousarray(np.asarray(args["L0_attn_A_log"][i], np.float32)[hs, :])
        d[f"dvec{i}"] = np.ascontiguousarray(np.asarray(args["L0_attn_D"][i], np.float32)[hs, None])
        d[f"wout{i}"] = np.ascontiguousarray(np.asarray(args["L0_attn_out_proj_w"][i], np.float32)[:, hs].T)
    def grp_tiles(x):
        s = x.reshape(x.shape[0], -1, 8, 128, 128)
        return np.ascontiguousarray(s.transpose(0, 1, 3, 2, 4).reshape(
            x.shape[0], s.shape[1], 128, 8 * 128))

    def grp_v(v):
        s = v.reshape(8, 4, 128, 2, 512)
        return np.ascontiguousarray(s.transpose(3, 0, 2, 1, 4).reshape(
            2, 8, 128, 4 * 512))

    wexp_t = args["L0_ffn_W_exp"].T.astype(np.float32)
    d["wexpc"] = grp_v(wexp_t)
    bcol = np.concatenate([args["L0_ffn_b_exp"][:, None].astype(np.float32),
                           np.zeros((HID, 1), np.float32)], 1)
    sb = bcol.reshape(8, 4, 128, 2)
    d["wexpcb"] = np.ascontiguousarray(sb.transpose(0, 2, 1, 3).reshape(
        8, 128, 4 * 2))
    d["upb"] = np.ascontiguousarray(args["L0_ffn_up_bias"][:, None])
    d["wcon"] = grp_v(np.asarray(args["L0_ffn_W_con"], np.float32))
    d["bcon"] = np.ascontiguousarray(args["L0_ffn_b_con"][:, None])
    d["dbias"] = np.ascontiguousarray(args["L0_ffn_down_bias"][:, None])
    tu = _bf_host_tiles(np.asarray(args["L0_ffn_up_weights"], np.float32))
    td = _bf_host_tiles(np.asarray(args["L0_ffn_down_weights"], np.float32))
    d["bfu_within"] = grp_tiles(
        np.stack([tu["n"][ss] for ss in range(6)] + [tu["t"][6]]))
    d["bfd_within"] = grp_tiles(
        np.stack([td["n"][0]] + [td["t"][ss] for ss in range(1, 7)]))
    d["bfu_cross"] = grp_tiles(np.stack(tu["xu"]))
    d["bfd_cross"] = grp_tiles(np.stack(td["xd"]))
    d["bfu_xv"] = np.ascontiguousarray(
        np.stack(tu["xv"], 1).reshape(128, 5 * 64))
    d["ident"] = np.eye(128, dtype=np.float32)
    return d


def kernel(**inputs):
    nc, _ = build_program(phases=("ln", "fold", "mamba", "ffn", "final"))
    in_maps = [_host_inputs(c, inputs) for c in range(8)]
    res = run_bass_kernel_spmd(nc, in_maps, core_ids=list(range(8)))
    out = np.empty((4, L, DM), np.float32)
    for c in range(8):
        b, h = c // 2, c % 2
        out[b, h * LHALF:(h + 1) * LHALF, :] = res.results[c]["y"]
    return out


# version 07
# speedup vs baseline: 1.0058x; 1.0058x over previous
import contextlib
import numpy as np
import concourse.bass as bass
import concourse.bacc as bacc
import concourse.mybir as mybir
from concourse.bass_utils import run_bass_kernel_spmd

F32 = mybir.dt.float32
F32R = mybir.dt.float32r
BF16 = mybir.dt.bfloat16
AF = mybir.ActivationFunctionType
ALU = mybir.AluOpType
AXX = mybir.AxisListType.X

L = 4096
DM = 1024
DH = 1024
NDT = 8
NST = 16
RNK = 64
DC = 4
HID = 4096
NHT = 32
NSTAGE = 12
TS = 1024
NSC = L // TS
TMM = 512
NCH = L // TMM
LHALF = L // 2
TF = 512
NFC = LHALF // TF
EPS = 1e-5
NTOK = L // 128

RG_PAIRS = [[0, 1], [2, 3], [4, 5], [6, 7]]

COMPUTE_ENGINES = ("tensor", "vector", "scalar", "gpsimd", "sync")
DMA_RING = 12


class _Op:
    __slots__ = ("idx", "eng", "fn", "reads", "writes", "is_dma", "is_async",
                 "deps", "need_ms", "ms_val", "ring_slot", "pre_throttle",
                 "tag")

    def __init__(self, idx, eng, fn, reads, writes, is_dma, tag):
        self.is_async = False
        self.idx = idx
        self.eng = eng
        self.fn = fn
        self.reads = reads
        self.writes = writes
        self.is_dma = is_dma
        self.deps = []
        self.need_ms = False
        self.ms_val = None
        self.ring_slot = None
        self.pre_throttle = None
        self.tag = tag


class SemPool:

    def __init__(self, nc, stack, rings=("sync", "gpsimd", "scalar", "vector")):
        self.csem = {}
        self.cctr = {}
        for e in COMPUTE_ENGINES:
            self.csem[e] = stack.enter_context(nc.semaphore(f"ms_{e}"))
            self.cctr[e] = 0
        self.rings = {}
        for e in rings:
            self.rings[e] = {
                "sems": [stack.enter_context(nc.semaphore(f"ms_{e[:2]}d{i}"))
                         for i in range(DMA_RING)],
                "uses": [0] * DMA_RING,
                "next": 0,
            }


class MiniSched:
    def __init__(self, nc, pool, name):
        self.nc = nc
        self.name = name
        self.ops = []
        self.last_writer = {}
        self.readers = {}
        self.csem = pool.csem
        self.cctr = pool.cctr
        self.rings = pool.rings
        self.planned = False

    def op(self, eng, fn, reads=(), writes=(), tag=""):
        o = _Op(len(self.ops), eng, fn, tuple(reads), tuple(writes), False, tag)
        self._link(o)
        self.ops.append(o)
        return o

    def dma(self, eng, fn, reads=(), writes=(), tag=""):
        o = _Op(len(self.ops), eng, fn, tuple(reads), tuple(writes), True, tag)
        self._link(o)
        self.ops.append(o)
        return o

    def cc(self, eng, fn, reads=(), writes=(), tag=""):
        o = _Op(len(self.ops), eng, fn, tuple(reads), tuple(writes), False, tag)
        o.is_async = True
        o.need_ms = True
        self._link(o)
        self.ops.append(o)
        return o

    def _link(self, o):
        deps = {}
        for r in o.reads:
            w = self.last_writer.get(r)
            if w is not None:
                deps[w.idx] = w
        for w_ in o.writes:
            w = self.last_writer.get(w_)
            if w is not None:
                deps[w.idx] = w
            for rd in self.readers.get(w_, ()):
                deps[rd.idx] = rd
        o.deps = [deps[i] for i in sorted(deps) if i != o.idx]
        for w_ in o.writes:
            self.last_writer[w_] = o
            self.readers[w_] = []
        for r in o.reads:
            self.readers.setdefault(r, []).append(o)

    def plan(self):
        assert not self.planned
        for o in self.ops:
            if o.is_dma:
                ring = self.rings[o.eng]
                s = ring["next"]
                if ring["uses"][s] > 0:
                    o.pre_throttle = (ring["sems"][s], 16 * ring["uses"][s])
                ring["uses"][s] += 1
                ring["next"] = (s + 1) % DMA_RING
                o.ring_slot = s
                o.ms_val = 16 * ring["uses"][s]
            for d in o.deps:
                if (not d.is_dma) and d.eng != o.eng:
                    d.need_ms = True
        for o in self.ops:
            if (not o.is_dma) and o.need_ms:
                self.cctr[o.eng] += 1
                o.ms_val = self.cctr[o.eng]
        self.planned = True

    def emit(self, eng_name, eng):
        assert self.planned
        for o in self.ops:
            if o.eng != eng_name:
                continue
            need_drain = False
            for d in o.deps:
                if d.is_dma:
                    ring = self.rings[d.eng]
                    eng.wait_ge(ring["sems"][d.ring_slot], d.ms_val)
                elif d.eng == o.eng and not d.is_async:
                    need_drain = True
                else:
                    assert d.ms_val is not None, (o.tag, d.tag)
                    eng.wait_ge(self.csem[d.eng], d.ms_val)
            if need_drain:
                eng.drain()
            if o.pre_throttle is not None:
                eng.wait_ge(*o.pre_throttle)
            instr = o.fn(eng)
            if o.is_dma:
                instr.then_inc(self.rings[o.eng]["sems"][o.ring_slot], 16)
            elif o.need_ms:
                instr.then_inc(self.csem[o.eng], 1)

    def quiesce_ops(self):
        for e, ring in self.rings.items():
            def fn(eng, _ring=ring):
                for s in range(DMA_RING):
                    if _ring["uses"][s] > 0:
                        eng.wait_ge(_ring["sems"][s], 16 * _ring["uses"][s])
                return eng.nop()
            self.op(e, fn, tag="quiesce")

    def run_block(self, blk):
        self.quiesce_ops()
        self.plan()

        @blk.sync
        def _(e):
            self.emit("sync", e)

        @blk.scalar
        def _(e):
            self.emit("scalar", e)

        @blk.vector
        def _(e):
            self.emit("vector", e)

        @blk.tensor
        def _(e):
            self.emit("tensor", e)

        @blk.gpsimd
        def _(e):
            self.emit("gpsimd", e)


class Env:
    pass


def gen_ln(nc, E, ms, B):
    xln, lsq, ls1, ls2, lsv, lmu, lnmu, lvar, lrst, lt0, xnf, xnb, xnbt = (
        B.xln, B.lsq, B.ls1, B.ls2, B.lsv, B.lmu, B.lnmu, B.lvar, B.lrst,
        B.lt0, B.xnf, B.xnb, B.xnbt)
    if True:
        for tt in range(NTOK):
            p = tt % 2
            ms.dma("sync", lambda e, tt=tt, p=p: e.dma_start(
                out=xln[:, p * DM:(p + 1) * DM],
                in_=E.x_in[tt * 128:(tt + 1) * 128, :]),
                writes=(f"xln{p}",), tag=f"ldx{tt}")
            ms.op("scalar", lambda e, p=p: e.activation(
                lsq[:, :], xln[:, p * DM:(p + 1) * DM], AF.Square,
                accum_out=ls2[:, p:p + 1]),
                reads=(f"xln{p}",), writes=("lsq", f"ls2{p}"), tag=f"sq{tt}")
            ms.op("vector", lambda e, p=p: e.reduce_sum(
                ls1[:, p:p + 1], xln[:, p * DM:(p + 1) * DM], axis=AXX),
                reads=(f"xln{p}",), writes=(f"ls1{p}",))
            ms.op("vector", lambda e, p=p: e.tensor_scalar_mul(
                lmu[:, p:p + 1], ls1[:, p:p + 1], 1.0 / DM),
                reads=(f"ls1{p}",), writes=(f"lmu{p}",))
            ms.op("vector", lambda e, p=p: e.tensor_scalar_mul(
                lnmu[:, p:p + 1], ls1[:, p:p + 1], -1.0 / DM),
                reads=(f"ls1{p}",), writes=(f"lnmu{p}",))
            ms.op("vector", lambda e, p=p: e.tensor_mul(
                lvar[:, p:p + 1], lmu[:, p:p + 1], lmu[:, p:p + 1]),
                reads=(f"lmu{p}",), writes=(f"lvar{p}",))
            ms.op("vector", lambda e, p=p: e.scalar_tensor_tensor(
                lvar[:, p:p + 1], ls2[:, p:p + 1], 1.0 / DM, lvar[:, p:p + 1],
                ALU.mult, ALU.subtract),
                reads=(f"ls2{p}", f"lvar{p}"), writes=(f"lvar{p}",))
            ms.op("vector", lambda e, p=p: e.tensor_scalar_add(
                lvar[:, p:p + 1], lvar[:, p:p + 1], EPS),
                reads=(f"lvar{p}",), writes=(f"lvar{p}",))
            ms.op("scalar", lambda e, p=p: e.activation(
                lsv[:, p:p + 1], lvar[:, p:p + 1], AF.Sqrt),
                reads=(f"lvar{p}",), writes=(f"lsv{p}",))
            ms.op("vector", lambda e, p=p: e.reciprocal(
                lrst[:, p:p + 1], lsv[:, p:p + 1]),
                reads=(f"lsv{p}",), writes=(f"lrst{p}",))
            ms.op("vector", lambda e, p=p: e.tensor_mul(
                lt0[:, p:p + 1], lrst[:, p:p + 1], lrst[:, p:p + 1]),
                reads=(f"lrst{p}",), writes=(f"lt0{p}",))
            ms.op("vector", lambda e, p=p: e.tensor_mul(
                lt0[:, p:p + 1], lt0[:, p:p + 1], lvar[:, p:p + 1]),
                reads=(f"lt0{p}", f"lvar{p}"), writes=(f"lt0{p}",))
            ms.op("vector", lambda e, p=p: e.tensor_scalar(
                lt0[:, p:p + 1], lt0[:, p:p + 1], -0.5, 1.5, ALU.mult, ALU.add),
                reads=(f"lt0{p}",), writes=(f"lt0{p}",))
            ms.op("vector", lambda e, p=p: e.tensor_mul(
                lrst[:, p:p + 1], lrst[:, p:p + 1], lt0[:, p:p + 1]),
                reads=(f"lrst{p}", f"lt0{p}"), writes=(f"lrst{p}",))
            ms.op("vector", lambda e, p=p: e.tensor_scalar(
                xnf[:, p * DM:(p + 1) * DM], xln[:, p * DM:(p + 1) * DM],
                lnmu[:, p:p + 1], lrst[:, p:p + 1], ALU.add, ALU.mult),
                reads=(f"xln{p}", f"lnmu{p}", f"lrst{p}"), writes=(f"xnf{p}",))
            ms.op("vector", lambda e, p=p: e.tensor_scalar(
                xnb[:, p * DM:(p + 1) * DM], xln[:, p * DM:(p + 1) * DM],
                lnmu[:, p:p + 1], lrst[:, p:p + 1], ALU.add, ALU.mult),
                reads=(f"xln{p}", f"lnmu{p}", f"lrst{p}"), writes=(f"xnb{p}",))
            ms.dma("gpsimd", lambda e, tt=tt, p=p: e.dma_start(
                out=E.xn_tm[tt * 128:(tt + 1) * 128, :],
                in_=xnf[:, p * DM:(p + 1) * DM]),
                reads=(f"xnf{p}",), tag=f"st_tm{tt}")
            for j in range(8):
                k = tt * 8 + j
                q = k % 2
                ms.dma("sync", lambda e, p=p, j=j, q=q: e.dma_start_transpose(
                    xnbt[:, q * 128:(q + 1) * 128],
                    xnb[:, p * DM + j * 128: p * DM + (j + 1) * 128]),
                    reads=(f"xnb{p}",), writes=(f"xnbt{q}",))
                ms.dma("gpsimd", lambda e, tt=tt, j=j, q=q: e.dma_start(
                    out=E.xn_fm[j * 128:(j + 1) * 128, tt * 128:(tt + 1) * 128],
                    in_=xnbt[:, q * 128:(q + 1) * 128]),
                    reads=(f"xnbt{q}",))
            yield


def gen_fold(nc, E, ms, B):
    fstg, fstr, fR, fvs, fva, fbf, fid, fide, fub, fhb, fps = (
        B.fstg, B.fstr, B.fR, B.fvs, B.fva, B.fbf, B.fid, B.fide, B.fub,
        B.fhb, B.fps)
    if True:
        ms.dma("scalar", lambda e: e.dma_start(out=fide[:, :], in_=E.ident_in[:, :]),
               writes=("fide",))
        ms.op("scalar", lambda e: e.activation(fid[:, :], fide[:, :], AF.Identity),
              reads=("fide",), writes=("fid",))
        ms.dma("scalar", lambda e: e.dma_start(
            out=fub[:, :], in_=bass.AP(E.upb, 0, [[1, 128], [128, 32]]),
            allow_slow_non_contiguous=True),
            writes=("fub",))

        slot = [0]

        def ldw(src, si, gi, gstride):
            sl = slot[0]
            slot[0] = (slot[0] + 1) % 4
            base = (si * gstride + gi) * 128 * 8 * 128
            ms.dma("gpsimd", lambda e, src=src, base=base, sl=sl: e.dma_start(
                out=fstg[:, sl * 8 * 128: (sl + 1) * 8 * 128],
                in_=bass.AP(src, base, [[8 * 128, 128], [1, 8 * 128]])),
                writes=(f"stg{sl}",), tag="ldw")
            ms.op("vector", lambda e, sl=sl: e.tensor_copy(
                fstr[:, sl * 8 * 128: (sl + 1) * 8 * 128],
                fstg[:, sl * 8 * 128: (sl + 1) * 8 * 128]),
                reads=(f"stg{sl}",), writes=(f"str{sl}",), tag="castw")
            return sl

        for bf in ("u", "d"):
            wsrc = E.bfu_within if bf == "u" else E.bfd_within
            xsrc = E.bfu_cross if bf == "u" else E.bfd_cross
            seed = 6 if bf == "u" else 0
            chain = list(range(5, -1, -1)) if bf == "u" else list(range(1, 7))

            for g in range(4):
                sl = slot[0]
                slot[0] = (slot[0] + 1) % 4
                base = (seed * 4 + g) * 128 * 8 * 128
                ms.dma("gpsimd", lambda e, base=base, sl=sl, wsrc=wsrc: e.dma_start(
                    out=fstg[:, sl * 8 * 128:(sl + 1) * 8 * 128],
                    in_=bass.AP(wsrc, base, [[8 * 128, 128], [1, 8 * 128]])),
                    writes=(f"stg{sl}",), tag="ldseed")
                ms.op("vector", lambda e, g=g, sl=sl: e.tensor_copy(
                    fR[:, g * 8 * 128:(g + 1) * 8 * 128],
                    fstg[:, sl * 8 * 128:(sl + 1) * 8 * 128]),
                    reads=(f"stg{sl}",),
                    writes=tuple(f"R{g * 8 + jj}" for jj in range(8)), tag="seedcast")
                yield

            if bf == "u" and "fRs" in E.dump_t:
                ms.dma("gpsimd", lambda e: e.dma_start(
                    out=E.dump_t["fRs"][:, :], in_=fR[:, :].bitcast(F32)),
                    reads=tuple(f"R{j}" for j in range(32)), tag="dbgRs")

            for ss in chain:
                for rnd in range(4):
                    sl = ldw(wsrc, ss, rnd, 4)
                    for q in range(8):
                        j = rnd * 8 + q
                        ms.op("tensor", lambda e, q=q, j=j, sl=sl: e.matmul(
                            fps[:, q * 512: q * 512 + 128],
                            fstr[:, sl * 8 * 128 + q * 128: sl * 8 * 128 + (q + 1) * 128],
                            fR[:, j * 128:(j + 1) * 128],
                            start=True, stop=True),
                            reads=(f"str{sl}", f"R{j}"), writes=(f"pb{q}",),
                            tag=f"cmm{bf}{ss}")
                    ms.op("scalar", lambda e, rnd=rnd: e.activation(
                        bass.AP(fR, rnd * 8 * 128, [[32 * 128, 128], [128, 8], [1, 128]]),
                        bass.AP(fps, 0, [[4096, 128], [512, 8], [1, 128]]),
                        AF.Identity),
                        reads=tuple(f"pb{q}" for q in range(8)),
                        writes=tuple(f"R{rnd * 8 + q}" for q in range(8)),
                        tag=f"cev{bf}{ss}")
                    yield

            if bf == "u" and "fRu" in E.dump_t:
                ms.dma("gpsimd", lambda e: e.dma_start(
                    out=E.dump_t["fRu"][:, :], in_=fR[:, :].bitcast(F32)),
                    reads=tuple(f"R{j}" for j in range(32)), tag="dbgR")

            chunks = [(0, 512), (512, 512), (1024, 2)] if bf == "u" else [(0, 512), (512, 512)]
            vsrc = E.wexpc if bf == "u" else E.wcon
            for (c0, w) in chunks:
                vslot = [0]
                for g in range(8):
                    sl = vslot[0]
                    vslot[0] ^= 1
                    if w == 512:
                        ci = c0 // 512
                        ms.dma("gpsimd", lambda e, g=g, ci=ci, sl=sl, vsrc=vsrc:
                               e.dma_start(
                            out=fvs[:, sl * 2048:(sl + 1) * 2048],
                            in_=bass.AP(vsrc, (ci * 8 + g) * 128 * 2048,
                                        [[2048, 128], [1, 2048]])),
                            writes=(f"vs{sl}",), tag="ldv")
                        ms.op("vector", lambda e, g=g, sl=sl: e.tensor_copy(
                            fva[:, g * 2048:(g + 1) * 2048],
                            fvs[:, sl * 2048:(sl + 1) * 2048]),
                            reads=(f"vs{sl}",), writes=(f"va{g}",), tag="castv")
                    else:
                        ms.dma("gpsimd", lambda e, g=g, sl=sl: e.dma_start(
                            out=fvs[:, sl * 2048: sl * 2048 + 8],
                            in_=bass.AP(E.wexpcb, g * 128 * 8, [[8, 128], [1, 8]])),
                            writes=(f"vs{sl}",), tag="ldvb")
                        ms.op("vector", lambda e, g=g, sl=sl: e.tensor_copy(
                            bass.AP(fva, g * 4 * 512,
                                    [[32 * 512, 128], [512, 4], [1, 2]]),
                            bass.AP(fvs, sl * 2048, [[2 * 2048, 128], [2, 4], [1, 2]])),
                            reads=(f"vs{sl}",), writes=(f"va{g}",), tag="castvb")
                    if g % 2 == 1:
                        yield
                if bf == "u":
                    passes = [("W", None)] + [("X", xi) for xi in range(5)]
                else:
                    passes = [("X", xi) for xi in range(4, -1, -1)] + [("W", None)]
                for (kind, xi) in passes:
                    ts_ = 0 if kind == "W" else (1 << xi)
                    if kind == "W" or ts_ < 4:
                        for g in range(8):
                            if kind == "X":
                                sl = ldw(xsrc, xi, g, 8)
                            ph = g % 2
                            for q in range(4):
                                jo = g * 4 + q
                                pslice = fps[:, ph * 2048 + q * 512: ph * 2048 + q * 512 + w]
                                if kind == "W":
                                    ms.op("tensor", lambda e, jo=jo, w=w, pslice=pslice:
                                          e.matmul(
                                        pslice, fR[:, jo * 128:(jo + 1) * 128],
                                        fva[:, jo * 512: jo * 512 + w],
                                        start=True, stop=True),
                                        reads=(f"R{jo}", f"va{jo // 4}"),
                                        writes=(f"pb{ph * 4 + q}",), tag=f"pw{bf}")
                                else:
                                    ja = jo & ~ts_
                                    jb = ja + ts_
                                    ms.op("tensor", lambda e, q=q, w=w, sl=sl,
                                          pslice=pslice, ja=ja: e.matmul(
                                        pslice,
                                        fstr[:, sl * 8 * 128 + (2 * q) * 128: sl * 8 * 128 + (2 * q + 1) * 128],
                                        fva[:, ja * 512: ja * 512 + w],
                                        start=True, stop=False),
                                        reads=(f"str{sl}", f"va{ja // 4}"),
                                        writes=(f"pb{ph * 4 + q}",), tag=f"px{bf}{xi}a")
                                    ms.op("tensor", lambda e, q=q, w=w, sl=sl,
                                          pslice=pslice, jb=jb: e.matmul(
                                        pslice,
                                        fstr[:, sl * 8 * 128 + (2 * q + 1) * 128: sl * 8 * 128 + (2 * q + 2) * 128],
                                        fva[:, jb * 512: jb * 512 + w],
                                        start=False, stop=True),
                                        reads=(f"str{sl}", f"va{jb // 4}"),
                                        writes=(f"pb{ph * 4 + q}",), tag=f"px{bf}{xi}b")
                            ms.op("scalar", lambda e, g=g, w=w, ph=ph: e.activation(
                                bass.AP(fva, g * 4 * 512,
                                        [[32 * 512, 128], [512, 4], [1, w]]),
                                bass.AP(fps, ph * 2048, [[4096, 128], [512, 4], [1, w]]),
                                AF.Identity),
                                reads=tuple(f"pb{ph * 4 + qq}" for qq in range(4)),
                                writes=(f"va{g}",), tag=f"pe{bf}")
                            yield
                    else:
                        gd = ts_ // 4
                        for g in range(8):
                            if g & gd:
                                continue
                            g2 = g | gd
                            sl1 = ldw(xsrc, xi, g, 8)
                            sl2 = ldw(xsrc, xi, g2, 8)
                            for (gg, sl, pbase) in ((g, sl1, 0), (g2, sl2, 4)):
                                for q in range(4):
                                    jo = gg * 4 + q
                                    ja = jo & ~ts_
                                    jb = ja + ts_
                                    pslice = fps[:, (pbase + q) * 512:
                                                 (pbase + q) * 512 + w]
                                    ms.op("tensor", lambda e, q=q, w=w, sl=sl,
                                          pslice=pslice, ja=ja: e.matmul(
                                        pslice,
                                        fstr[:, sl * 8 * 128 + (2 * q) * 128: sl * 8 * 128 + (2 * q + 1) * 128],
                                        fva[:, ja * 512: ja * 512 + w],
                                        start=True, stop=False),
                                        reads=(f"str{sl}", f"va{ja // 4}"),
                                        writes=(f"pb{pbase + q}",), tag=f"px{bf}{xi}a")
                                    ms.op("tensor", lambda e, q=q, w=w, sl=sl,
                                          pslice=pslice, jb=jb: e.matmul(
                                        pslice,
                                        fstr[:, sl * 8 * 128 + (2 * q + 1) * 128: sl * 8 * 128 + (2 * q + 2) * 128],
                                        fva[:, jb * 512: jb * 512 + w],
                                        start=False, stop=True),
                                        reads=(f"str{sl}", f"va{jb // 4}"),
                                        writes=(f"pb{pbase + q}",), tag=f"px{bf}{xi}b")
                            for (gg, pbase) in ((g, 0), (g2, 4)):
                                ms.op("scalar", lambda e, gg=gg, w=w, pbase=pbase:
                                      e.activation(
                                    bass.AP(fva, gg * 4 * 512,
                                            [[32 * 512, 128], [512, 4], [1, w]]),
                                    bass.AP(fps, pbase * 512,
                                            [[4096, 128], [512, 4], [1, w]]),
                                    AF.Identity),
                                    reads=tuple(f"pb{pbase + qq}" for qq in range(4)),
                                    writes=(f"va{gg}",), tag=f"pe{bf}")
                            yield
                    if (bf == "u" and c0 == 1024 and kind == "W"
                            and "vW" in E.dump_t):
                        ms.dma("gpsimd", lambda e: e.dma_start(
                            out=E.dump_t["vW"][:, :], in_=fva[:, :].bitcast(F32)),
                            reads=tuple(f"va{g}" for g in range(8)), tag="dbgW")
                if bf == "u" and w == 2:
                    ms.op("vector", lambda e: e.tensor_add(
                        fhb[:, :],
                        bass.AP(fva, 0, [[32 * 512, 128], [512, 32], [1, 1]]).bitcast(F32),
                        fub[:, :]),
                        reads=tuple(f"va{g}" for g in range(8)) + ("fub",),
                        writes=("fhb",), tag="bias")
                    ms.dma("gpsimd", lambda e: e.dma_start(
                        out=bass.AP(E.hbias, 0, [[1, 128], [128, 32]]),
                        in_=fhb[:, :], allow_slow_non_contiguous=True),
                        reads=("fhb",), tag="sthb")
                elif bf == "u":
                    for tb in range(w // 128):
                        for j in range(32):
                            ms.op("tensor", lambda e, j=j, tb=tb: e.transpose(
                                fps[:, j * 128:(j + 1) * 128].bitcast(F32R),
                                fva[:, j * 512 + tb * 128: j * 512 + (tb + 1) * 128],
                                fid[:, :]),
                                reads=(f"va{j // 4}", "fid"),
                                writes=(f"pb{j // 4}",), tag="tr")
                        ms.op("scalar", lambda e: e.activation(
                            fbf[:, :], fps[:, :], AF.Identity),
                            reads=tuple(f"pb{qq}" for qq in range(8)),
                            writes=("fbf0", "fbf1"), tag="trev")
                        for j in range(32):
                            ms.dma("gpsimd", lambda e, j=j, tb=tb, c0=c0: e.dma_start(
                                out=E.wexp_p[c0 + tb * 128: c0 + (tb + 1) * 128,
                                             j * 128:(j + 1) * 128],
                                in_=fbf[:, j * 128:(j + 1) * 128]),
                                reads=(f"fbf{j // 16}",), tag="stwe")
                        yield
                else:
                    for g in range(8):
                        fb = g % 2
                        ms.op("scalar", lambda e, g=g, w=w, fb=fb: e.activation(
                            bass.AP(fbf, fb * 2048, [[4096, 128], [512, 4], [1, w]]),
                            bass.AP(fva, g * 4 * 512,
                                    [[32 * 512, 128], [512, 4], [1, w]]),
                            AF.Identity),
                            reads=(f"va{g}",), writes=(f"fbf{fb}",), tag="dnev")
                        for q in range(4):
                            jo = g * 4 + q
                            ms.dma("gpsimd", lambda e, jo=jo, q=q, fb=fb, c0=c0, w=w:
                                   e.dma_start(
                                out=E.wcon_p[jo * 128:(jo + 1) * 128, c0:c0 + w],
                                in_=bass.AP(fbf, fb * 2048 + q * 512,
                                            [[4096, 128], [1, w]])),
                                reads=(f"fbf{fb}",), tag="stwc")
                        if fb == 1:
                            yield


def phase_lnfold(nc, E, st, do_ln=True, do_fold=True):
    with contextlib.ExitStack() as est:
        t_ = lambda name, shape, dt=F32: est.enter_context(
            nc.sbuf_tensor(name, shape, dt))
        B = Env()
        if do_ln:
            B.xln = t_("xln", [128, 2 * DM])
            B.lsq = t_("lsq", [128, DM])
            for nm in ("ls1", "ls2", "lsv", "lmu", "lnmu", "lvar", "lrst", "lt0"):
                setattr(B, nm, t_(nm, [128, 2]))
            B.xnf = t_("xnf", [128, 2 * DM])
            B.xnb = t_("xnb", [128, 2 * DM], BF16)
            B.xnbt = t_("xnbt", [128, 2 * 128], BF16)
        if do_fold:
            B.fstg = t_("fstg", [128, 4 * 8 * 128])
            B.fstr = t_("fstr", [128, 4 * 8 * 128], F32R)
            B.fR = t_("fR", [128, 32 * 128], F32R)
            B.fvs = t_("fvs", [128, 2 * 4 * 512])
            B.fva = t_("fva", [128, 32 * 512], F32R)
            B.fbf = t_("fbf", [128, 4096], BF16)
            B.fid = t_("fid", [128, 128], F32R)
            B.fide = t_("fide", [128, 128])
            B.fub = t_("fub", [128, 32])
            B.fhb = t_("fhb", [128, 32])
            B.fps = est.enter_context(nc.psum_tensor("fps", [128, 4096], F32))
        blk = est.enter_context(nc.Block())
        ms = MiniSched(nc, E.pool, "lf")
        gens = []
        if do_ln:
            gens.append((gen_ln(nc, E, ms, B), 1))
        if do_fold:
            gens.append((gen_fold(nc, E, ms, B), 10))
        live = [g for g in gens]
        while live:
            nxt = []
            for (g, k) in live:
                alive = True
                for _ in range(k):
                    try:
                        next(g)
                    except StopIteration:
                        alive = False
                        break
                if alive:
                    nxt.append((g, k))
            live = nxt
        ms.run_block(blk)


def phase_mamba(nc, E, st, li):
    xsrc = E.xn_fm if li == 0 else E.cout_o1
    cin = E.cin_o1 if li == 0 else E.cin_o2
    cout = E.cout_o1 if li == 0 else E.cout_o2
    TM = 512
    NC = L // TM

    with contextlib.ExitStack() as est:
        t_ = lambda name, shape, dt=F32: est.enter_context(
            nc.sbuf_tensor(f"{name}_{li}", shape, dt))
        wstg = t_("wstg", [128, 2 * 2048])
        winb = t_("winb", [128, 8 * 2048], BF16)
        wxpb = t_("wxpb", [128, 8 * 96], BF16)
        mcv = t_("mcv", [128, 8 * DC])
        mbc = t_("mbc", [128, 8])
        xrh = t_("xrh", [128, 2 * 8 * TM], BF16)
        xcs = t_("xcs", [128, 8 * (TM + 3)])
        cac = t_("cac", [128, TM])
        mub = t_("mub", [128, 8 * TM], BF16)
        mzb = t_("mzb", [128, 8 * TM], BF16)
        xdp = t_("xdp", [128, 8 * TM])
        mps = est.enter_context(nc.psum_tensor(f"mps_{li}", [128, 4096], F32))
        blk = est.enter_context(nc.Block())
        ms = MiniSched(nc, E.pool, "ma")

        for kd in range(8):
            sl = kd % 2
            ms.dma("sync", lambda e, kd=kd, sl=sl: e.dma_start(
                out=wstg[:, sl * 2048:(sl + 1) * 2048],
                in_=E.win[li][kd * 128:(kd + 1) * 128, :]),
                writes=(f"wstg{sl}",), tag="ldwin")
            ms.op("scalar", lambda e, kd=kd, sl=sl: e.activation(
                winb[:, kd * 2048:(kd + 1) * 2048],
                wstg[:, sl * 2048:(sl + 1) * 2048], AF.Identity),
                reads=(f"wstg{sl}",), writes=(f"winb{kd}",), tag="cwin")
        for kd in range(8):
            sl = kd % 2
            ms.dma("sync", lambda e, kd=kd, sl=sl: e.dma_start(
                out=bass.AP(wstg, sl * 2048, [[2 * 2048, 128], [1, 96]]),
                in_=E.wxp[li][kd * 128:(kd + 1) * 128, :]),
                writes=(f"wstg{sl}",), tag="ldwxp")
            ms.op("scalar", lambda e, kd=kd, sl=sl: e.activation(
                wxpb[:, kd * 96:(kd + 1) * 96],
                bass.AP(wstg, sl * 2048, [[2 * 2048, 128], [1, 96]]), AF.Identity),
                reads=(f"wstg{sl}",), writes=(f"wxpb{kd}",), tag="cwxp")
        ms.dma("sync", lambda e: e.dma_start(out=mcv[:, :], in_=bass.AP(
            E.wconv[li], 0, [[DC, 128], [128 * DC, 8], [1, DC]])),
            writes=("mcv",), tag="ldcv")
        ms.dma("sync", lambda e: e.dma_start(out=mbc[:, :], in_=bass.AP(
            E.bconv[li], 0, [[1, 128], [128, 8]]),
            allow_slow_non_contiguous=True), writes=("mbc",), tag="ldbc")

        for t in range(NC):
            p = t % 2
            if li == 0:
                xap = bass.AP(xsrc, t * TM, [[L, 128], [128 * L, 8], [1, TM]])
            else:
                xap = bass.AP(xsrc, t * DM * TM, [[TM, 128], [128 * TM, 8], [1, TM]])
            ms.dma("sync", lambda e, xap=xap, p=p: e.dma_start(
                out=xrh[:, p * 8 * TM:(p + 1) * 8 * TM], in_=xap),
                writes=(f"xrh{p}",), tag="ldxr")
            for m in range(16):
                bank = m % 8
                for kd in range(8):
                    ms.op("tensor", lambda e, m=m, kd=kd, bank=bank, p=p: e.matmul(
                        mps[:, bank * 512: bank * 512 + TM],
                        winb[:, kd * 2048 + m * 128: kd * 2048 + (m + 1) * 128],
                        xrh[:, p * 8 * TM + kd * TM: p * 8 * TM + (kd + 1) * TM],
                        start=(kd == 0), stop=(kd == 7)),
                        reads=(f"winb{kd}", f"xrh{p}"), writes=(f"pb{bank}",),
                        tag="mmin")
                if m < 8:
                    ms.op("scalar", lambda e, m=m, bank=bank: e.activation(
                        xcs[:, m * (TM + 3) + 3: (m + 1) * (TM + 3)],
                        mps[:, bank * 512: bank * 512 + TM], AF.Identity),
                        reads=(f"pb{bank}",), writes=(f"xcs{m}",), tag="evxc")
                else:
                    d = m - 8
                    ms.op("scalar", lambda e, d=d, bank=bank: e.activation(
                        mzb[:, d * TM:(d + 1) * TM],
                        mps[:, bank * 512: bank * 512 + TM], AF.Silu),
                        reads=(f"pb{bank}",), writes=(f"mzb{d}",), tag="evz")
            ms.dma("gpsimd", lambda e, t=t: e.dma_start(
                out=bass.AP(E.zt[li], t * TM, [[L, 128], [128 * L, 8], [1, TM]]),
                in_=mzb[:, :]),
                reads=tuple(f"mzb{d}" for d in range(8)), tag="stz")
            for d in range(8):
                if t == 0:
                    ms.op("vector", lambda e, d=d: e.memset(
                        xcs[:, d * (TM + 3): d * (TM + 3) + 3], 0.0),
                        writes=(f"xch{d}",), tag="cvz")
                ms.op("vector", lambda e, d=d: e.tensor_scalar_mul(
                    cac[:, :], xcs[:, d * (TM + 3): d * (TM + 3) + TM],
                    mcv[:, d * DC: d * DC + 1]),
                    reads=(f"xcs{d}", f"xch{d}", "mcv"), writes=("cac",), tag="cv0")
                for k in range(1, DC):
                    ms.op("vector", lambda e, d=d, k=k: e.scalar_tensor_tensor(
                        cac[:, :], xcs[:, d * (TM + 3) + k: d * (TM + 3) + k + TM],
                        mcv[:, d * DC + k: d * DC + k + 1], cac[:, :],
                        ALU.mult, ALU.add),
                        reads=(f"xcs{d}", f"xch{d}", "mcv", "cac"), writes=("cac",),
                        tag=f"cv{k}")
                ms.op("scalar", lambda e, d=d: e.activation(
                    mub[:, d * TM:(d + 1) * TM], cac[:, :], AF.Silu,
                    bias=mbc[:, d:d + 1]),
                    reads=("cac", "mbc"), writes=(f"mub{d}",), tag="evu")
                ms.op("vector", lambda e, d=d: e.tensor_copy(
                    xcs[:, d * (TM + 3): d * (TM + 3) + 3],
                    xcs[:, d * (TM + 3) + TM: d * (TM + 3) + TM + 3]),
                    reads=(f"xcs{d}",), writes=(f"xch{d}",), tag="halo")
            ms.dma("gpsimd", lambda e, t=t: e.dma_start(
                out=bass.AP(E.ut[li], t * TM, [[L, 128], [128 * L, 8], [1, TM]]),
                in_=mub[:, :]),
                reads=tuple(f"mub{d}" for d in range(8)), tag="stu")
            for kd in range(8):
                ms.op("tensor", lambda e, kd=kd: e.matmul(
                    mps[0:96, 0:TM],
                    wxpb[:, kd * 96:(kd + 1) * 96],
                    mub[:, kd * TM:(kd + 1) * TM],
                    start=(kd == 0), stop=(kd == 7)),
                    reads=(f"mub{kd}",), writes=("pb0",), tag="mmxp")
            ms.op("scalar", lambda e, t=t: e.activation(
                bass.AP(xdp, t * TM, [[8 * TM, 96], [1, TM]]),
                mps[0:96, 0:TM], AF.Identity),
                reads=("pb0",), writes=(f"xdp{t}",), tag="evxp")
            ms.dma("gpsimd", lambda e, t=t: e.dma_start(
                out=bass.AP(E.cin_xd[li], t * 96 * TM, [[TM, 96], [1, TM]]),
                in_=bass.AP(xdp, t * TM, [[8 * TM, 96], [1, TM]])),
                reads=(f"xdp{t}",), writes=(f"cinxd{t}",), tag="stxd")
            ms.cc("gpsimd", lambda e, t=t: e.collective_compute(
                "AllReduce", ALU.add, replica_groups=RG_PAIRS,
                ins=[bass.AP(E.cin_xd[li], t * 96 * TM, [[TM, 96], [1, TM]])],
                outs=[bass.AP(E.cout_xd[li], t * 96 * TM, [[TM, 96], [1, TM]])]),
                reads=(f"cinxd{t}",), writes=(f"xdred{t}",), tag="ccxd")
        ms.op("gpsimd", lambda e: e.nop(),
              reads=tuple(f"xdred{t}" for t in range(NC)), tag="ccwait")
        ms.run_block(blk)

    TS_ = 512
    NS = L // TS_
    with contextlib.ExitStack() as est:
        t_ = lambda name, shape, dt=F32: est.enter_context(
            nc.sbuf_tensor(f"{name}b{li}", shape, dt))
        dtb = t_("dtb", [64, L], BF16)
        bcb = t_("bcb", [32, L], BF16)
        wdtb = t_("wdtb", [64, 1024], BF16)
        woutb = t_("woutb", [128, 8 * 1024], BF16)
        wstg2 = t_("wstg2", [128, 1024])
        maneg = t_("maneg", [128, 8 * NST])
        mbdt = t_("mbdt", [128, 8])
        mdv = t_("mdv", [128, 8])
        idb = t_("idb", [128, 128], BF16)
        idf = t_("idf", [128, 128])
        mdel = t_("mdel", [128, 8 * TS_])
        mdu = t_("mdu", [128, 8 * TS_], BF16)
        muc = t_("muc", [128, 8 * TS_], BF16)
        mzc = t_("mzc", [128, 8 * TS_], BF16)
        mbt = t_("mbt", [128, 2 * NST * TS_], BF16)
        mct = t_("mct", [128, NST * TS_], BF16)
        mda = t_("mda", [128, 4 * TS_])
        mdbu = t_("mdbu", [128, 2 * NST * TS_], BF16)
        mh = t_("mh", [128, NST * TS_], BF16)
        mhst = t_("mhst", [128, 8 * NST])
        mtt = t_("mtt", [128, TS_])
        msp = t_("msp", [128, 4 * TS_])
        my2 = t_("my2", [128, 8 * TS_], BF16)
        ost = t_("ost", [128, 2 * TS_], BF16 if li == 0 else F32)
        mps2 = est.enter_context(nc.psum_tensor(f"mps2_{li}", [128, 4096], F32))
        blk = est.enter_context(nc.Block())
        ms = MiniSched(nc, E.pool, "mb")

        for q in range(NS):
            ms.dma("sync", lambda e, q=q: e.dma_start(
                out=wstg2[0:96, 0:TS_],
                in_=bass.AP(E.cout_xd[li], q * 96 * TS_, [[TS_, 96], [1, TS_]])),
                writes=("wstg0",), tag="ldxd")
            ms.op("scalar", lambda e, q=q: e.activation(
                dtb[:, q * TS_:(q + 1) * TS_], wstg2[0:64, 0:TS_], AF.Identity),
                reads=("wstg0",), writes=("dtb",), tag="cdt")
            ms.op("scalar", lambda e, q=q: e.activation(
                bcb[:, q * TS_:(q + 1) * TS_], wstg2[64:96, 0:TS_], AF.Identity),
                reads=("wstg0",), writes=("bcb",), tag="cbc")
        ms.dma("gpsimd", lambda e: e.dma_start(out=E.bct[li][:, :], in_=bcb[:, :]),
               reads=("bcb",), writes=("bct",), tag="stbc")
        ms.dma("sync", lambda e: e.dma_start(out=wstg2[0:64, 0:1024],
                                             in_=E.wdt[li][:, :]),
               writes=("wstg0",), tag="ldwdt")
        ms.op("scalar", lambda e: e.activation(wdtb[:, :], wstg2[0:64, 0:1024],
                                               AF.Identity),
              reads=("wstg0",), writes=("wdtb",), tag="cwdt")
        for kd in range(8):
            ms.dma("sync", lambda e, kd=kd: e.dma_start(
                out=wstg2[:, 0:1024],
                in_=E.wout[li][kd * 128:(kd + 1) * 128, :]),
                writes=("wstg0",), tag="ldwo")
            ms.op("scalar", lambda e, kd=kd: e.activation(
                woutb[:, kd * 1024:(kd + 1) * 1024],
                wstg2[:, 0:1024], AF.Identity),
                reads=("wstg0",), writes=(f"woutb{kd}",), tag="cwo")
        ms.dma("sync", lambda e: e.dma_start(
            out=maneg[:, :], in_=bass.AP(E.alog[li], 0,
                                         [[NST, 128], [128 * NST, 8], [1, NST]])),
            writes=("maneg",), tag="lda")
        ms.op("scalar", lambda e: e.activation(maneg[:, :], maneg[:, :], AF.Exp),
              reads=("maneg",), writes=("maneg",), tag="expa")
        ms.op("vector", lambda e: e.tensor_scalar_mul(maneg[:, :], maneg[:, :], -1.0),
              reads=("maneg",), writes=("maneg",), tag="nega")
        ms.dma("sync", lambda e: e.dma_start(
            out=mbdt[:, :], in_=bass.AP(E.bdt[li], 0, [[1, 128], [128, 8]]),
            allow_slow_non_contiguous=True), writes=("mbdt",), tag="ldbdt")
        ms.dma("sync", lambda e: e.dma_start(
            out=mdv[:, :], in_=bass.AP(E.dvec[li], 0, [[1, 128], [128, 8]]),
            allow_slow_non_contiguous=True), writes=("mdv",), tag="lddv")
        ms.dma("sync", lambda e: e.dma_start(out=idf[:, :], in_=E.ident_in[:, :]),
               writes=("idf",), tag="ldid")
        ms.op("scalar", lambda e: e.activation(idb[:, :], idf[:, :], AF.Identity),
              reads=("idf",), writes=("idb",), tag="cid")

        def sec_C(t):
            ms.dma("sync", lambda e, t=t: e.dma_start(
                out=muc[:, :],
                in_=bass.AP(E.ut[li], t * TS_, [[L, 128], [128 * L, 8], [1, TS_]])),
                writes=("muc",), tag="lduc")
            ms.dma("sync", lambda e, t=t: e.dma_start(
                out=mzc[:, :],
                in_=bass.AP(E.zt[li], t * TS_, [[L, 128], [128 * L, 8], [1, TS_]])),
                writes=("mzc",), tag="ldzc")
            sb = (t % 2) * NST * TS_
            for n in range(NST):
                ms.dma("sync", lambda e, n=n, t=t, sb=sb: e.dma_start(
                    out=mbt[:, sb + n * TS_: sb + (n + 1) * TS_],
                    in_=bass.AP(E.bct[li], n * L + t * TS_, [[0, 128], [1, TS_]])),
                    reads=("bct",), writes=(f"mbt{t % 2}_{n // 4}",), tag="bcb_")
                ms.dma("sync", lambda e, n=n, t=t: e.dma_start(
                    out=mct[:, n * TS_:(n + 1) * TS_],
                    in_=bass.AP(E.bct[li], (NST + n) * L + t * TS_,
                                [[0, 128], [1, TS_]])),
                    reads=("bct",), writes=(f"mct{n // 4}",), tag="bcc_")
            for d in range(8):
                sp = d % 2
                bk = d % 4
                ms.op("tensor", lambda e, d=d, t=t, bk=bk: e.matmul(
                    mps2[:, bk * 512: bk * 512 + TS_],
                    wdtb[:, d * 128:(d + 1) * 128],
                    dtb[:, t * TS_:(t + 1) * TS_],
                    start=True, stop=True),
                    reads=("wdtb", "dtb"), writes=(f"pb{bk}",), tag="mmdt")
                ms.op("scalar", lambda e, d=d, sp=sp, bk=bk: e.activation(
                    msp[:, sp * 2 * TS_: sp * 2 * TS_ + TS_],
                    mps2[:, bk * 512: bk * 512 + TS_], AF.Abs,
                    bias=mbdt[:, d:d + 1]),
                    reads=(f"pb{bk}", "mbdt"), writes=(f"mspa{sp}",), tag="spabs")
                ms.op("scalar", lambda e, d=d, sp=sp, bk=bk: e.activation(
                    msp[:, sp * 2 * TS_ + TS_: sp * 2 * TS_ + 2 * TS_],
                    mps2[:, bk * 512: bk * 512 + TS_], AF.Relu,
                    bias=mbdt[:, d:d + 1]),
                    reads=(f"pb{bk}", "mbdt"), writes=(f"mspr{sp}",), tag="sprelu")
                ms.op("scalar", lambda e, sp=sp: e.activation(
                    msp[:, sp * 2 * TS_: sp * 2 * TS_ + TS_],
                    msp[:, sp * 2 * TS_: sp * 2 * TS_ + TS_], AF.Exp, scale=-1.0),
                    reads=(f"mspa{sp}",), writes=(f"mspa{sp}",), tag="spexp")
                ms.op("scalar", lambda e, sp=sp: e.activation(
                    msp[:, sp * 2 * TS_: sp * 2 * TS_ + TS_],
                    msp[:, sp * 2 * TS_: sp * 2 * TS_ + TS_], AF.Ln, bias=1.0),
                    reads=(f"mspa{sp}",), writes=(f"mspa{sp}",), tag="spln")
                ms.op("vector", lambda e, d=d, sp=sp: e.tensor_add(
                    mdel[:, d * TS_:(d + 1) * TS_],
                    msp[:, sp * 2 * TS_: sp * 2 * TS_ + TS_],
                    msp[:, sp * 2 * TS_ + TS_: sp * 2 * TS_ + 2 * TS_]),
                    reads=(f"mspa{sp}", f"mspr{sp}"), writes=(f"mdel{d}",),
                    tag="spadd")
                ms.op("vector", lambda e, d=d: e.tensor_mul(
                    mdu[:, d * TS_:(d + 1) * TS_],
                    mdel[:, d * TS_:(d + 1) * TS_],
                    muc[:, d * TS_:(d + 1) * TS_]),
                    reads=(f"mdel{d}", "muc"), writes=(f"mdu{d}",), tag="du")
                if d == 0:
                    for n in range(4):
                        ms.op("scalar", lambda e, n=n: e.activation(
                            mda[:, n * TS_:(n + 1) * TS_],
                            mdel[:, 0:TS_], AF.Exp,
                            scale=maneg[:, n:n + 1]),
                            reads=("mdel0", "maneg"), writes=(f"mda{n}",),
                            tag="exppre")

        def tails(t, d):
            bk = d % 4
            ms.op("vector", lambda e, d=d, bk=bk: e.scalar_tensor_tensor(
                mtt[:, :], muc[:, d * TS_:(d + 1) * TS_],
                mdv[:, d:d + 1],
                mps2[:, bk * 512: bk * 512 + TS_],
                ALU.mult, ALU.add),
                reads=("muc", "mdv", f"pb{bk}"), writes=("mtt",), tag="tail1")
            ms.op("vector", lambda e, d=d: e.tensor_mul(
                my2[:, d * TS_:(d + 1) * TS_], mtt[:, :],
                mzc[:, d * TS_:(d + 1) * TS_]),
                reads=("mtt", "mzc"), writes=(f"my2{d}",), tag="tail2")

        def wave1(t, kd):
            for mo in range(4):
                ms.op("tensor", lambda e, mo=mo, kd=kd: e.matmul(
                    mps2[:, (4 + mo) * 512: (4 + mo) * 512 + TS_],
                    woutb[:, kd * 1024 + mo * 128: kd * 1024 + (mo + 1) * 128],
                    my2[:, kd * TS_:(kd + 1) * TS_],
                    start=(kd == 0), stop=(kd == 7)),
                    reads=(f"woutb{kd}", f"my2{kd}"), writes=(f"pb{4 + mo}",),
                    tag="mmo1")

        def sec_D(t):
            for d in range(8):
                sx = (d % 2) * NST * TS_
                kx = f"X{d % 2}"
                ms.op("vector", lambda e, d=d, sx=sx, t=t: e.tensor_mul(
                    mdbu[:, sx: sx + NST * TS_],
                    bass.AP(mdu, d * TS_, [[8 * TS_, 128], [0, NST], [1, TS_]]),
                    mbt[:, (t % 2) * NST * TS_: ((t % 2) + 1) * NST * TS_]),
                    reads=(f"mdu{d}",) + tuple(f"mbt{t % 2}_{g}" for g in range(4)),
                    writes=(kx,), tag="dbu")
                for n in range(NST):
                    pp = n % 4
                    if not (d == 0 and n < 4):
                        ms.op("scalar", lambda e, d=d, n=n, pp=pp: e.activation(
                            mda[:, pp * TS_:(pp + 1) * TS_],
                            mdel[:, d * TS_:(d + 1) * TS_], AF.Exp,
                            scale=maneg[:, d * NST + n: d * NST + n + 1]),
                            reads=(f"mdel{d}", "maneg"), writes=(f"mda{pp}",),
                            tag="expda")
                    if t == 0:
                        init = 0.0
                        rd_in = ()
                    else:
                        init = mhst[:, d * NST + n: d * NST + n + 1]
                        rd_in = (f"mhst{d}",)
                    ms.op("vector",
                          lambda e, n=n, pp=pp, sx=sx, init=init: e.tensor_tensor_scan(
                              mh[:, n * TS_:(n + 1) * TS_],
                              mda[:, pp * TS_:(pp + 1) * TS_],
                              mdbu[:, sx + n * TS_: sx + (n + 1) * TS_],
                              init, ALU.mult, ALU.add),
                          reads=(f"mda{pp}", kx) + rd_in,
                          writes=(f"mhn{n}",), tag="scan")
                ms.op("vector", lambda e, d=d: e.tensor_copy(
                    mhst[:, d * NST:(d + 1) * NST],
                    bass.AP(mh, TS_ - 1, [[NST * TS_, 128], [TS_, NST], [1, 1]])),
                    reads=tuple(f"mhn{n}" for n in range(NST)),
                    writes=(f"mhst{d}",), tag="hsave")
                ms.op("vector", lambda e, sx=sx: e.tensor_mul(
                    mdbu[:, sx: sx + NST * TS_], mh[:, :], mct[:, :]),
                    reads=tuple(f"mhn{n}" for n in range(NST))
                    + tuple(f"mct{g}" for g in range(4)),
                    writes=(kx,), tag="hc")
                for n in range(NST):
                    ms.op("tensor", lambda e, d=d, n=n, sx=sx: e.matmul(
                        mps2[:, (d % 4) * 512: (d % 4) * 512 + TS_],
                        idb[:, :],
                        mdbu[:, sx + n * TS_: sx + (n + 1) * TS_],
                        start=(n == 0), stop=(n == NST - 1)),
                        reads=(kx, "idb"), writes=(f"pb{d % 4}",), tag="mmy")
                if d >= 1:
                    tails(t, d - 1)
                    wave1(t, d - 1)
            tails(t, 7)
            wave1(t, 7)

        def sec_E2(t):
            for mo in range(4):
                po = mo % 2
                ms.op("scalar", lambda e, mo=mo, po=po: e.activation(
                    ost[:, po * TS_:(po + 1) * TS_],
                    mps2[:, (4 + mo) * 512: (4 + mo) * 512 + TS_], AF.Identity),
                    reads=(f"pb{4 + mo}",), writes=(f"ost{po}",), tag="evo")
                ms.dma("gpsimd", lambda e, t=t, mo=mo, po=po: e.dma_start(
                    out=bass.AP(cin, t * DM * TS_ + mo * 128 * TS_,
                                [[TS_, 128], [1, TS_]]),
                    in_=ost[:, po * TS_:(po + 1) * TS_]),
                    reads=(f"ost{po}",), writes=(f"cino{t}_{mo}",), tag="sto1")
            for mo in range(4, 8):
                po = mo % 2
                for kd in range(8):
                    ms.op("tensor", lambda e, mo=mo, kd=kd: e.matmul(
                        mps2[:, mo * 512: mo * 512 + TS_],
                        woutb[:, kd * 1024 + mo * 128: kd * 1024 + (mo + 1) * 128],
                        my2[:, kd * TS_:(kd + 1) * TS_],
                        start=(kd == 0), stop=(kd == 7)),
                        reads=(f"woutb{kd}", f"my2{kd}"), writes=(f"pb{mo}",),
                        tag="mmo2")
                ms.op("scalar", lambda e, mo=mo, po=po: e.activation(
                    ost[:, po * TS_:(po + 1) * TS_],
                    mps2[:, mo * 512: mo * 512 + TS_], AF.Identity),
                    reads=(f"pb{mo}",), writes=(f"ost{po}",), tag="evo")
                ms.dma("gpsimd", lambda e, t=t, mo=mo, po=po: e.dma_start(
                    out=bass.AP(cin, t * DM * TS_ + mo * 128 * TS_,
                                [[TS_, 128], [1, TS_]]),
                    in_=ost[:, po * TS_:(po + 1) * TS_]),
                    reads=(f"ost{po}",), writes=(f"cino{t}_{mo}",), tag="sto1")
            ms.cc("gpsimd", lambda e, t=t: e.collective_compute(
                "AllReduce", ALU.add, replica_groups=RG_PAIRS,
                ins=[bass.AP(cin, t * DM * TS_, [[TS_, DM], [1, TS_]])],
                outs=[bass.AP(cout, t * DM * TS_, [[TS_, DM], [1, TS_]])]),
                reads=tuple(f"cino{t}_{mo}" for mo in range(8)),
                writes=(f"ored{t}",), tag="cco")

        sec_C(0)
        for t in range(NS):
            sec_D(t)
            if t + 1 < NS:
                sec_C(t + 1)
            sec_E2(t)
        ms.op("gpsimd", lambda e: e.nop(),
              reads=tuple(f"ored{t}" for t in range(NS)), tag="ccwait")
        ms.run_block(blk)


def phase_ffn(nc, E, st):
    pass
    with (
        nc.sbuf_tensor("we", [128, 8 * 32 * 128], BF16) as we,
        nc.sbuf_tensor("wc", [128, 32 * 8 * 128], BF16) as wc,
        nc.sbuf_tensor("fxn", [128, 2 * 8 * TF], BF16) as fxn,
        nc.sbuf_tensor("fh2", [128, 32 * TF], BF16) as fh2,
        nc.sbuf_tensor("fhb2", [128, 32], F32) as fhb2,
        nc.sbuf_tensor("fbs", [128, 8], F32) as fbs,
        nc.sbuf_tensor("fbc", [128, 8], F32) as fbc,
        nc.sbuf_tensor("fbd", [128, 8], F32) as fbd,
        nc.sbuf_tensor("fo", [128, 2 * TF], F32) as fo,
        nc.psum_tensor("fps2", [128, 4096], F32) as fps2,
        nc.Block() as blk,
    ):
        ms = MiniSched(nc, E.pool, "ff")

        def mk_hreg(e):
            if not hasattr(E, "hreg"):
                E.hreg = e.alloc_register("hoff")
                pid = e.partition_id()
                e.reg_alu(E.hreg, pid, 1, ALU.bitwise_and)
                e.reg_mul(E.hreg, E.hreg, LHALF)
            return e.nop()
        ms.op("sync", mk_hreg, tag="hreg")
        for kd in range(8):
            ms.dma("sync", lambda e, kd=kd: e.dma_start(
                out=we[:, kd * 32 * 128:(kd + 1) * 32 * 128],
                in_=E.wexp_p[kd * 128:(kd + 1) * 128, :]),
                writes=(f"we{kd}",), tag="ldwe")
        for kh in range(32):
            ms.dma("sync", lambda e, kh=kh: e.dma_start(
                out=wc[:, kh * 8 * 128:(kh + 1) * 8 * 128],
                in_=E.wcon_p[kh * 128:(kh + 1) * 128, :]),
                writes=(f"wc{kh}",), tag="ldwc")
        ms.dma("sync", lambda e: e.dma_start(
            out=fhb2[:, :], in_=bass.AP(E.hbias, 0, [[1, 128], [128, 32]]),
            allow_slow_non_contiguous=True), writes=("fhb2",))
        ms.dma("sync", lambda e: e.dma_start(
            out=fbc[:, :], in_=bass.AP(E.bcon, 0, [[1, 128], [128, 8]]),
            allow_slow_non_contiguous=True), writes=("fbc",))
        ms.dma("sync", lambda e: e.dma_start(
            out=fbd[:, :], in_=bass.AP(E.dbias, 0, [[1, 128], [128, 8]]),
            allow_slow_non_contiguous=True), writes=("fbd",))
        ms.op("vector", lambda e: e.tensor_add(fbs[:, :], fbc[:, :], fbd[:, :]),
              reads=("fbc", "fbd"), writes=("fbs",))

        for fc in range(NFC):
            p = fc % 2

            def ldxn(e, p=p, fc=fc):
                with e.register() as r:
                    e.reg_add(r, E.hreg, fc * TF)
                    return e.dma_start(
                        out=fxn[:, p * 8 * TF:(p + 1) * 8 * TF],
                        in_=bass.AP(E.xn_fm, r, [[L, 128], [128 * L, 8], [1, TF]]))
            ms.dma("sync", ldxn, writes=(f"fxn{p}",), tag="ldxn")
            for mh in range(32):
                bank = mh % 4
                for kd in range(8):
                    ms.op("tensor", lambda e, mh=mh, kd=kd, bank=bank, p=p: e.matmul(
                        fps2[:, bank * 512: bank * 512 + TF],
                        we[:, (kd * 32 + mh) * 128:(kd * 32 + mh + 1) * 128],
                        fxn[:, p * 8 * TF + kd * TF: p * 8 * TF + (kd + 1) * TF],
                        start=(kd == 0), stop=(kd == 7)),
                        reads=(f"we{kd}", f"fxn{p}"), writes=(f"pb{bank}",),
                        tag="mmh")
                ms.op("scalar", lambda e, mh=mh, bank=bank: e.activation(
                    fh2[:, mh * TF:(mh + 1) * TF],
                    fps2[:, bank * 512: bank * 512 + TF],
                    AF.Gelu, bias=fhb2[:, mh:mh + 1]),
                    reads=(f"pb{bank}", "fhb2"), writes=(f"fh2_{mh}",), tag="gelu")
            for mo in range(8):
                bank = 4 + mo % 4
                fsl = mo % 2
                for kh in range(32):
                    ms.op("tensor", lambda e, mo=mo, kh=kh, bank=bank: e.matmul(
                        fps2[:, bank * 512: bank * 512 + TF],
                        wc[:, (kh * 8 + mo) * 128:(kh * 8 + mo + 1) * 128],
                        fh2[:, kh * TF:(kh + 1) * TF],
                        start=(kh == 0), stop=(kh == 31)),
                        reads=(f"wc{kh}", f"fh2_{kh}"), writes=(f"pb{bank}",),
                        tag="mmc")
                ms.op("scalar", lambda e, mo=mo, bank=bank, fsl=fsl: e.activation(
                    fo[:, fsl * TF:(fsl + 1) * TF],
                    fps2[:, bank * 512: bank * 512 + TF],
                    AF.Identity, bias=fbs[:, mo:mo + 1]),
                    reads=(f"pb{bank}", "fbs"), writes=(f"fo{fsl}",), tag="fev")
                ms.dma("gpsimd", lambda e, fc=fc, mo=mo, fsl=fsl: e.dma_start(
                    out=bass.AP(E.ffn_o, mo * 128 * LHALF + fc * TF,
                                [[LHALF, 128], [1, TF]]),
                    in_=fo[:, fsl * TF:(fsl + 1) * TF]),
                    reads=(f"fo{fsl}",), writes=(f"ffo{fc}_{mo}",), tag="stff")
        ms.run_block(blk)


def phase_final(nc, E, st, with_attn):
    pass
    with (
        nc.sbuf_tensor("gfo", [128, 8 * TF], F32) as gfo,
        nc.sbuf_tensor("gat", [128, 8 * TF], F32) as gat,
        nc.sbuf_tensor("gx", [128, 2 * DM], F32) as gx,
        nc.sbuf_tensor("gxn", [128, 2 * DM], F32) as gxn,
        nc.sbuf_tensor("gtm", [128, 2 * DM], F32) as gtm,
        nc.sbuf_tensor("gid", [128, 128], F32) as gid,
        nc.psum_tensor("gps", [128, 2 * DM], F32) as gps,
        nc.Block() as blk,
    ):
        ms = MiniSched(nc, E.pool, "fn")
        ms.dma("sync", lambda e: e.dma_start(out=gid[:, :], in_=E.ident_in[:, :]),
               writes=("gid",))
        for fc in range(NFC):
            cT = fc * TF
            ms.dma("sync", lambda e, cT=cT: e.dma_start(
                out=gfo[:, :],
                in_=bass.AP(E.ffn_o, cT, [[LHALF, 128], [128 * LHALF, 8], [1, TF]])),
                writes=("gfo",), tag="ldfo")
            if with_attn:
                def ldat(e, fc=fc):
                    with e.register() as r:
                        e.reg_mul(r, E.hreg, DM)
                        e.reg_add(r, r, fc * DM * TF)
                        return e.dma_start(
                            out=gat[:, :],
                            in_=bass.AP(E.cout_o2, r,
                                        [[TF, 128], [128 * TF, 8], [1, TF]]))
                ms.dma("sync", ldat, writes=("gat",), tag="ldat")
                ms.op("vector", lambda e: e.tensor_add(gfo[:, :], gfo[:, :], gat[:, :]),
                      reads=("gfo", "gat"), writes=("gfo",), tag="addat")
            for v in range(4):
                pv = v % 2
                for j in range(8):
                    ms.op("tensor", lambda e, j=j, v=v, pv=pv: e.transpose(
                        gps[:, pv * DM + j * 128: pv * DM + (j + 1) * 128],
                        gfo[:, j * TF + v * 128: j * TF + (v + 1) * 128],
                        gid[:, :]),
                        reads=("gfo", "gid"), writes=(f"gpb{pv * 2 + j // 4}",),
                        tag="ftr")
                r0c = fc * TF + v * 128

                def ldx(e, r0c=r0c, pv=pv):
                    with e.register() as r:
                        e.reg_add(r, E.hreg, r0c)
                        e.reg_mul(r, r, DM)
                        return e.dma_start(
                            out=gx[:, pv * DM:(pv + 1) * DM],
                            in_=bass.AP(E.x_in, r, [[DM, 128], [1, DM]]))
                ms.dma("sync", ldx, writes=(f"gx{pv}",), tag="ldx")

                def ldxn2(e, r0c=r0c, pv=pv):
                    with e.register() as r:
                        e.reg_add(r, E.hreg, r0c)
                        e.reg_mul(r, r, DM)
                        return e.dma_start(
                            out=gxn[:, pv * DM:(pv + 1) * DM],
                            in_=bass.AP(E.xn_tm, r, [[DM, 128], [1, DM]]))
                ms.dma("sync", ldxn2, writes=(f"gxn{pv}",), tag="ldxn2")
                ms.op("vector", lambda e, pv=pv: e.tensor_add(
                    gtm[:, pv * DM:(pv + 1) * DM],
                    gps[:, pv * DM:(pv + 1) * DM],
                    gx[:, pv * DM:(pv + 1) * DM]),
                    reads=(f"gpb{pv * 2}", f"gpb{pv * 2 + 1}", f"gx{pv}"),
                    writes=(f"gtm{pv}",), tag="add1")
                if with_attn:
                    ms.op("vector", lambda e, pv=pv: e.tensor_add(
                        gtm[:, pv * DM:(pv + 1) * DM],
                        gtm[:, pv * DM:(pv + 1) * DM],
                        gxn[:, pv * DM:(pv + 1) * DM]),
                        reads=(f"gtm{pv}", f"gxn{pv}"), writes=(f"gtm{pv}",),
                        tag="add2")
                ms.dma("gpsimd", lambda e, fc=fc, v=v, pv=pv: e.dma_start(
                    out=E.y_out[fc * TF + v * 128: fc * TF + (v + 1) * 128, :],
                    in_=gtm[:, pv * DM:(pv + 1) * DM]),
                    reads=(f"gtm{pv}",), tag="sty")
        ms.run_block(blk)


def build_program(dump=(), phases=("ln",)):
    nc = bacc.Bacc(target_bir_lowering=False, trn_type="TRN2")
    dump = set(dump)
    E = Env()

    E.x_in = nc.dram_tensor("x", [L, DM], F32, kind="ExternalInput")
    E.y_out = nc.dram_tensor("y", [LHALF, DM], F32, kind="ExternalOutput")
    E.xn_fm = nc.dram_tensor("xn_fm", [DM, L], BF16, kind="Internal")
    E.xn_tm = nc.dram_tensor("xn_tm", [L, DM], F32, kind="Internal")
    E.wexpc = nc.dram_tensor("wexpc", [2, 8, 128, 4 * 512], F32,
                             kind="ExternalInput")
    E.wexpcb = nc.dram_tensor("wexpcb", [8, 128, 4 * 2], F32,
                              kind="ExternalInput")
    E.upb = nc.dram_tensor("upb", [HID, 1], F32, kind="ExternalInput")
    E.wcon = nc.dram_tensor("wcon", [2, 8, 128, 4 * 512], F32,
                            kind="ExternalInput")
    E.bcon = nc.dram_tensor("bcon", [DM, 1], F32, kind="ExternalInput")
    E.dbias = nc.dram_tensor("dbias", [DM, 1], F32, kind="ExternalInput")
    E.bfu_within = nc.dram_tensor("bfu_within", [7, 4, 128, 8 * 128], F32,
                                  kind="ExternalInput")
    E.bfd_within = nc.dram_tensor("bfd_within", [7, 4, 128, 8 * 128], F32,
                                  kind="ExternalInput")
    E.bfu_cross = nc.dram_tensor("bfu_cross", [5, 8, 128, 8 * 128], F32,
                                 kind="ExternalInput")
    E.bfd_cross = nc.dram_tensor("bfd_cross", [5, 8, 128, 8 * 128], F32,
                                 kind="ExternalInput")
    E.bfu_xv = nc.dram_tensor("bfu_xv", [128, 10 * 32], F32, kind="ExternalInput")
    E.ident_in = nc.dram_tensor("ident", [128, 128], F32, kind="ExternalInput")
    E.wexp_p = nc.dram_tensor("wexp_p", [DM, HID], BF16, kind="Internal")
    E.hbias = nc.dram_tensor("hbias", [HID, 1], F32, kind="Internal")
    E.wcon_p = nc.dram_tensor("wcon_p", [HID, DM], BF16, kind="Internal")
    E.ffn_o = nc.dram_tensor("ffn_o", [DM, LHALF], F32, kind="Internal")
    E.cin_o2 = nc.dram_tensor("cino2", [DM, L], F32, kind="Internal")
    E.cout_o2 = nc.dram_tensor("couto2", [DM, L], F32, kind="Internal")
    E.cin_o1 = nc.dram_tensor("cino1", [DM, L], BF16, kind="Internal")
    E.cout_o1 = nc.dram_tensor("couto1", [DM, L], BF16, kind="Internal")
    E.win, E.wconv, E.bconv, E.wxp, E.wdt, E.bdt = [], [], [], [], [], []
    E.alog, E.dvec, E.wout, E.zt, E.ut, E.bct = [], [], [], [], [], []
    E.cin_xd, E.cout_xd = [], []
    for i in range(2):
        E.win.append(nc.dram_tensor(f"win{i}", [DM, 2 * DH], F32, kind="ExternalInput"))
        E.wconv.append(nc.dram_tensor(f"wconv{i}", [DH, DC], F32, kind="ExternalInput"))
        E.bconv.append(nc.dram_tensor(f"bconv{i}", [DH, 1], F32, kind="ExternalInput"))
        E.wxp.append(nc.dram_tensor(f"wxp{i}", [DH, 96], F32, kind="ExternalInput"))
        E.wdt.append(nc.dram_tensor(f"wdt{i}", [RNK, DH], F32, kind="ExternalInput"))
        E.bdt.append(nc.dram_tensor(f"bdt{i}", [DH, 1], F32, kind="ExternalInput"))
        E.alog.append(nc.dram_tensor(f"alog{i}", [DH, NST], F32, kind="ExternalInput"))
        E.dvec.append(nc.dram_tensor(f"dvec{i}", [DH, 1], F32, kind="ExternalInput"))
        E.wout.append(nc.dram_tensor(f"wout{i}", [DH, DM], F32, kind="ExternalInput"))
        E.zt.append(nc.dram_tensor(f"zt{i}", [DH, L], BF16, kind="Internal"))
        E.ut.append(nc.dram_tensor(f"ut{i}", [DH, L], BF16, kind="Internal"))
        E.bct.append(nc.dram_tensor(f"bct{i}", [2 * NST, L], BF16, kind="Internal"))
        E.cin_xd.append(nc.dram_tensor(f"cinxd{i}", [96, L], F32, kind="Internal"))
        E.cout_xd.append(nc.dram_tensor(f"coutxd{i}", [96, L], F32, kind="Internal"))

    E.dump_t = {}

    def mkdump(name, shape, dtype=F32):
        if name in dump:
            E.dump_t[name] = nc.dram_tensor("dbg_" + name, shape, dtype,
                                            kind="ExternalOutput")

    mkdump("xnfm", [DM, L], BF16)
    mkdump("xntm", [L, DM], F32)
    mkdump("wexpp", [DM, HID], BF16)
    mkdump("fRu", [128, 32 * 128], F32)
    mkdump("fRs", [128, 32 * 128], F32)
    mkdump("vW", [128, 32 * 512], F32)
    mkdump("wconp", [HID, DM], BF16)
    mkdump("hbias", [HID, 1], F32)
    mkdump("ffn", [DM, LHALF], F32)
    mkdump("ut0", [DH, L], BF16)
    mkdump("zt0", [DH, L], BF16)
    mkdump("xd0", [96, L], F32)
    mkdump("o1", [DM, L], BF16)
    mkdump("o2", [DM, L], F32)

    st = contextlib.ExitStack()
    with st:
        E.pool = SemPool(nc, st)
        if "ln" in phases or "fold" in phases:
            phase_lnfold(nc, E, st, do_ln="ln" in phases,
                         do_fold="fold" in phases)
        if "mamba" in phases:
            phase_mamba(nc, E, st, 0)
            phase_mamba(nc, E, st, 1)
        if "ffn" in phases:
            phase_ffn(nc, E, st)
        if "final" in phases:
            phase_final(nc, E, st, with_attn="mamba" in phases)

        if E.dump_t:
            with nc.Block() as blk:
                ms = MiniSched(nc, E.pool, "du")
                for nm, src in (("xnfm", E.xn_fm), ("xntm", E.xn_tm),
                                ("wexpp", E.wexp_p), ("wconp", E.wcon_p),
                                ("hbias", E.hbias), ("ffn", E.ffn_o),
                                ("ut0", E.ut[0]), ("zt0", E.zt[0]),
                                ("xd0", E.cout_xd[0]), ("o1", E.cout_o1),
                                ("o2", E.cout_o2)):
                    if nm in E.dump_t:
                        ms.dma("gpsimd", lambda e, nm=nm, src=src: e.dma_start(
                            out=E.dump_t[nm][:, :], in_=src[:, :]))
                ms.run_block(blk)
        if "final" not in phases:
            with (
                nc.sbuf_tensor("fin", [128, DM], F32) as fin,
                nc.Block() as blk,
            ):
                ms = MiniSched(nc, E.pool, "fi")
                for c in range(LHALF // 128):
                    ms.dma("gpsimd", lambda e, c=c: e.dma_start(
                        out=fin[:, :], in_=E.x_in[c * 128:(c + 1) * 128, :]),
                        writes=("fin",))
                    ms.dma("gpsimd", lambda e, c=c: e.dma_start(
                        out=E.y_out[c * 128:(c + 1) * 128, :], in_=fin[:, :]),
                        reads=("fin",))
                ms.run_block(blk)

    nc.has_collectives = True
    nc.finalize()
    return nc, E


def _bf_host_tiles(weights):
    out = {"n": [], "t": [], "xu": [], "xd": [], "xv": []}
    for st_ in range(7):
        stride = 1 << st_
        dn = np.zeros((32, 128, 128), np.float32)
        w = weights[st_].reshape(HID // (2 * stride), stride, 2, 2)
        for fa in range(HID):
            if fa & stride:
                continue
            fb = fa + stride
            q = fa // (2 * stride)
            t_ = fa % stride
            j = fa // 128
            ia, ib = fa % 128, fb % 128
            dn[j, ia, ia] = w[q, t_, 0, 0]
            dn[j, ia, ib] = w[q, t_, 0, 1]
            dn[j, ib, ia] = w[q, t_, 1, 0]
            dn[j, ib, ib] = w[q, t_, 1, 1]
        out["n"].append(dn)
        out["t"].append(np.ascontiguousarray(dn.transpose(0, 2, 1)))
    for st_ in range(7, NSTAGE):
        stride = 1 << st_
        xu = np.zeros((32, 2, 128, 128), np.float32)
        xd = np.zeros((32, 2, 128, 128), np.float32)
        xv = np.zeros((128, 32, 2), np.float32)
        w = weights[st_].reshape(HID // (2 * stride), stride, 2, 2)
        for fo in range(HID):
            jo = fo // 128
            ii = fo % 128
            fa = fo & ~stride
            q = fa // (2 * stride)
            t_ = fa % stride
            if fo == fa:
                cu = (w[q, t_, 0, 0], w[q, t_, 0, 1])
                cd = (w[q, t_, 0, 0], w[q, t_, 1, 0])
            else:
                cu = (w[q, t_, 1, 0], w[q, t_, 1, 1])
                cd = (w[q, t_, 0, 1], w[q, t_, 1, 1])
            xu[jo, 0, ii, ii], xu[jo, 1, ii, ii] = cu
            xd[jo, 0, ii, ii], xd[jo, 1, ii, ii] = cd
            xv[ii, jo, 0], xv[ii, jo, 1] = cu
        out["xu"].append(xu)
        out["xd"].append(xd)
        out["xv"].append(xv)
    return out


def _host_inputs(core, args):
    b, h = core // 2, core % 2
    hs = slice(h * DH, (h + 1) * DH)
    d = {}
    d["x"] = np.ascontiguousarray(args["L0_attn_x"][b])
    for i in range(2):
        ipw = np.asarray(args["L0_attn_in_proj_w"][i], np.float32)
        wxc = ipw[h * DH:(h + 1) * DH, :]
        wz = ipw[2 * DH + h * DH: 2 * DH + (h + 1) * DH, :]
        d[f"win{i}"] = np.ascontiguousarray(np.concatenate([wxc, wz], 0).T)
        d[f"wconv{i}"] = np.ascontiguousarray(np.asarray(args["L0_attn_conv_w"][i], np.float32)[hs, 0, :])
        d[f"bconv{i}"] = np.ascontiguousarray(np.asarray(args["L0_attn_conv_b"][i], np.float32)[hs, None])
        d[f"wxp{i}"] = np.ascontiguousarray(np.asarray(args["L0_attn_x_proj_w"][i], np.float32)[:, hs].T)
        d[f"wdt{i}"] = np.ascontiguousarray(np.asarray(args["L0_attn_dt_proj_w"][i], np.float32)[hs, :].T)
        d[f"bdt{i}"] = np.ascontiguousarray(np.asarray(args["L0_attn_dt_proj_b"][i], np.float32)[hs, None])
        d[f"alog{i}"] = np.ascontiguousarray(np.asarray(args["L0_attn_A_log"][i], np.float32)[hs, :])
        d[f"dvec{i}"] = np.ascontiguousarray(np.asarray(args["L0_attn_D"][i], np.float32)[hs, None])
        d[f"wout{i}"] = np.ascontiguousarray(np.asarray(args["L0_attn_out_proj_w"][i], np.float32)[:, hs].T)
    def grp_tiles(x):
        s = x.reshape(x.shape[0], -1, 8, 128, 128)
        return np.ascontiguousarray(s.transpose(0, 1, 3, 2, 4).reshape(
            x.shape[0], s.shape[1], 128, 8 * 128))

    def grp_v(v):
        s = v.reshape(8, 4, 128, 2, 512)
        return np.ascontiguousarray(s.transpose(3, 0, 2, 1, 4).reshape(
            2, 8, 128, 4 * 512))

    wexp_t = args["L0_ffn_W_exp"].T.astype(np.float32)
    d["wexpc"] = grp_v(wexp_t)
    bcol = np.concatenate([args["L0_ffn_b_exp"][:, None].astype(np.float32),
                           np.zeros((HID, 1), np.float32)], 1)
    sb = bcol.reshape(8, 4, 128, 2)
    d["wexpcb"] = np.ascontiguousarray(sb.transpose(0, 2, 1, 3).reshape(
        8, 128, 4 * 2))
    d["upb"] = np.ascontiguousarray(args["L0_ffn_up_bias"][:, None])
    d["wcon"] = grp_v(np.asarray(args["L0_ffn_W_con"], np.float32))
    d["bcon"] = np.ascontiguousarray(args["L0_ffn_b_con"][:, None])
    d["dbias"] = np.ascontiguousarray(args["L0_ffn_down_bias"][:, None])
    tu = _bf_host_tiles(np.asarray(args["L0_ffn_up_weights"], np.float32))
    td = _bf_host_tiles(np.asarray(args["L0_ffn_down_weights"], np.float32))
    d["bfu_within"] = grp_tiles(
        np.stack([tu["n"][ss] for ss in range(6)] + [tu["t"][6]]))
    d["bfd_within"] = grp_tiles(
        np.stack([td["n"][0]] + [td["t"][ss] for ss in range(1, 7)]))
    d["bfu_cross"] = grp_tiles(np.stack(tu["xu"]))
    d["bfd_cross"] = grp_tiles(np.stack(td["xd"]))
    d["bfu_xv"] = np.ascontiguousarray(
        np.stack(tu["xv"], 1).reshape(128, 5 * 64))
    d["ident"] = np.eye(128, dtype=np.float32)
    return d


def kernel(**inputs):
    nc, _ = build_program(phases=("ln", "fold", "mamba", "ffn", "final"))
    in_maps = [_host_inputs(c, inputs) for c in range(8)]
    res = run_bass_kernel_spmd(nc, in_maps, core_ids=list(range(8)))
    out = np.empty((4, L, DM), np.float32)
    for c in range(8):
        b, h = c // 2, c % 2
        out[b, h * LHALF:(h + 1) * LHALF, :] = res.results[c]["y"]
    return out


# version 09 (final)
# speedup vs baseline: 1.0394x; 1.0334x over previous
import contextlib
import numpy as np
import concourse.bass as bass
import concourse.bacc as bacc
import concourse.mybir as mybir
from concourse.bass_utils import run_bass_kernel_spmd

F32 = mybir.dt.float32
F32R = mybir.dt.float32r
BF16 = mybir.dt.bfloat16
AF = mybir.ActivationFunctionType
ALU = mybir.AluOpType
AXX = mybir.AxisListType.X

L = 4096
DM = 1024
DH = 1024
NDT = 8
NST = 16
RNK = 64
DC = 4
HID = 4096
NHT = 32
NSTAGE = 12
TS = 1024
NSC = L // TS
TMM = 512
NCH = L // TMM
LHALF = L // 2
TF = 512
NFC = LHALF // TF
EPS = 1e-5
NTOK = L // 128

RG_PAIRS = [[0, 1], [2, 3], [4, 5], [6, 7]]

COMPUTE_ENGINES = ("tensor", "vector", "scalar", "gpsimd", "sync")
DMA_RING = 12


class _Op:
    __slots__ = ("idx", "eng", "fn", "reads", "writes", "is_dma", "is_async",
                 "deps", "need_ms", "ms_val", "ring_slot", "pre_throttle",
                 "tag")

    def __init__(self, idx, eng, fn, reads, writes, is_dma, tag):
        self.is_async = False
        self.idx = idx
        self.eng = eng
        self.fn = fn
        self.reads = reads
        self.writes = writes
        self.is_dma = is_dma
        self.deps = []
        self.need_ms = False
        self.ms_val = None
        self.ring_slot = None
        self.pre_throttle = None
        self.tag = tag


class SemPool:

    def __init__(self, nc, stack, rings=("sync", "gpsimd", "scalar", "vector")):
        self.csem = {}
        self.cctr = {}
        for e in COMPUTE_ENGINES:
            self.csem[e] = stack.enter_context(nc.semaphore(f"ms_{e}"))
            self.cctr[e] = 0
        self.rings = {}
        for e in rings:
            self.rings[e] = {
                "sems": [stack.enter_context(nc.semaphore(f"ms_{e[:2]}d{i}"))
                         for i in range(DMA_RING)],
                "uses": [0] * DMA_RING,
                "next": 0,
            }


class MiniSched:
    def __init__(self, nc, pool, name):
        self.nc = nc
        self.name = name
        self.ops = []
        self.last_writer = {}
        self.readers = {}
        self.csem = pool.csem
        self.cctr = pool.cctr
        self.rings = pool.rings
        self.planned = False

    def op(self, eng, fn, reads=(), writes=(), tag=""):
        o = _Op(len(self.ops), eng, fn, tuple(reads), tuple(writes), False, tag)
        self._link(o)
        self.ops.append(o)
        return o

    def dma(self, eng, fn, reads=(), writes=(), tag=""):
        o = _Op(len(self.ops), eng, fn, tuple(reads), tuple(writes), True, tag)
        self._link(o)
        self.ops.append(o)
        return o

    def cc(self, eng, fn, reads=(), writes=(), tag=""):
        o = _Op(len(self.ops), eng, fn, tuple(reads), tuple(writes), False, tag)
        o.is_async = True
        o.need_ms = True
        self._link(o)
        self.ops.append(o)
        return o

    def _link(self, o):
        deps = {}
        for r in o.reads:
            w = self.last_writer.get(r)
            if w is not None:
                deps[w.idx] = w
        for w_ in o.writes:
            w = self.last_writer.get(w_)
            if w is not None:
                deps[w.idx] = w
            for rd in self.readers.get(w_, ()):
                deps[rd.idx] = rd
        o.deps = [deps[i] for i in sorted(deps) if i != o.idx]
        for w_ in o.writes:
            self.last_writer[w_] = o
            self.readers[w_] = []
        for r in o.reads:
            self.readers.setdefault(r, []).append(o)

    def plan(self):
        assert not self.planned
        for o in self.ops:
            if o.is_dma:
                ring = self.rings[o.eng]
                s = ring["next"]
                if ring["uses"][s] > 0:
                    o.pre_throttle = (ring["sems"][s], 16 * ring["uses"][s])
                ring["uses"][s] += 1
                ring["next"] = (s + 1) % DMA_RING
                o.ring_slot = s
                o.ms_val = 16 * ring["uses"][s]
            for d in o.deps:
                if (not d.is_dma) and d.eng != o.eng:
                    d.need_ms = True
        for o in self.ops:
            if (not o.is_dma) and o.need_ms:
                self.cctr[o.eng] += 1
                o.ms_val = self.cctr[o.eng]
        self.planned = True

    def emit(self, eng_name, eng):
        assert self.planned
        for o in self.ops:
            if o.eng != eng_name:
                continue
            need_drain = False
            for d in o.deps:
                if d.is_dma:
                    ring = self.rings[d.eng]
                    eng.wait_ge(ring["sems"][d.ring_slot], d.ms_val)
                elif d.eng == o.eng and not d.is_async:
                    need_drain = True
                else:
                    assert d.ms_val is not None, (o.tag, d.tag)
                    eng.wait_ge(self.csem[d.eng], d.ms_val)
            if need_drain:
                eng.drain()
            if o.pre_throttle is not None:
                eng.wait_ge(*o.pre_throttle)
            instr = o.fn(eng)
            if o.is_dma:
                instr.then_inc(self.rings[o.eng]["sems"][o.ring_slot], 16)
            elif o.need_ms:
                instr.then_inc(self.csem[o.eng], 1)

    def quiesce_ops(self):
        for e, ring in self.rings.items():
            def fn(eng, _ring=ring):
                for s in range(DMA_RING):
                    if _ring["uses"][s] > 0:
                        eng.wait_ge(_ring["sems"][s], 16 * _ring["uses"][s])
                return eng.nop()
            self.op(e, fn, tag="quiesce")

    def run_block(self, blk):
        self.quiesce_ops()
        self.plan()

        @blk.sync
        def _(e):
            self.emit("sync", e)

        @blk.scalar
        def _(e):
            self.emit("scalar", e)

        @blk.vector
        def _(e):
            self.emit("vector", e)

        @blk.tensor
        def _(e):
            self.emit("tensor", e)

        @blk.gpsimd
        def _(e):
            self.emit("gpsimd", e)


class Env:
    pass


def gen_ln(nc, E, ms, B):
    xln, lsq, ls1, ls2, lsv, lmu, lnmu, lvar, lrst, lt0, xnf, xnb, xnbt = (
        B.xln, B.lsq, B.ls1, B.ls2, B.lsv, B.lmu, B.lnmu, B.lvar, B.lrst,
        B.lt0, B.xnf, B.xnb, B.xnbt)
    if True:
        for tt in range(NTOK):
            p = tt % 2
            ms.dma("sync", lambda e, tt=tt, p=p: e.dma_start(
                out=xln[:, p * DM:(p + 1) * DM],
                in_=E.x_in[tt * 128:(tt + 1) * 128, :]),
                writes=(f"xln{p}",), tag=f"ldx{tt}")
            ms.op("scalar", lambda e, p=p: e.activation(
                lsq[:, :], xln[:, p * DM:(p + 1) * DM], AF.Square,
                accum_out=ls2[:, p:p + 1]),
                reads=(f"xln{p}",), writes=("lsq", f"ls2{p}"), tag=f"sq{tt}")
            ms.op("vector", lambda e, p=p: e.reduce_sum(
                ls1[:, p:p + 1], xln[:, p * DM:(p + 1) * DM], axis=AXX),
                reads=(f"xln{p}",), writes=(f"ls1{p}",))
            ms.op("vector", lambda e, p=p: e.tensor_scalar_mul(
                lmu[:, p:p + 1], ls1[:, p:p + 1], 1.0 / DM),
                reads=(f"ls1{p}",), writes=(f"lmu{p}",))
            ms.op("vector", lambda e, p=p: e.tensor_scalar_mul(
                lnmu[:, p:p + 1], ls1[:, p:p + 1], -1.0 / DM),
                reads=(f"ls1{p}",), writes=(f"lnmu{p}",))
            ms.op("vector", lambda e, p=p: e.tensor_mul(
                lvar[:, p:p + 1], lmu[:, p:p + 1], lmu[:, p:p + 1]),
                reads=(f"lmu{p}",), writes=(f"lvar{p}",))
            ms.op("vector", lambda e, p=p: e.scalar_tensor_tensor(
                lvar[:, p:p + 1], ls2[:, p:p + 1], 1.0 / DM, lvar[:, p:p + 1],
                ALU.mult, ALU.subtract),
                reads=(f"ls2{p}", f"lvar{p}"), writes=(f"lvar{p}",))
            ms.op("vector", lambda e, p=p: e.tensor_scalar_add(
                lvar[:, p:p + 1], lvar[:, p:p + 1], EPS),
                reads=(f"lvar{p}",), writes=(f"lvar{p}",))
            ms.op("scalar", lambda e, p=p: e.activation(
                lsv[:, p:p + 1], lvar[:, p:p + 1], AF.Sqrt),
                reads=(f"lvar{p}",), writes=(f"lsv{p}",))
            ms.op("vector", lambda e, p=p: e.reciprocal(
                lrst[:, p:p + 1], lsv[:, p:p + 1]),
                reads=(f"lsv{p}",), writes=(f"lrst{p}",))
            ms.op("vector", lambda e, p=p: e.tensor_mul(
                lt0[:, p:p + 1], lrst[:, p:p + 1], lrst[:, p:p + 1]),
                reads=(f"lrst{p}",), writes=(f"lt0{p}",))
            ms.op("vector", lambda e, p=p: e.tensor_mul(
                lt0[:, p:p + 1], lt0[:, p:p + 1], lvar[:, p:p + 1]),
                reads=(f"lt0{p}", f"lvar{p}"), writes=(f"lt0{p}",))
            ms.op("vector", lambda e, p=p: e.tensor_scalar(
                lt0[:, p:p + 1], lt0[:, p:p + 1], -0.5, 1.5, ALU.mult, ALU.add),
                reads=(f"lt0{p}",), writes=(f"lt0{p}",))
            ms.op("vector", lambda e, p=p: e.tensor_mul(
                lrst[:, p:p + 1], lrst[:, p:p + 1], lt0[:, p:p + 1]),
                reads=(f"lrst{p}", f"lt0{p}"), writes=(f"lrst{p}",))
            ms.op("vector", lambda e, p=p: e.tensor_scalar(
                xnf[:, p * DM:(p + 1) * DM], xln[:, p * DM:(p + 1) * DM],
                lnmu[:, p:p + 1], lrst[:, p:p + 1], ALU.add, ALU.mult),
                reads=(f"xln{p}", f"lnmu{p}", f"lrst{p}"), writes=(f"xnf{p}",))
            ms.op("vector", lambda e, p=p: e.tensor_scalar(
                xnb[:, p * DM:(p + 1) * DM], xln[:, p * DM:(p + 1) * DM],
                lnmu[:, p:p + 1], lrst[:, p:p + 1], ALU.add, ALU.mult),
                reads=(f"xln{p}", f"lnmu{p}", f"lrst{p}"), writes=(f"xnb{p}",))
            ms.dma("gpsimd", lambda e, tt=tt, p=p: e.dma_start(
                out=E.xn_tm[tt * 128:(tt + 1) * 128, :],
                in_=xnf[:, p * DM:(p + 1) * DM]),
                reads=(f"xnf{p}",), tag=f"st_tm{tt}")
            for j in range(8):
                k = tt * 8 + j
                q = k % 2
                ms.dma("sync", lambda e, p=p, j=j, q=q: e.dma_start_transpose(
                    xnbt[:, q * 128:(q + 1) * 128],
                    xnb[:, p * DM + j * 128: p * DM + (j + 1) * 128]),
                    reads=(f"xnb{p}",), writes=(f"xnbt{q}",))
                ms.dma("gpsimd", lambda e, tt=tt, j=j, q=q: e.dma_start(
                    out=E.xn_fm[j * 128:(j + 1) * 128, tt * 128:(tt + 1) * 128],
                    in_=xnbt[:, q * 128:(q + 1) * 128]),
                    reads=(f"xnbt{q}",))
            yield


def gen_fold(nc, E, ms, B):
    fstg, fstr, fR, fva, fwr, fS, fxvu, fxvd, fbf, fide, fub, fhb, fps = (
        B.fstg, B.fstr, B.fR, B.fva, B.fwr, B.fS, B.fxvu, B.fxvd, B.fbf,
        B.fide, B.fub, B.fhb, B.fps)
    if True:
        ms.dma("scalar", lambda e: e.dma_start(out=fide[:, :], in_=E.ident_in[:, :]),
               writes=("fide",))
        ms.dma("scalar", lambda e: e.dma_start(
            out=fub[:, :], in_=bass.AP(E.upb, 0, [[1, 128], [128, 32]]),
            allow_slow_non_contiguous=True),
            writes=("fub",))
        ms.dma("scalar", lambda e: e.dma_start(out=fxvu[:, :], in_=E.bfu_xv[:, :]),
               writes=("fxvu",))
        ms.dma("scalar", lambda e: e.dma_start(out=fxvd[:, :], in_=E.bfd_xv[:, :]),
               writes=("fxvd",))

        slot = [0]

        def ldw(src, si, gi, gstride):
            sl = slot[0]
            slot[0] = (slot[0] + 1) % 4
            base = (si * gstride + gi) * 128 * 8 * 128
            ms.dma("gpsimd", lambda e, src=src, base=base, sl=sl: e.dma_start(
                out=fstg[:, sl * 8 * 128: (sl + 1) * 8 * 128],
                in_=bass.AP(src, base, [[8 * 128, 128], [1, 8 * 128]])),
                writes=(f"stg{sl}",), tag="ldw")
            ms.op("vector", lambda e, sl=sl: e.tensor_copy(
                fstr[:, sl * 8 * 128: (sl + 1) * 8 * 128],
                fstg[:, sl * 8 * 128: (sl + 1) * 8 * 128]),
                reads=(f"stg{sl}",), writes=(f"str{sl}",), tag="castw")
            return sl

        for bf in ("u", "d"):
            wsrc = E.bfu_within if bf == "u" else E.bfd_within
            xsrc = E.bfu_cross if bf == "u" else E.bfd_cross
            seed = 6 if bf == "u" else 0
            chain = list(range(5, -1, -1)) if bf == "u" else list(range(1, 7))

            for g in range(4):
                sl = slot[0]
                slot[0] = (slot[0] + 1) % 4
                base = (seed * 4 + g) * 128 * 8 * 128
                ms.dma("gpsimd", lambda e, base=base, sl=sl, wsrc=wsrc: e.dma_start(
                    out=fstg[:, sl * 8 * 128:(sl + 1) * 8 * 128],
                    in_=bass.AP(wsrc, base, [[8 * 128, 128], [1, 8 * 128]])),
                    writes=(f"stg{sl}",), tag="ldseed")
                ms.op("vector", lambda e, g=g, sl=sl: e.tensor_copy(
                    fR[:, g * 8 * 128:(g + 1) * 8 * 128],
                    fstg[:, sl * 8 * 128:(sl + 1) * 8 * 128]),
                    reads=(f"stg{sl}",),
                    writes=tuple(f"R{g * 8 + jj}" for jj in range(8)), tag="seedcast")
                yield

            if bf == "u" and "fRs" in E.dump_t:
                ms.dma("gpsimd", lambda e: e.dma_start(
                    out=E.dump_t["fRs"][:, :], in_=fR[:, :].bitcast(F32)),
                    reads=tuple(f"R{j}" for j in range(32)), tag="dbgRs")

            for ss in chain:
                for rnd in range(4):
                    sl = ldw(wsrc, ss, rnd, 4)
                    for q in range(8):
                        j = rnd * 8 + q
                        ms.op("tensor", lambda e, q=q, j=j, sl=sl: e.matmul(
                            fps[:, q * 512: q * 512 + 128],
                            fstr[:, sl * 8 * 128 + q * 128: sl * 8 * 128 + (q + 1) * 128],
                            fR[:, j * 128:(j + 1) * 128],
                            start=True, stop=True),
                            reads=(f"str{sl}", f"R{j}"), writes=(f"pb{q}",),
                            tag=f"cmm{bf}{ss}")
                    ms.op("scalar", lambda e, rnd=rnd: e.activation(
                        bass.AP(fR, rnd * 8 * 128, [[32 * 128, 128], [128, 8], [1, 128]]),
                        bass.AP(fps, 0, [[4096, 128], [512, 8], [1, 128]]),
                        AF.Identity),
                        reads=tuple(f"pb{q}" for q in range(8)),
                        writes=tuple(f"R{rnd * 8 + q}" for q in range(8)),
                        tag=f"cev{bf}{ss}")
                    yield

            if bf == "u" and "fRu" in E.dump_t:
                ms.dma("gpsimd", lambda e: e.dma_start(
                    out=E.dump_t["fRu"][:, :], in_=fR[:, :].bitcast(F32)),
                    reads=tuple(f"R{j}" for j in range(32)), tag="dbgR")

            chunks = [(0, 512), (512, 512), (1024, 2)] if bf == "u" else [(0, 512), (512, 512)]
            vsrc = E.wexpc if bf == "u" else E.wcon
            for (c0, w) in chunks:
                for g in range(8):
                    if w == 512:
                        ci = c0 // 512
                        ms.dma("gpsimd", lambda e, g=g, ci=ci, vsrc=vsrc:
                               e.dma_start(
                            out=fva[:, g * 2048:(g + 1) * 2048],
                            in_=bass.AP(vsrc, (ci * 8 + g) * 128 * 2048,
                                        [[2048, 128], [1, 2048]])),
                            writes=(f"va{g}",), tag="ldv")
                    else:
                        ms.dma("gpsimd", lambda e, g=g: e.dma_start(
                            out=bass.AP(fva, g * 4 * 512,
                                        [[32 * 512, 128], [512, 4], [1, 2]]),
                            in_=bass.AP(E.wexpcb, g * 128 * 8, [[8, 128], [1, 8]])),
                            writes=(f"va{g}",), tag="ldvb")
                    if g % 2 == 1:
                        yield
                if bf == "u":
                    passes = [("W", None)] + [("X", xi) for xi in range(5)]
                else:
                    passes = [("X", xi) for xi in range(4, -1, -1)] + [("W", None)]
                for (kind, xi) in passes:
                    if kind == "W":
                        for g in range(8):
                            ph = g % 2
                            ms.op("vector", lambda e, g=g, ph=ph: e.tensor_copy(
                                fwr[:, ph * 2048:(ph + 1) * 2048],
                                fva[:, g * 2048:(g + 1) * 2048]),
                                reads=(f"va{g}",), writes=(f"fwr{ph}",),
                                tag="wcast")
                            for q in range(4):
                                jo = g * 4 + q
                                pslice = fps[:, ph * 2048 + q * 512: ph * 2048 + q * 512 + w]
                                ms.op("tensor", lambda e, jo=jo, q=q, w=w, ph=ph,
                                      pslice=pslice: e.matmul(
                                    pslice, fR[:, jo * 128:(jo + 1) * 128],
                                    fwr[:, ph * 2048 + q * 512: ph * 2048 + q * 512 + w],
                                    start=True, stop=True),
                                    reads=(f"R{jo}", f"fwr{ph}"),
                                    writes=(f"pb{ph * 4 + q}",), tag=f"pw{bf}")
                            ms.op("scalar", lambda e, g=g, w=w, ph=ph: e.activation(
                                bass.AP(fva, g * 4 * 512,
                                        [[32 * 512, 128], [512, 4], [1, w]]),
                                bass.AP(fps, ph * 2048, [[4096, 128], [512, 4], [1, w]]),
                                AF.Identity),
                                reads=tuple(f"pb{ph * 4 + qq}" for qq in range(4)),
                                writes=(f"va{g}",), tag=f"pe{bf}")
                            yield
                    else:
                        fxv = fxvu if bf == "u" else fxvd
                        kxv = "fxvu" if bf == "u" else "fxvd"
                        ts_ = 1 << xi
                        pi = 0
                        for ja in range(32):
                            if ja & ts_:
                                continue
                            jb = ja + ts_
                            ga, gb = ja // 4, jb // 4
                            ca0 = xi * 64 + ja * 2
                            cb0 = xi * 64 + jb * 2
                            ms.op("vector", lambda e, ja=ja, w=w, cb0=cb0, fxv=fxv:
                                  e.tensor_scalar_mul(
                                fS[:, 0:w], fva[:, ja * 512: ja * 512 + w],
                                fxv[:, cb0:cb0 + 1]),
                                reads=(f"va{ga}", kxv), writes=("fS",), tag="xs")
                            ms.op("vector", lambda e, ja=ja, w=w, ca0=ca0, fxv=fxv:
                                  e.tensor_scalar_mul(
                                fva[:, ja * 512: ja * 512 + w],
                                fva[:, ja * 512: ja * 512 + w],
                                fxv[:, ca0:ca0 + 1]),
                                reads=(f"va{ga}", kxv), writes=(f"va{ga}",),
                                tag="xa")
                            ms.op("vector", lambda e, ja=ja, jb=jb, w=w, ca0=ca0,
                                  fxv=fxv: e.scalar_tensor_tensor(
                                fva[:, ja * 512: ja * 512 + w],
                                fva[:, jb * 512: jb * 512 + w],
                                fxv[:, ca0 + 1:ca0 + 2],
                                fva[:, ja * 512: ja * 512 + w],
                                ALU.mult, ALU.add),
                                reads=(f"va{gb}", f"va{ga}", kxv),
                                writes=(f"va{ga}",), tag="xb")
                            ms.op("vector", lambda e, jb=jb, w=w, cb0=cb0, fxv=fxv:
                                  e.tensor_scalar_mul(
                                fva[:, jb * 512: jb * 512 + w],
                                fva[:, jb * 512: jb * 512 + w],
                                fxv[:, cb0 + 1:cb0 + 2]),
                                reads=(f"va{gb}", kxv), writes=(f"va{gb}",),
                                tag="xc")
                            ms.op("vector", lambda e, jb=jb, w=w: e.tensor_add(
                                fva[:, jb * 512: jb * 512 + w],
                                fva[:, jb * 512: jb * 512 + w],
                                fS[:, 0:w]),
                                reads=(f"va{gb}", "fS"), writes=(f"va{gb}",),
                                tag="xd")
                            pi += 1
                            if pi % 4 == 0:
                                yield
                    if (bf == "u" and c0 == 1024 and kind == "W"
                            and "vW" in E.dump_t):
                        ms.dma("gpsimd", lambda e: e.dma_start(
                            out=E.dump_t["vW"][:, :], in_=fva[:, :]),
                            reads=tuple(f"va{g}" for g in range(8)), tag="dbgW")
                if bf == "u" and w == 2:
                    ms.op("vector", lambda e: e.tensor_add(
                        fhb[:, :],
                        bass.AP(fva, 0, [[32 * 512, 128], [512, 32], [1, 1]]),
                        fub[:, :]),
                        reads=tuple(f"va{g}" for g in range(8)) + ("fub",),
                        writes=("fhb",), tag="bias")
                    ms.dma("gpsimd", lambda e: e.dma_start(
                        out=bass.AP(E.hbias, 0, [[1, 128], [128, 32]]),
                        in_=fhb[:, :], allow_slow_non_contiguous=True),
                        reads=("fhb",), tag="sthb")
                elif bf == "u":
                    for tb in range(w // 128):
                        for j in range(32):
                            ms.op("tensor", lambda e, j=j, tb=tb: e.transpose(
                                fps[:, j * 128:(j + 1) * 128],
                                fva[:, j * 512 + tb * 128: j * 512 + (tb + 1) * 128],
                                fide[:, :]),
                                reads=(f"va{j // 4}", "fide"),
                                writes=(f"pb{j // 4}",), tag="tr")
                        ms.op("scalar", lambda e: e.activation(
                            fbf[:, :], fps[:, :], AF.Identity),
                            reads=tuple(f"pb{qq}" for qq in range(8)),
                            writes=("fbf0", "fbf1"), tag="trev")
                        for j in range(32):
                            ms.dma("gpsimd", lambda e, j=j, tb=tb, c0=c0: e.dma_start(
                                out=E.wexp_p[c0 + tb * 128: c0 + (tb + 1) * 128,
                                             j * 128:(j + 1) * 128],
                                in_=fbf[:, j * 128:(j + 1) * 128]),
                                reads=(f"fbf{j // 16}",), tag="stwe")
                        yield
                else:
                    for g in range(8):
                        fb = g % 2
                        ms.op("scalar", lambda e, g=g, w=w, fb=fb: e.activation(
                            bass.AP(fbf, fb * 2048, [[4096, 128], [512, 4], [1, w]]),
                            bass.AP(fva, g * 4 * 512,
                                    [[32 * 512, 128], [512, 4], [1, w]]),
                            AF.Identity),
                            reads=(f"va{g}",), writes=(f"fbf{fb}",), tag="dnev")
                        for q in range(4):
                            jo = g * 4 + q
                            ms.dma("gpsimd", lambda e, jo=jo, q=q, fb=fb, c0=c0, w=w:
                                   e.dma_start(
                                out=E.wcon_p[jo * 128:(jo + 1) * 128, c0:c0 + w],
                                in_=bass.AP(fbf, fb * 2048 + q * 512,
                                            [[4096, 128], [1, w]])),
                                reads=(f"fbf{fb}",), tag="stwc")
                        if fb == 1:
                            yield


def phase_lnfold(nc, E, st, do_ln=True, do_fold=True):
    with contextlib.ExitStack() as est:
        t_ = lambda name, shape, dt=F32: est.enter_context(
            nc.sbuf_tensor(name, shape, dt))
        B = Env()
        if do_ln:
            B.xln = t_("xln", [128, 2 * DM])
            B.lsq = t_("lsq", [128, DM])
            for nm in ("ls1", "ls2", "lsv", "lmu", "lnmu", "lvar", "lrst", "lt0"):
                setattr(B, nm, t_(nm, [128, 2]))
            B.xnf = t_("xnf", [128, 2 * DM])
            B.xnb = t_("xnb", [128, 2 * DM], BF16)
            B.xnbt = t_("xnbt", [128, 2 * 128], BF16)
        if do_fold:
            B.fstg = t_("fstg", [128, 4 * 8 * 128])
            B.fstr = t_("fstr", [128, 4 * 8 * 128], F32R)
            B.fR = t_("fR", [128, 32 * 128], F32R)
            B.fva = t_("fva", [128, 32 * 512])
            B.fwr = t_("fwr", [128, 2 * 2048], F32R)
            B.fS = t_("fS", [128, 512])
            B.fxvu = t_("fxvu", [128, 320])
            B.fxvd = t_("fxvd", [128, 320])
            B.fbf = t_("fbf", [128, 4096], BF16)
            B.fide = t_("fide", [128, 128])
            B.fub = t_("fub", [128, 32])
            B.fhb = t_("fhb", [128, 32])
            B.fps = est.enter_context(nc.psum_tensor("fps", [128, 4096], F32))
        blk = est.enter_context(nc.Block())
        ms = MiniSched(nc, E.pool, "lf")
        gens = []
        if do_ln:
            gens.append((gen_ln(nc, E, ms, B), 1))
        if do_fold:
            gens.append((gen_fold(nc, E, ms, B), 10))
        live = [g for g in gens]
        while live:
            nxt = []
            for (g, k) in live:
                alive = True
                for _ in range(k):
                    try:
                        next(g)
                    except StopIteration:
                        alive = False
                        break
                if alive:
                    nxt.append((g, k))
            live = nxt
        ms.run_block(blk)


def phase_mamba(nc, E, st, li):
    xsrc = E.xn_fm if li == 0 else E.cout_o1
    cin = E.cin_o1 if li == 0 else E.cin_o2
    cout = E.cout_o1 if li == 0 else E.cout_o2
    TM = 512
    NC = L // TM

    with contextlib.ExitStack() as est:
        t_ = lambda name, shape, dt=F32: est.enter_context(
            nc.sbuf_tensor(f"{name}_{li}", shape, dt))
        wstg = t_("wstg", [128, 2 * 2048])
        winb = t_("winb", [128, 8 * 2048], BF16)
        wxpb = t_("wxpb", [128, 8 * 96], BF16)
        mcv = t_("mcv", [128, 8 * DC])
        mbc = t_("mbc", [128, 8])
        xrh = t_("xrh", [128, 2 * 8 * TM], BF16)
        xcs = t_("xcs", [128, 8 * (TM + 3)])
        cac = t_("cac", [128, TM])
        mub = t_("mub", [128, 8 * TM], BF16)
        mzb = t_("mzb", [128, 8 * TM], BF16)
        xdp = t_("xdp", [128, 8 * TM])
        mps = est.enter_context(nc.psum_tensor(f"mps_{li}", [128, 4096], F32))
        blk = est.enter_context(nc.Block())
        ms = MiniSched(nc, E.pool, "ma")

        for kd in range(8):
            sl = kd % 2
            ms.dma("sync", lambda e, kd=kd, sl=sl: e.dma_start(
                out=wstg[:, sl * 2048:(sl + 1) * 2048],
                in_=E.win[li][kd * 128:(kd + 1) * 128, :]),
                writes=(f"wstg{sl}",), tag="ldwin")
            ms.op("scalar", lambda e, kd=kd, sl=sl: e.activation(
                winb[:, kd * 2048:(kd + 1) * 2048],
                wstg[:, sl * 2048:(sl + 1) * 2048], AF.Identity),
                reads=(f"wstg{sl}",), writes=(f"winb{kd}",), tag="cwin")
        for kd in range(8):
            sl = kd % 2
            ms.dma("sync", lambda e, kd=kd, sl=sl: e.dma_start(
                out=bass.AP(wstg, sl * 2048, [[2 * 2048, 128], [1, 96]]),
                in_=E.wxp[li][kd * 128:(kd + 1) * 128, :]),
                writes=(f"wstg{sl}",), tag="ldwxp")
            ms.op("scalar", lambda e, kd=kd, sl=sl: e.activation(
                wxpb[:, kd * 96:(kd + 1) * 96],
                bass.AP(wstg, sl * 2048, [[2 * 2048, 128], [1, 96]]), AF.Identity),
                reads=(f"wstg{sl}",), writes=(f"wxpb{kd}",), tag="cwxp")
        ms.dma("sync", lambda e: e.dma_start(out=mcv[:, :], in_=bass.AP(
            E.wconv[li], 0, [[DC, 128], [128 * DC, 8], [1, DC]])),
            writes=("mcv",), tag="ldcv")
        ms.dma("sync", lambda e: e.dma_start(out=mbc[:, :], in_=bass.AP(
            E.bconv[li], 0, [[1, 128], [128, 8]]),
            allow_slow_non_contiguous=True), writes=("mbc",), tag="ldbc")

        for t in range(NC):
            p = t % 2
            if li == 0:
                xap = bass.AP(xsrc, t * TM, [[L, 128], [128 * L, 8], [1, TM]])
            else:
                xap = bass.AP(xsrc, t * DM * TM, [[TM, 128], [128 * TM, 8], [1, TM]])
            ms.dma("sync", lambda e, xap=xap, p=p: e.dma_start(
                out=xrh[:, p * 8 * TM:(p + 1) * 8 * TM], in_=xap),
                writes=(f"xrh{p}",), tag="ldxr")
            for m in range(16):
                bank = m % 8
                for kd in range(8):
                    ms.op("tensor", lambda e, m=m, kd=kd, bank=bank, p=p: e.matmul(
                        mps[:, bank * 512: bank * 512 + TM],
                        winb[:, kd * 2048 + m * 128: kd * 2048 + (m + 1) * 128],
                        xrh[:, p * 8 * TM + kd * TM: p * 8 * TM + (kd + 1) * TM],
                        start=(kd == 0), stop=(kd == 7)),
                        reads=(f"winb{kd}", f"xrh{p}"), writes=(f"pb{bank}",),
                        tag="mmin")
                if m < 8:
                    ms.op("scalar", lambda e, m=m, bank=bank: e.activation(
                        xcs[:, m * (TM + 3) + 3: (m + 1) * (TM + 3)],
                        mps[:, bank * 512: bank * 512 + TM], AF.Identity),
                        reads=(f"pb{bank}",), writes=(f"xcs{m}",), tag="evxc")
                else:
                    d = m - 8
                    ms.op("scalar", lambda e, d=d, bank=bank: e.activation(
                        mzb[:, d * TM:(d + 1) * TM],
                        mps[:, bank * 512: bank * 512 + TM], AF.Silu),
                        reads=(f"pb{bank}",), writes=(f"mzb{d}",), tag="evz")
            ms.dma("gpsimd", lambda e, t=t: e.dma_start(
                out=bass.AP(E.zt[li], t * TM, [[L, 128], [128 * L, 8], [1, TM]]),
                in_=mzb[:, :]),
                reads=tuple(f"mzb{d}" for d in range(8)), tag="stz")
            for d in range(8):
                if t == 0:
                    ms.op("vector", lambda e, d=d: e.memset(
                        xcs[:, d * (TM + 3): d * (TM + 3) + 3], 0.0),
                        writes=(f"xch{d}",), tag="cvz")
                ms.op("vector", lambda e, d=d: e.tensor_scalar_mul(
                    cac[:, :], xcs[:, d * (TM + 3): d * (TM + 3) + TM],
                    mcv[:, d * DC: d * DC + 1]),
                    reads=(f"xcs{d}", f"xch{d}", "mcv"), writes=("cac",), tag="cv0")
                for k in range(1, DC):
                    ms.op("vector", lambda e, d=d, k=k: e.scalar_tensor_tensor(
                        cac[:, :], xcs[:, d * (TM + 3) + k: d * (TM + 3) + k + TM],
                        mcv[:, d * DC + k: d * DC + k + 1], cac[:, :],
                        ALU.mult, ALU.add),
                        reads=(f"xcs{d}", f"xch{d}", "mcv", "cac"), writes=("cac",),
                        tag=f"cv{k}")
                ms.op("scalar", lambda e, d=d: e.activation(
                    mub[:, d * TM:(d + 1) * TM], cac[:, :], AF.Silu,
                    bias=mbc[:, d:d + 1]),
                    reads=("cac", "mbc"), writes=(f"mub{d}",), tag="evu")
                ms.op("vector", lambda e, d=d: e.tensor_copy(
                    xcs[:, d * (TM + 3): d * (TM + 3) + 3],
                    xcs[:, d * (TM + 3) + TM: d * (TM + 3) + TM + 3]),
                    reads=(f"xcs{d}",), writes=(f"xch{d}",), tag="halo")
            ms.dma("gpsimd", lambda e, t=t: e.dma_start(
                out=bass.AP(E.ut[li], t * TM, [[L, 128], [128 * L, 8], [1, TM]]),
                in_=mub[:, :]),
                reads=tuple(f"mub{d}" for d in range(8)), tag="stu")
            for kd in range(8):
                ms.op("tensor", lambda e, kd=kd: e.matmul(
                    mps[0:96, 0:TM],
                    wxpb[:, kd * 96:(kd + 1) * 96],
                    mub[:, kd * TM:(kd + 1) * TM],
                    start=(kd == 0), stop=(kd == 7)),
                    reads=(f"mub{kd}",), writes=("pb0",), tag="mmxp")
            ms.op("scalar", lambda e, t=t: e.activation(
                bass.AP(xdp, t * TM, [[8 * TM, 96], [1, TM]]),
                mps[0:96, 0:TM], AF.Identity),
                reads=("pb0",), writes=(f"xdp{t}",), tag="evxp")
            ms.dma("gpsimd", lambda e, t=t: e.dma_start(
                out=bass.AP(E.cin_xd[li], t * 96 * TM, [[TM, 96], [1, TM]]),
                in_=bass.AP(xdp, t * TM, [[8 * TM, 96], [1, TM]])),
                reads=(f"xdp{t}",), writes=(f"cinxd{t}",), tag="stxd")
            ms.cc("gpsimd", lambda e, t=t: e.collective_compute(
                "AllReduce", ALU.add, replica_groups=RG_PAIRS,
                ins=[bass.AP(E.cin_xd[li], t * 96 * TM, [[TM, 96], [1, TM]])],
                outs=[bass.AP(E.cout_xd[li], t * 96 * TM, [[TM, 96], [1, TM]])]),
                reads=(f"cinxd{t}",), writes=(f"xdred{t}",), tag="ccxd")
        ms.op("gpsimd", lambda e: e.nop(),
              reads=tuple(f"xdred{t}" for t in range(NC)), tag="ccwait")
        ms.run_block(blk)

    TS_ = 512
    NS = L // TS_
    with contextlib.ExitStack() as est:
        t_ = lambda name, shape, dt=F32: est.enter_context(
            nc.sbuf_tensor(f"{name}b{li}", shape, dt))
        dtb = t_("dtb", [64, L], BF16)
        bcb = t_("bcb", [32, L], BF16)
        wdtb = t_("wdtb", [64, 1024], BF16)
        woutb = t_("woutb", [128, 8 * 1024], BF16)
        wstg2 = t_("wstg2", [128, 1024])
        maneg = t_("maneg", [128, 8 * NST])
        mbdt = t_("mbdt", [128, 8])
        mdv = t_("mdv", [128, 8])
        idb = t_("idb", [128, 128], BF16)
        idf = t_("idf", [128, 128])
        mdel = t_("mdel", [128, 8 * TS_])
        mdu = t_("mdu", [128, 8 * TS_], BF16)
        muc = t_("muc", [128, 8 * TS_], BF16)
        mzc = t_("mzc", [128, 8 * TS_], BF16)
        mbt = t_("mbt", [128, 2 * NST * TS_], BF16)
        mct = t_("mct", [128, NST * TS_], BF16)
        mda = t_("mda", [128, 4 * TS_])
        mdbu = t_("mdbu", [128, 2 * NST * TS_], BF16)
        mh = t_("mh", [128, NST * TS_], BF16)
        mhst = t_("mhst", [128, 8 * NST])
        mtt = t_("mtt", [128, TS_])
        msp = t_("msp", [128, 4 * TS_])
        my2 = t_("my2", [128, 8 * TS_], BF16)
        ost = t_("ost", [128, 2 * TS_], BF16 if li == 0 else F32)
        mps2 = est.enter_context(nc.psum_tensor(f"mps2_{li}", [128, 4096], F32))
        blk = est.enter_context(nc.Block())
        ms = MiniSched(nc, E.pool, "mb")

        for q in range(NS):
            ms.dma("sync", lambda e, q=q: e.dma_start(
                out=wstg2[0:96, 0:TS_],
                in_=bass.AP(E.cout_xd[li], q * 96 * TS_, [[TS_, 96], [1, TS_]])),
                writes=("wstg0",), tag="ldxd")
            ms.op("scalar", lambda e, q=q: e.activation(
                dtb[:, q * TS_:(q + 1) * TS_], wstg2[0:64, 0:TS_], AF.Identity),
                reads=("wstg0",), writes=("dtb",), tag="cdt")
            ms.op("scalar", lambda e, q=q: e.activation(
                bcb[:, q * TS_:(q + 1) * TS_], wstg2[64:96, 0:TS_], AF.Identity),
                reads=("wstg0",), writes=("bcb",), tag="cbc")
        ms.dma("gpsimd", lambda e: e.dma_start(out=E.bct[li][:, :], in_=bcb[:, :]),
               reads=("bcb",), writes=("bct",), tag="stbc")
        ms.dma("sync", lambda e: e.dma_start(out=wstg2[0:64, 0:1024],
                                             in_=E.wdt[li][:, :]),
               writes=("wstg0",), tag="ldwdt")
        ms.op("scalar", lambda e: e.activation(wdtb[:, :], wstg2[0:64, 0:1024],
                                               AF.Identity),
              reads=("wstg0",), writes=("wdtb",), tag="cwdt")
        for kd in range(8):
            ms.dma("sync", lambda e, kd=kd: e.dma_start(
                out=wstg2[:, 0:1024],
                in_=E.wout[li][kd * 128:(kd + 1) * 128, :]),
                writes=("wstg0",), tag="ldwo")
            ms.op("scalar", lambda e, kd=kd: e.activation(
                woutb[:, kd * 1024:(kd + 1) * 1024],
                wstg2[:, 0:1024], AF.Identity),
                reads=("wstg0",), writes=(f"woutb{kd}",), tag="cwo")
        ms.dma("sync", lambda e: e.dma_start(
            out=maneg[:, :], in_=bass.AP(E.alog[li], 0,
                                         [[NST, 128], [128 * NST, 8], [1, NST]])),
            writes=("maneg",), tag="lda")
        ms.op("scalar", lambda e: e.activation(maneg[:, :], maneg[:, :], AF.Exp),
              reads=("maneg",), writes=("maneg",), tag="expa")
        ms.op("vector", lambda e: e.tensor_scalar_mul(maneg[:, :], maneg[:, :], -1.0),
              reads=("maneg",), writes=("maneg",), tag="nega")
        ms.dma("sync", lambda e: e.dma_start(
            out=mbdt[:, :], in_=bass.AP(E.bdt[li], 0, [[1, 128], [128, 8]]),
            allow_slow_non_contiguous=True), writes=("mbdt",), tag="ldbdt")
        ms.dma("sync", lambda e: e.dma_start(
            out=mdv[:, :], in_=bass.AP(E.dvec[li], 0, [[1, 128], [128, 8]]),
            allow_slow_non_contiguous=True), writes=("mdv",), tag="lddv")
        ms.dma("sync", lambda e: e.dma_start(out=idf[:, :], in_=E.ident_in[:, :]),
               writes=("idf",), tag="ldid")
        ms.op("scalar", lambda e: e.activation(idb[:, :], idf[:, :], AF.Identity),
              reads=("idf",), writes=("idb",), tag="cid")

        def sec_C(t):
            sb = (t % 2) * NST * TS_
            for n in range(NST):
                ms.dma("sync", lambda e, n=n, t=t, sb=sb: e.dma_start(
                    out=mbt[:, sb + n * TS_: sb + (n + 1) * TS_],
                    in_=bass.AP(E.bct[li], n * L + t * TS_, [[0, 128], [1, TS_]])),
                    reads=("bct",), writes=(f"mbt{t % 2}_{n // 4}",), tag="bcb_")
            for hh in range(2):
                ms.dma("gpsimd", lambda e, t=t, hh=hh: e.dma_start(
                    out=muc[:, hh * 4 * TS_:(hh + 1) * 4 * TS_],
                    in_=bass.AP(E.ut[li], hh * 4 * 128 * L + t * TS_,
                                [[L, 128], [128 * L, 4], [1, TS_]])),
                    writes=(f"muc{hh}",), tag="lduc")
                ms.dma("gpsimd", lambda e, t=t, hh=hh: e.dma_start(
                    out=mzc[:, hh * 4 * TS_:(hh + 1) * 4 * TS_],
                    in_=bass.AP(E.zt[li], hh * 4 * 128 * L + t * TS_,
                                [[L, 128], [128 * L, 4], [1, TS_]])),
                    writes=(f"mzc{hh}",), tag="ldzc")
            for n in range(NST):
                ms.dma("scalar", lambda e, n=n, t=t: e.dma_start(
                    out=mct[:, n * TS_:(n + 1) * TS_],
                    in_=bass.AP(E.bct[li], (NST + n) * L + t * TS_,
                                [[0, 128], [1, TS_]])),
                    reads=("bct",), writes=(f"mct{n // 4}",), tag="bcc_")
            for d in range(8):
                sp = d % 2
                bk = d % 4
                ms.op("tensor", lambda e, d=d, t=t, bk=bk: e.matmul(
                    mps2[:, bk * 512: bk * 512 + TS_],
                    wdtb[:, d * 128:(d + 1) * 128],
                    dtb[:, t * TS_:(t + 1) * TS_],
                    start=True, stop=True),
                    reads=("wdtb", "dtb"), writes=(f"pb{bk}",), tag="mmdt")
                ms.op("scalar", lambda e, d=d, sp=sp, bk=bk: e.activation(
                    msp[:, sp * 2 * TS_: sp * 2 * TS_ + TS_],
                    mps2[:, bk * 512: bk * 512 + TS_], AF.Abs,
                    bias=mbdt[:, d:d + 1]),
                    reads=(f"pb{bk}", "mbdt"), writes=(f"mspa{sp}",), tag="spabs")
                ms.op("scalar", lambda e, d=d, sp=sp, bk=bk: e.activation(
                    msp[:, sp * 2 * TS_ + TS_: sp * 2 * TS_ + 2 * TS_],
                    mps2[:, bk * 512: bk * 512 + TS_], AF.Relu,
                    bias=mbdt[:, d:d + 1]),
                    reads=(f"pb{bk}", "mbdt"), writes=(f"mspr{sp}",), tag="sprelu")
                ms.op("scalar", lambda e, sp=sp: e.activation(
                    msp[:, sp * 2 * TS_: sp * 2 * TS_ + TS_],
                    msp[:, sp * 2 * TS_: sp * 2 * TS_ + TS_], AF.Exp, scale=-1.0),
                    reads=(f"mspa{sp}",), writes=(f"mspa{sp}",), tag="spexp")
                ms.op("scalar", lambda e, sp=sp: e.activation(
                    msp[:, sp * 2 * TS_: sp * 2 * TS_ + TS_],
                    msp[:, sp * 2 * TS_: sp * 2 * TS_ + TS_], AF.Ln, bias=1.0),
                    reads=(f"mspa{sp}",), writes=(f"mspa{sp}",), tag="spln")
                ms.op("vector", lambda e, d=d, sp=sp: e.tensor_add(
                    mdel[:, d * TS_:(d + 1) * TS_],
                    msp[:, sp * 2 * TS_: sp * 2 * TS_ + TS_],
                    msp[:, sp * 2 * TS_ + TS_: sp * 2 * TS_ + 2 * TS_]),
                    reads=(f"mspa{sp}", f"mspr{sp}"), writes=(f"mdel{d}",),
                    tag="spadd")
                ms.op("vector", lambda e, d=d: e.tensor_mul(
                    mdu[:, d * TS_:(d + 1) * TS_],
                    mdel[:, d * TS_:(d + 1) * TS_],
                    muc[:, d * TS_:(d + 1) * TS_]),
                    reads=(f"mdel{d}", f"muc{d // 4}"), writes=(f"mdu{d}",),
                    tag="du")
                if d == 0:
                    for n in range(4):
                        ms.op("scalar", lambda e, n=n: e.activation(
                            mda[:, n * TS_:(n + 1) * TS_],
                            mdel[:, 0:TS_], AF.Exp,
                            scale=maneg[:, n:n + 1]),
                            reads=("mdel0", "maneg"), writes=(f"mda{n}",),
                            tag="exppre")

        def tails(t, d):
            bk = d % 4
            ms.op("vector", lambda e, d=d, bk=bk: e.scalar_tensor_tensor(
                mtt[:, :], muc[:, d * TS_:(d + 1) * TS_],
                mdv[:, d:d + 1],
                mps2[:, bk * 512: bk * 512 + TS_],
                ALU.mult, ALU.add),
                reads=(f"muc{d // 4}", "mdv", f"pb{bk}"), writes=("mtt",),
                tag="tail1")
            ms.op("vector", lambda e, d=d: e.tensor_mul(
                my2[:, d * TS_:(d + 1) * TS_], mtt[:, :],
                mzc[:, d * TS_:(d + 1) * TS_]),
                reads=("mtt", f"mzc{d // 4}"), writes=(f"my2{d}",), tag="tail2")

        def wave1(t, kd):
            for mo in range(4):
                ms.op("tensor", lambda e, mo=mo, kd=kd: e.matmul(
                    mps2[:, (4 + mo) * 512: (4 + mo) * 512 + TS_],
                    woutb[:, kd * 1024 + mo * 128: kd * 1024 + (mo + 1) * 128],
                    my2[:, kd * TS_:(kd + 1) * TS_],
                    start=(kd == 0), stop=(kd == 7)),
                    reads=(f"woutb{kd}", f"my2{kd}"), writes=(f"pb{4 + mo}",),
                    tag="mmo1")

        def sec_D(t):
            for d in range(8):
                sx = (d % 2) * NST * TS_
                kx = f"X{d % 2}"
                ms.op("vector", lambda e, d=d, sx=sx, t=t: e.tensor_mul(
                    mdbu[:, sx: sx + NST * TS_],
                    bass.AP(mdu, d * TS_, [[8 * TS_, 128], [0, NST], [1, TS_]]),
                    mbt[:, (t % 2) * NST * TS_: ((t % 2) + 1) * NST * TS_]),
                    reads=(f"mdu{d}",) + tuple(f"mbt{t % 2}_{g}" for g in range(4)),
                    writes=(kx,), tag="dbu")
                for n in range(NST):
                    pp = n % 4
                    if not (d == 0 and n < 4):
                        ms.op("scalar", lambda e, d=d, n=n, pp=pp: e.activation(
                            mda[:, pp * TS_:(pp + 1) * TS_],
                            mdel[:, d * TS_:(d + 1) * TS_], AF.Exp,
                            scale=maneg[:, d * NST + n: d * NST + n + 1]),
                            reads=(f"mdel{d}", "maneg"), writes=(f"mda{pp}",),
                            tag="expda")
                    if t == 0:
                        init = 0.0
                        rd_in = ()
                    else:
                        init = mhst[:, d * NST + n: d * NST + n + 1]
                        rd_in = (f"mhst{d}",)
                    ms.op("vector",
                          lambda e, n=n, pp=pp, sx=sx, init=init: e.tensor_tensor_scan(
                              mh[:, n * TS_:(n + 1) * TS_],
                              mda[:, pp * TS_:(pp + 1) * TS_],
                              mdbu[:, sx + n * TS_: sx + (n + 1) * TS_],
                              init, ALU.mult, ALU.add),
                          reads=(f"mda{pp}", kx) + rd_in,
                          writes=(f"mhn{n}",), tag="scan")
                ms.op("vector", lambda e, d=d: e.tensor_copy(
                    mhst[:, d * NST:(d + 1) * NST],
                    bass.AP(mh, TS_ - 1, [[NST * TS_, 128], [TS_, NST], [1, 1]])),
                    reads=tuple(f"mhn{n}" for n in range(NST)),
                    writes=(f"mhst{d}",), tag="hsave")
                ms.op("vector", lambda e, sx=sx: e.tensor_mul(
                    mdbu[:, sx: sx + NST * TS_], mh[:, :], mct[:, :]),
                    reads=tuple(f"mhn{n}" for n in range(NST))
                    + tuple(f"mct{g}" for g in range(4)),
                    writes=(kx,), tag="hc")
                for n in range(NST):
                    ms.op("tensor", lambda e, d=d, n=n, sx=sx: e.matmul(
                        mps2[:, (d % 4) * 512: (d % 4) * 512 + TS_],
                        idb[:, :],
                        mdbu[:, sx + n * TS_: sx + (n + 1) * TS_],
                        start=(n == 0), stop=(n == NST - 1)),
                        reads=(kx, "idb"), writes=(f"pb{d % 4}",), tag="mmy")
                if d >= 1:
                    tails(t, d - 1)
                    wave1(t, d - 1)
            tails(t, 7)
            wave1(t, 7)

        def sec_E2(t):
            for mo in range(4):
                po = mo % 2
                ms.op("scalar", lambda e, mo=mo, po=po: e.activation(
                    ost[:, po * TS_:(po + 1) * TS_],
                    mps2[:, (4 + mo) * 512: (4 + mo) * 512 + TS_], AF.Identity),
                    reads=(f"pb{4 + mo}",), writes=(f"ost{po}",), tag="evo")
                ms.dma("gpsimd", lambda e, t=t, mo=mo, po=po: e.dma_start(
                    out=bass.AP(cin, t * DM * TS_ + mo * 128 * TS_,
                                [[TS_, 128], [1, TS_]]),
                    in_=ost[:, po * TS_:(po + 1) * TS_]),
                    reads=(f"ost{po}",), writes=(f"cino{t}_{mo}",), tag="sto1")
            for mo in range(4, 8):
                po = mo % 2
                for kd in range(8):
                    ms.op("tensor", lambda e, mo=mo, kd=kd: e.matmul(
                        mps2[:, mo * 512: mo * 512 + TS_],
                        woutb[:, kd * 1024 + mo * 128: kd * 1024 + (mo + 1) * 128],
                        my2[:, kd * TS_:(kd + 1) * TS_],
                        start=(kd == 0), stop=(kd == 7)),
                        reads=(f"woutb{kd}", f"my2{kd}"), writes=(f"pb{mo}",),
                        tag="mmo2")
                ms.op("scalar", lambda e, mo=mo, po=po: e.activation(
                    ost[:, po * TS_:(po + 1) * TS_],
                    mps2[:, mo * 512: mo * 512 + TS_], AF.Identity),
                    reads=(f"pb{mo}",), writes=(f"ost{po}",), tag="evo")
                ms.dma("gpsimd", lambda e, t=t, mo=mo, po=po: e.dma_start(
                    out=bass.AP(cin, t * DM * TS_ + mo * 128 * TS_,
                                [[TS_, 128], [1, TS_]]),
                    in_=ost[:, po * TS_:(po + 1) * TS_]),
                    reads=(f"ost{po}",), writes=(f"cino{t}_{mo}",), tag="sto1")
            ms.cc("gpsimd", lambda e, t=t: e.collective_compute(
                "AllReduce", ALU.add, replica_groups=RG_PAIRS,
                ins=[bass.AP(cin, t * DM * TS_, [[TS_, DM], [1, TS_]])],
                outs=[bass.AP(cout, t * DM * TS_, [[TS_, DM], [1, TS_]])]),
                reads=tuple(f"cino{t}_{mo}" for mo in range(8)),
                writes=(f"ored{t}",), tag="cco")

        sec_C(0)
        for t in range(NS):
            sec_D(t)
            if t + 1 < NS:
                sec_C(t + 1)
            sec_E2(t)
        ms.op("gpsimd", lambda e: e.nop(),
              reads=tuple(f"ored{t}" for t in range(NS)), tag="ccwait")
        ms.run_block(blk)


def phase_ffn(nc, E, st):
    pass
    with (
        nc.sbuf_tensor("we", [128, 8 * 32 * 128], BF16) as we,
        nc.sbuf_tensor("wc", [128, 32 * 8 * 128], BF16) as wc,
        nc.sbuf_tensor("fxn", [128, 2 * 8 * TF], BF16) as fxn,
        nc.sbuf_tensor("fh2", [128, 32 * TF], BF16) as fh2,
        nc.sbuf_tensor("fhb2", [128, 32], F32) as fhb2,
        nc.sbuf_tensor("fbs", [128, 8], F32) as fbs,
        nc.sbuf_tensor("fbc", [128, 8], F32) as fbc,
        nc.sbuf_tensor("fbd", [128, 8], F32) as fbd,
        nc.sbuf_tensor("fo", [128, 2 * TF], F32) as fo,
        nc.psum_tensor("fps2", [128, 4096], F32) as fps2,
        nc.Block() as blk,
    ):
        ms = MiniSched(nc, E.pool, "ff")

        def mk_hreg(e):
            if not hasattr(E, "hreg"):
                E.hreg = e.alloc_register("hoff")
                pid = e.partition_id()
                e.reg_alu(E.hreg, pid, 1, ALU.bitwise_and)
                e.reg_mul(E.hreg, E.hreg, LHALF)
            return e.nop()
        ms.op("sync", mk_hreg, tag="hreg")
        for kd in range(8):
            ms.dma("sync", lambda e, kd=kd: e.dma_start(
                out=we[:, kd * 32 * 128:(kd + 1) * 32 * 128],
                in_=E.wexp_p[kd * 128:(kd + 1) * 128, :]),
                writes=(f"we{kd}",), tag="ldwe")
        for kh in range(32):
            ms.dma("sync", lambda e, kh=kh: e.dma_start(
                out=wc[:, kh * 8 * 128:(kh + 1) * 8 * 128],
                in_=E.wcon_p[kh * 128:(kh + 1) * 128, :]),
                writes=(f"wc{kh}",), tag="ldwc")
        ms.dma("sync", lambda e: e.dma_start(
            out=fhb2[:, :], in_=bass.AP(E.hbias, 0, [[1, 128], [128, 32]]),
            allow_slow_non_contiguous=True), writes=("fhb2",))
        ms.dma("sync", lambda e: e.dma_start(
            out=fbc[:, :], in_=bass.AP(E.bcon, 0, [[1, 128], [128, 8]]),
            allow_slow_non_contiguous=True), writes=("fbc",))
        ms.dma("sync", lambda e: e.dma_start(
            out=fbd[:, :], in_=bass.AP(E.dbias, 0, [[1, 128], [128, 8]]),
            allow_slow_non_contiguous=True), writes=("fbd",))
        ms.op("vector", lambda e: e.tensor_add(fbs[:, :], fbc[:, :], fbd[:, :]),
              reads=("fbc", "fbd"), writes=("fbs",))

        for fc in range(NFC):
            p = fc % 2

            def ldxn(e, p=p, fc=fc):
                with e.register() as r:
                    e.reg_add(r, E.hreg, fc * TF)
                    return e.dma_start(
                        out=fxn[:, p * 8 * TF:(p + 1) * 8 * TF],
                        in_=bass.AP(E.xn_fm, r, [[L, 128], [128 * L, 8], [1, TF]]))
            ms.dma("sync", ldxn, writes=(f"fxn{p}",), tag="ldxn")
            for mh in range(32):
                bank = mh % 4
                for kd in range(8):
                    ms.op("tensor", lambda e, mh=mh, kd=kd, bank=bank, p=p: e.matmul(
                        fps2[:, bank * 512: bank * 512 + TF],
                        we[:, (kd * 32 + mh) * 128:(kd * 32 + mh + 1) * 128],
                        fxn[:, p * 8 * TF + kd * TF: p * 8 * TF + (kd + 1) * TF],
                        start=(kd == 0), stop=(kd == 7)),
                        reads=(f"we{kd}", f"fxn{p}"), writes=(f"pb{bank}",),
                        tag="mmh")
                ms.op("scalar", lambda e, mh=mh, bank=bank: e.activation(
                    fh2[:, mh * TF:(mh + 1) * TF],
                    fps2[:, bank * 512: bank * 512 + TF],
                    AF.Gelu, bias=fhb2[:, mh:mh + 1]),
                    reads=(f"pb{bank}", "fhb2"), writes=(f"fh2_{mh}",), tag="gelu")
            for mo in range(8):
                bank = 4 + mo % 4
                fsl = mo % 2
                for kh in range(32):
                    ms.op("tensor", lambda e, mo=mo, kh=kh, bank=bank: e.matmul(
                        fps2[:, bank * 512: bank * 512 + TF],
                        wc[:, (kh * 8 + mo) * 128:(kh * 8 + mo + 1) * 128],
                        fh2[:, kh * TF:(kh + 1) * TF],
                        start=(kh == 0), stop=(kh == 31)),
                        reads=(f"wc{kh}", f"fh2_{kh}"), writes=(f"pb{bank}",),
                        tag="mmc")
                ms.op("scalar", lambda e, mo=mo, bank=bank, fsl=fsl: e.activation(
                    fo[:, fsl * TF:(fsl + 1) * TF],
                    fps2[:, bank * 512: bank * 512 + TF],
                    AF.Identity, bias=fbs[:, mo:mo + 1]),
                    reads=(f"pb{bank}", "fbs"), writes=(f"fo{fsl}",), tag="fev")
                ms.dma("gpsimd", lambda e, fc=fc, mo=mo, fsl=fsl: e.dma_start(
                    out=bass.AP(E.ffn_o, mo * 128 * LHALF + fc * TF,
                                [[LHALF, 128], [1, TF]]),
                    in_=fo[:, fsl * TF:(fsl + 1) * TF]),
                    reads=(f"fo{fsl}",), writes=(f"ffo{fc}_{mo}",), tag="stff")
        ms.run_block(blk)


def phase_final(nc, E, st, with_attn):
    pass
    with (
        nc.sbuf_tensor("gfo", [128, 8 * TF], F32) as gfo,
        nc.sbuf_tensor("gat", [128, 8 * TF], F32) as gat,
        nc.sbuf_tensor("gx", [128, 2 * DM], F32) as gx,
        nc.sbuf_tensor("gxn", [128, 2 * DM], F32) as gxn,
        nc.sbuf_tensor("gtm", [128, 2 * DM], F32) as gtm,
        nc.sbuf_tensor("gid", [128, 128], F32) as gid,
        nc.psum_tensor("gps", [128, 2 * DM], F32) as gps,
        nc.Block() as blk,
    ):
        ms = MiniSched(nc, E.pool, "fn")
        ms.dma("sync", lambda e: e.dma_start(out=gid[:, :], in_=E.ident_in[:, :]),
               writes=("gid",))
        for fc in range(NFC):
            cT = fc * TF
            ms.dma("sync", lambda e, cT=cT: e.dma_start(
                out=gfo[:, :],
                in_=bass.AP(E.ffn_o, cT, [[LHALF, 128], [128 * LHALF, 8], [1, TF]])),
                writes=("gfo",), tag="ldfo")
            if with_attn:
                def ldat(e, fc=fc):
                    with e.register() as r:
                        e.reg_mul(r, E.hreg, DM)
                        e.reg_add(r, r, fc * DM * TF)
                        return e.dma_start(
                            out=gat[:, :],
                            in_=bass.AP(E.cout_o2, r,
                                        [[TF, 128], [128 * TF, 8], [1, TF]]))
                ms.dma("sync", ldat, writes=("gat",), tag="ldat")
                ms.op("vector", lambda e: e.tensor_add(gfo[:, :], gfo[:, :], gat[:, :]),
                      reads=("gfo", "gat"), writes=("gfo",), tag="addat")
            for v in range(4):
                pv = v % 2
                for j in range(8):
                    ms.op("tensor", lambda e, j=j, v=v, pv=pv: e.transpose(
                        gps[:, pv * DM + j * 128: pv * DM + (j + 1) * 128],
                        gfo[:, j * TF + v * 128: j * TF + (v + 1) * 128],
                        gid[:, :]),
                        reads=("gfo", "gid"), writes=(f"gpb{pv * 2 + j // 4}",),
                        tag="ftr")
                r0c = fc * TF + v * 128

                def ldx(e, r0c=r0c, pv=pv):
                    with e.register() as r:
                        e.reg_add(r, E.hreg, r0c)
                        e.reg_mul(r, r, DM)
                        return e.dma_start(
                            out=gx[:, pv * DM:(pv + 1) * DM],
                            in_=bass.AP(E.x_in, r, [[DM, 128], [1, DM]]))
                ms.dma("sync", ldx, writes=(f"gx{pv}",), tag="ldx")

                def ldxn2(e, r0c=r0c, pv=pv):
                    with e.register() as r:
                        e.reg_add(r, E.hreg, r0c)
                        e.reg_mul(r, r, DM)
                        return e.dma_start(
                            out=gxn[:, pv * DM:(pv + 1) * DM],
                            in_=bass.AP(E.xn_tm, r, [[DM, 128], [1, DM]]))
                ms.dma("sync", ldxn2, writes=(f"gxn{pv}",), tag="ldxn2")
                ms.op("vector", lambda e, pv=pv: e.tensor_add(
                    gtm[:, pv * DM:(pv + 1) * DM],
                    gps[:, pv * DM:(pv + 1) * DM],
                    gx[:, pv * DM:(pv + 1) * DM]),
                    reads=(f"gpb{pv * 2}", f"gpb{pv * 2 + 1}", f"gx{pv}"),
                    writes=(f"gtm{pv}",), tag="add1")
                if with_attn:
                    ms.op("vector", lambda e, pv=pv: e.tensor_add(
                        gtm[:, pv * DM:(pv + 1) * DM],
                        gtm[:, pv * DM:(pv + 1) * DM],
                        gxn[:, pv * DM:(pv + 1) * DM]),
                        reads=(f"gtm{pv}", f"gxn{pv}"), writes=(f"gtm{pv}",),
                        tag="add2")
                ms.dma("gpsimd", lambda e, fc=fc, v=v, pv=pv: e.dma_start(
                    out=E.y_out[fc * TF + v * 128: fc * TF + (v + 1) * 128, :],
                    in_=gtm[:, pv * DM:(pv + 1) * DM]),
                    reads=(f"gtm{pv}",), tag="sty")
        ms.run_block(blk)


def build_program(dump=(), phases=("ln",)):
    nc = bacc.Bacc(target_bir_lowering=False, trn_type="TRN2")
    dump = set(dump)
    E = Env()

    E.x_in = nc.dram_tensor("x", [L, DM], F32, kind="ExternalInput")
    E.y_out = nc.dram_tensor("y", [LHALF, DM], F32, kind="ExternalOutput")
    E.xn_fm = nc.dram_tensor("xn_fm", [DM, L], BF16, kind="Internal")
    E.xn_tm = nc.dram_tensor("xn_tm", [L, DM], F32, kind="Internal")
    E.wexpc = nc.dram_tensor("wexpc", [2, 8, 128, 4 * 512], F32,
                             kind="ExternalInput")
    E.wexpcb = nc.dram_tensor("wexpcb", [8, 128, 4 * 2], F32,
                              kind="ExternalInput")
    E.upb = nc.dram_tensor("upb", [HID, 1], F32, kind="ExternalInput")
    E.wcon = nc.dram_tensor("wcon", [2, 8, 128, 4 * 512], F32,
                            kind="ExternalInput")
    E.bcon = nc.dram_tensor("bcon", [DM, 1], F32, kind="ExternalInput")
    E.dbias = nc.dram_tensor("dbias", [DM, 1], F32, kind="ExternalInput")
    E.bfu_within = nc.dram_tensor("bfu_within", [7, 4, 128, 8 * 128], F32,
                                  kind="ExternalInput")
    E.bfd_within = nc.dram_tensor("bfd_within", [7, 4, 128, 8 * 128], F32,
                                  kind="ExternalInput")
    E.bfu_cross = nc.dram_tensor("bfu_cross", [5, 8, 128, 8 * 128], F32,
                                 kind="ExternalInput")
    E.bfd_cross = nc.dram_tensor("bfd_cross", [5, 8, 128, 8 * 128], F32,
                                 kind="ExternalInput")
    E.bfu_xv = nc.dram_tensor("bfu_xv", [128, 10 * 32], F32, kind="ExternalInput")
    E.bfd_xv = nc.dram_tensor("bfd_xv", [128, 10 * 32], F32, kind="ExternalInput")
    E.ident_in = nc.dram_tensor("ident", [128, 128], F32, kind="ExternalInput")
    E.wexp_p = nc.dram_tensor("wexp_p", [DM, HID], BF16, kind="Internal")
    E.hbias = nc.dram_tensor("hbias", [HID, 1], F32, kind="Internal")
    E.wcon_p = nc.dram_tensor("wcon_p", [HID, DM], BF16, kind="Internal")
    E.ffn_o = nc.dram_tensor("ffn_o", [DM, LHALF], F32, kind="Internal")
    E.cin_o2 = nc.dram_tensor("cino2", [DM, L], F32, kind="Internal")
    E.cout_o2 = nc.dram_tensor("couto2", [DM, L], F32, kind="Internal")
    E.cin_o1 = nc.dram_tensor("cino1", [DM, L], BF16, kind="Internal")
    E.cout_o1 = nc.dram_tensor("couto1", [DM, L], BF16, kind="Internal")
    E.win, E.wconv, E.bconv, E.wxp, E.wdt, E.bdt = [], [], [], [], [], []
    E.alog, E.dvec, E.wout, E.zt, E.ut, E.bct = [], [], [], [], [], []
    E.cin_xd, E.cout_xd = [], []
    for i in range(2):
        E.win.append(nc.dram_tensor(f"win{i}", [DM, 2 * DH], F32, kind="ExternalInput"))
        E.wconv.append(nc.dram_tensor(f"wconv{i}", [DH, DC], F32, kind="ExternalInput"))
        E.bconv.append(nc.dram_tensor(f"bconv{i}", [DH, 1], F32, kind="ExternalInput"))
        E.wxp.append(nc.dram_tensor(f"wxp{i}", [DH, 96], F32, kind="ExternalInput"))
        E.wdt.append(nc.dram_tensor(f"wdt{i}", [RNK, DH], F32, kind="ExternalInput"))
        E.bdt.append(nc.dram_tensor(f"bdt{i}", [DH, 1], F32, kind="ExternalInput"))
        E.alog.append(nc.dram_tensor(f"alog{i}", [DH, NST], F32, kind="ExternalInput"))
        E.dvec.append(nc.dram_tensor(f"dvec{i}", [DH, 1], F32, kind="ExternalInput"))
        E.wout.append(nc.dram_tensor(f"wout{i}", [DH, DM], F32, kind="ExternalInput"))
        E.zt.append(nc.dram_tensor(f"zt{i}", [DH, L], BF16, kind="Internal"))
        E.ut.append(nc.dram_tensor(f"ut{i}", [DH, L], BF16, kind="Internal"))
        E.bct.append(nc.dram_tensor(f"bct{i}", [2 * NST, L], BF16, kind="Internal"))
        E.cin_xd.append(nc.dram_tensor(f"cinxd{i}", [96, L], F32, kind="Internal"))
        E.cout_xd.append(nc.dram_tensor(f"coutxd{i}", [96, L], F32, kind="Internal"))

    E.dump_t = {}

    def mkdump(name, shape, dtype=F32):
        if name in dump:
            E.dump_t[name] = nc.dram_tensor("dbg_" + name, shape, dtype,
                                            kind="ExternalOutput")

    mkdump("xnfm", [DM, L], BF16)
    mkdump("xntm", [L, DM], F32)
    mkdump("wexpp", [DM, HID], BF16)
    mkdump("fRu", [128, 32 * 128], F32)
    mkdump("fRs", [128, 32 * 128], F32)
    mkdump("vW", [128, 32 * 512], F32)
    mkdump("wconp", [HID, DM], BF16)
    mkdump("hbias", [HID, 1], F32)
    mkdump("ffn", [DM, LHALF], F32)
    mkdump("ut0", [DH, L], BF16)
    mkdump("zt0", [DH, L], BF16)
    mkdump("xd0", [96, L], F32)
    mkdump("o1", [DM, L], BF16)
    mkdump("o2", [DM, L], F32)

    st = contextlib.ExitStack()
    with st:
        E.pool = SemPool(nc, st)
        if "ln" in phases or "fold" in phases:
            phase_lnfold(nc, E, st, do_ln="ln" in phases,
                         do_fold="fold" in phases)
        if "mamba" in phases:
            phase_mamba(nc, E, st, 0)
            phase_mamba(nc, E, st, 1)
        if "ffn" in phases:
            phase_ffn(nc, E, st)
        if "final" in phases:
            phase_final(nc, E, st, with_attn="mamba" in phases)

        if E.dump_t:
            with nc.Block() as blk:
                ms = MiniSched(nc, E.pool, "du")
                for nm, src in (("xnfm", E.xn_fm), ("xntm", E.xn_tm),
                                ("wexpp", E.wexp_p), ("wconp", E.wcon_p),
                                ("hbias", E.hbias), ("ffn", E.ffn_o),
                                ("ut0", E.ut[0]), ("zt0", E.zt[0]),
                                ("xd0", E.cout_xd[0]), ("o1", E.cout_o1),
                                ("o2", E.cout_o2)):
                    if nm in E.dump_t:
                        ms.dma("gpsimd", lambda e, nm=nm, src=src: e.dma_start(
                            out=E.dump_t[nm][:, :], in_=src[:, :]))
                ms.run_block(blk)
        if "final" not in phases:
            with (
                nc.sbuf_tensor("fin", [128, DM], F32) as fin,
                nc.Block() as blk,
            ):
                ms = MiniSched(nc, E.pool, "fi")
                for c in range(LHALF // 128):
                    ms.dma("gpsimd", lambda e, c=c: e.dma_start(
                        out=fin[:, :], in_=E.x_in[c * 128:(c + 1) * 128, :]),
                        writes=("fin",))
                    ms.dma("gpsimd", lambda e, c=c: e.dma_start(
                        out=E.y_out[c * 128:(c + 1) * 128, :], in_=fin[:, :]),
                        reads=("fin",))
                ms.run_block(blk)

    nc.has_collectives = True
    nc.finalize()
    return nc, E


def _bf_host_tiles(weights):
    out = {"n": [], "t": [], "xu": [], "xd": [], "xv": [], "xvd": []}
    for st_ in range(7):
        stride = 1 << st_
        dn = np.zeros((32, 128, 128), np.float32)
        w = weights[st_].reshape(HID // (2 * stride), stride, 2, 2)
        for fa in range(HID):
            if fa & stride:
                continue
            fb = fa + stride
            q = fa // (2 * stride)
            t_ = fa % stride
            j = fa // 128
            ia, ib = fa % 128, fb % 128
            dn[j, ia, ia] = w[q, t_, 0, 0]
            dn[j, ia, ib] = w[q, t_, 0, 1]
            dn[j, ib, ia] = w[q, t_, 1, 0]
            dn[j, ib, ib] = w[q, t_, 1, 1]
        out["n"].append(dn)
        out["t"].append(np.ascontiguousarray(dn.transpose(0, 2, 1)))
    for st_ in range(7, NSTAGE):
        stride = 1 << st_
        xu = np.zeros((32, 2, 128, 128), np.float32)
        xd = np.zeros((32, 2, 128, 128), np.float32)
        xv = np.zeros((128, 32, 2), np.float32)
        xvd = np.zeros((128, 32, 2), np.float32)
        w = weights[st_].reshape(HID // (2 * stride), stride, 2, 2)
        for fo in range(HID):
            jo = fo // 128
            ii = fo % 128
            fa = fo & ~stride
            q = fa // (2 * stride)
            t_ = fa % stride
            if fo == fa:
                cu = (w[q, t_, 0, 0], w[q, t_, 0, 1])
                cd = (w[q, t_, 0, 0], w[q, t_, 1, 0])
            else:
                cu = (w[q, t_, 1, 0], w[q, t_, 1, 1])
                cd = (w[q, t_, 0, 1], w[q, t_, 1, 1])
            xu[jo, 0, ii, ii], xu[jo, 1, ii, ii] = cu
            xd[jo, 0, ii, ii], xd[jo, 1, ii, ii] = cd
            xv[ii, jo, 0], xv[ii, jo, 1] = cu
            xvd[ii, jo, 0], xvd[ii, jo, 1] = cd
        out["xu"].append(xu)
        out["xd"].append(xd)
        out["xv"].append(xv)
        out["xvd"].append(xvd)
    return out


def _host_inputs(core, args):
    b, h = core // 2, core % 2
    hs = slice(h * DH, (h + 1) * DH)
    d = {}
    d["x"] = np.ascontiguousarray(args["L0_attn_x"][b])
    for i in range(2):
        ipw = np.asarray(args["L0_attn_in_proj_w"][i], np.float32)
        wxc = ipw[h * DH:(h + 1) * DH, :]
        wz = ipw[2 * DH + h * DH: 2 * DH + (h + 1) * DH, :]
        d[f"win{i}"] = np.ascontiguousarray(np.concatenate([wxc, wz], 0).T)
        d[f"wconv{i}"] = np.ascontiguousarray(np.asarray(args["L0_attn_conv_w"][i], np.float32)[hs, 0, :])
        d[f"bconv{i}"] = np.ascontiguousarray(np.asarray(args["L0_attn_conv_b"][i], np.float32)[hs, None])
        d[f"wxp{i}"] = np.ascontiguousarray(np.asarray(args["L0_attn_x_proj_w"][i], np.float32)[:, hs].T)
        d[f"wdt{i}"] = np.ascontiguousarray(np.asarray(args["L0_attn_dt_proj_w"][i], np.float32)[hs, :].T)
        d[f"bdt{i}"] = np.ascontiguousarray(np.asarray(args["L0_attn_dt_proj_b"][i], np.float32)[hs, None])
        d[f"alog{i}"] = np.ascontiguousarray(np.asarray(args["L0_attn_A_log"][i], np.float32)[hs, :])
        d[f"dvec{i}"] = np.ascontiguousarray(np.asarray(args["L0_attn_D"][i], np.float32)[hs, None])
        d[f"wout{i}"] = np.ascontiguousarray(np.asarray(args["L0_attn_out_proj_w"][i], np.float32)[:, hs].T)
    def grp_tiles(x):
        s = x.reshape(x.shape[0], -1, 8, 128, 128)
        return np.ascontiguousarray(s.transpose(0, 1, 3, 2, 4).reshape(
            x.shape[0], s.shape[1], 128, 8 * 128))

    def grp_v(v):
        s = v.reshape(8, 4, 128, 2, 512)
        return np.ascontiguousarray(s.transpose(3, 0, 2, 1, 4).reshape(
            2, 8, 128, 4 * 512))

    wexp_t = args["L0_ffn_W_exp"].T.astype(np.float32)
    d["wexpc"] = grp_v(wexp_t)
    bcol = np.concatenate([args["L0_ffn_b_exp"][:, None].astype(np.float32),
                           np.zeros((HID, 1), np.float32)], 1)
    sb = bcol.reshape(8, 4, 128, 2)
    d["wexpcb"] = np.ascontiguousarray(sb.transpose(0, 2, 1, 3).reshape(
        8, 128, 4 * 2))
    d["upb"] = np.ascontiguousarray(args["L0_ffn_up_bias"][:, None])
    d["wcon"] = grp_v(np.asarray(args["L0_ffn_W_con"], np.float32))
    d["bcon"] = np.ascontiguousarray(args["L0_ffn_b_con"][:, None])
    d["dbias"] = np.ascontiguousarray(args["L0_ffn_down_bias"][:, None])
    tu = _bf_host_tiles(np.asarray(args["L0_ffn_up_weights"], np.float32))
    td = _bf_host_tiles(np.asarray(args["L0_ffn_down_weights"], np.float32))
    d["bfu_within"] = grp_tiles(
        np.stack([tu["n"][ss] for ss in range(6)] + [tu["t"][6]]))
    d["bfd_within"] = grp_tiles(
        np.stack([td["n"][0]] + [td["t"][ss] for ss in range(1, 7)]))
    d["bfu_cross"] = grp_tiles(np.stack(tu["xu"]))
    d["bfd_cross"] = grp_tiles(np.stack(td["xd"]))
    d["bfu_xv"] = np.ascontiguousarray(
        np.stack(tu["xv"], 1).reshape(128, 5 * 64))
    d["bfd_xv"] = np.ascontiguousarray(
        np.stack(td["xvd"], 1).reshape(128, 5 * 64))
    d["ident"] = np.eye(128, dtype=np.float32)
    return d


def kernel(**inputs):
    nc, _ = build_program(phases=("ln", "fold", "mamba", "ffn", "final"))
    in_maps = [_host_inputs(c, inputs) for c in range(8)]
    res = run_bass_kernel_spmd(nc, in_maps, core_ids=list(range(8)))
    out = np.empty((4, L, DM), np.float32)
    for c in range(8):
        b, h = c // 2, c % 2
        out[b, h * LHALF:(h + 1) * LHALF, :] = res.results[c]["y"]
    return out
